# version 28
# baseline (speedup 1.0000x reference)
import math
import numpy as np
import ml_dtypes
import concourse.bass as bass
import concourse.mybir as mybir
from concourse.bass_utils import run_bass_kernel_spmd

F32 = mybir.dt.float32
BF16 = mybir.dt.bfloat16
I8 = mybir.dt.int8
AF = mybir.ActivationFunctionType
ALU = mybir.AluOpType
AX = mybir.AxisListType

D = 2048
TOWN = 1024
HALO = 256
SEGT = HALO + TOWN
NT = SEGT // 128
NTO = TOWN // 128
G = 16
FF0 = 8192
FF1 = 5632
NEG = -1.0e30
INVSQ = 1.0 / math.sqrt(128.0)

COMPUTE = ("pe", "act", "dve", "sp", "pool")
QACT_DOMS = ("qact0", "qact1", "qact2", "qact3")
NSLOT = 4
QUEUES = ("qsp", "qact", "qpool")
SPLIT_TGRP = True
BATCH_SITES = {"g2a", "g2b", "h1"}
DRAIN_K = 10**9
DOMS = COMPUTE + tuple(f"{q}{s}" for q in QUEUES for s in range(NSLOT))


class Tok:
    __slots__ = ("dom", "val", "fr")

    def __init__(self, dom, val, fr):
        self.dom, self.val, self.fr = dom, val, fr


def _merge(dst, src):
    for k, v in src.items():
        if dst.get(k, 0) < v:
            dst[k] = v


class JTok:
    __slots__ = ("doms", "fr")

    def __init__(self, doms, fr):
        self.doms = doms
        self.fr = fr


def join(toks):
    toks = [t for t in toks if t is not None]
    fr = {}
    doms = {}
    for t in toks:
        _merge(fr, t.fr)
        if isinstance(t, JTok):
            _merge(doms, t.doms)
        else:
            if doms.get(t.dom, 0) < t.val:
                doms[t.dom] = t.val
    return JTok(doms, fr)


class Prog:
    def __init__(self):
        self.ops = {e: [] for e in COMPUTE}
        self.cnt = {d: 0 for d in DOMS}
        self.base = {}

    def op(self, eng, fn, deps=(), queue=None):
        dom = queue if queue is not None else eng
        inc = 16 if queue is not None else 1
        fr = dict(self.base)
        direct = 0
        for t in deps:
            if t is not None:
                _merge(fr, t.fr)
                if isinstance(t, JTok):
                    v = t.doms.get(eng, 0)
                    if v > direct:
                        direct = v
                elif t.dom == eng and t.val > direct:
                    direct = t.val
        val = self.cnt[dom] + inc
        self.cnt[dom] = val
        fr2 = dict(fr)
        fr2[dom] = val
        self.ops[eng].append((fr, fn, dom, inc, direct))
        return Tok(dom, val, fr2)

    def barrier(self, exclude=()):
        self.base = {d: self.cnt[d] for d in DOMS
                     if self.cnt[d] > 0 and d not in exclude}

    def emitters(self, sems):
        out = {}
        for eng in COMPUTE:
            oplist = self.ops[eng]

            def body(e, eng=eng, oplist=oplist):
                guaranteed = {}
                semcnt = 0
                drainmark = 0
                for fr, fn, dom, inc, direct in oplist:
                    if eng != "pe" and direct > drainmark and \
                            direct > semcnt - DRAIN_K:
                        e.drain()
                        drainmark = semcnt
                    for d in sorted(fr):
                        if d == eng:
                            continue
                        v = fr[d]
                        if v > guaranteed.get(d, 0):
                            e.wait_ge(sems[d], v)
                            guaranteed[d] = v
                    inst = fn(e)
                    if inst is not None:
                        inst.then_inc(sems[dom], inc)
                    if dom == eng:
                        semcnt += inc
            out[eng] = body
        return out


class Arena:
    def __init__(self, pool_ap, nbytes):
        self.pool = pool_ap
        self.live = {}
        self.free_list = [(0, nbytes)]
        self.hw = 0

    def alloc(self, name, cols, dtype):
        size = (cols * mybir.dt.size(dtype) + 63) & ~63
        cands = [(sz, i) for i, (off, sz) in enumerate(self.free_list)
                 if sz >= size]
        if cands:
            _, i = min(cands)
            off, sz = self.free_list[i]
            if size <= 4096:
                start = off + sz - size
                if sz == size:
                    del self.free_list[i]
                else:
                    self.free_list[i] = (off, sz - size)
            else:
                start = off
                if sz == size:
                    del self.free_list[i]
                else:
                    self.free_list[i] = (off + size, sz - size)
            self.live[name] = (start, size)
            self.hw = max(self.hw, start + size)
            return self.pool[:, start:start + cols * mybir.dt.size(dtype)].bitcast(dtype)
        raise MemoryError(f"arena: no room for {name} ({size}B) "
                          f"live={sorted((k, v[1]) for k, v in self.live.items())}")

    def free(self, *names):
        for name in names:
            off, sz = self.live.pop(name)
            self.free_list.append((off, sz))
        self.free_list.sort()
        out = []
        for off, sz in self.free_list:
            if out and out[-1][0] + out[-1][1] == off:
                out[-1][1] += sz
            else:
                out.append([off, sz])
        self.free_list = [tuple(x) for x in out]



class KB:
    def __init__(self):
        nc = bass.Bass("TRN2", target_bir_lowering=False, debug=False)
        self.nc = nc
        self.inp = {}
        self.prog = Prog()
        self.pool_h = nc.alloc_sbuf_tensor("pool", [128, 146176], mybir.dt.uint8)
        self.acc = nc.alloc_sbuf_tensor("accb", [128, 8 * D], F32).ap()
        self.ar = Arena(self.pool_h.ap(), 146176)
        self.psum = [nc.alloc_psum_tensor(f"ps{i}", [128, 512], F32).ap() for i in range(4)]
        self.psT = [nc.alloc_psum_tensor(f"psTf{i}", [128, 512], F32).ap() for i in range(4)]
        self.psT_last = [None] * 4
        self.bank_last = {}
        self.ps_rr = 0
        self.tr_rr = 0
        self.persist = {}
        self.xnT_scr = nc.dram_tensor("xnT_scr", [128, 16 * TOWN], BF16).ap()
        self.qidx = {q: 0 for q in QUEUES}
        self.qlast = {f"{q}{s}": None for q in QUEUES for s in range(NSLOT)}

    def din(self, name, shape, dtype=F32):
        ap = self.nc.dram_tensor(name, list(shape), dtype, kind="ExternalInput").ap()
        self.inp[name] = ap
        return ap

    def dout(self, name, shape, dtype=F32):
        return self.nc.dram_tensor(name, list(shape), dtype, kind="ExternalOutput").ap()

    def dma(self, out, in_, deps=(), q="qsp"):
        engmap = {"qsp": "sp", "qact": "act", "qpool": "pool"}
        slot = self.qidx[q] % NSLOT
        self.qidx[q] += 1
        dom = f"{q}{slot}"
        dd = list(deps)
        if self.qlast[dom] is not None:
            dd.append(self.qlast[dom])
        t = self.prog.op(engmap[q],
                         lambda E, o=out, i=in_: E.dma_start(out=o, in_=i),
                         dd, queue=dom)
        self.qlast[dom] = t
        return t

    def pe(self, fn, deps=()):
        return self.prog.op("pe", fn, deps)

    def act(self, fn, deps=()):
        return self.prog.op("act", fn, deps)

    def dve(self, fn, deps=()):
        return self.prog.op("dve", fn, deps)

    def pool_op(self, fn, deps=()):
        return self.prog.op("pool", fn, deps)

    def bank(self, deps):
        i = self.ps_rr % len(self.psum)
        self.ps_rr += 1
        dd = list(deps)
        if i in self.bank_last:
            dd.append(self.bank_last[i])
        return i, self.psum[i], dd

    def mm16(self, bank_ap, lhsT_fn, rhs_fn, deps, nk=16):
        t = None
        for f in range(nk):
            dd = deps if f == 0 else ()
            lt, rh = lhsT_fn(f), rhs_fn(f)
            t = self.pe(lambda E, lt=lt, rh=rh, o=bank_ap, s=(f == 0),
                        p=(f == nk - 1): E.matmul(o, lt, rh, start=s, stop=p),
                        dd)
        return t

    def _tp(self, out_ap, src_ap, fp32, dd):
        if fp32:
            return self.pe(lambda E, o=out_ap, s=src_ap:
                           E.transpose(o, s, self.identf), dd)
        return self.pe(lambda E, o=out_ap, s=src_ap:
                       E.matmul(o, s, self.identb, start=True, stop=True), dd)

    def ttile(self, src_ap, dst_ap, deps, fp32=False, evac="act"):
        r = self.tr_rr % 4
        self.tr_rr += 1
        ps = self.psT[r]
        dd = list(deps) + ([self.psT_last[r]] if self.psT_last[r] is not None
                           else [])
        n = src_ap.shape[1]
        tt = self._tp(ps[0:n, 0:128], src_ap, fp32, dd)
        if len(dst_ap.shape) == 3:
            sv = ps.rearrange("p (a s) -> p a s", s=128)[0:n, 0:1, :]
        else:
            sv = ps[0:n, 0:128]
        if evac == "act":
            te = self.act(lambda E, d=dst_ap, s=sv: E.copy(d, s), [tt])
        else:
            te = self.dve(lambda E, d=dst_ap, s=sv: E.tensor_copy(d, s), [tt])
        self.psT_last[r] = te
        return te

    def tgrp(self, srcs, evacs, deps, fp32=False, site=""):
        if SPLIT_TGRP and site not in BATCH_SITES:
            toks = []
            for eng, dst, j0, j1, npart in evacs:
                for k in range(j0, j1):
                    if len(dst.shape) == 3:
                        d = dst[:, k - j0:k - j0 + 1, :]
                    else:
                        d = dst[:, (k - j0) * 128:(k - j0 + 1) * 128]
                    toks.append(self.ttile(srcs[k], d, deps, fp32=fp32,
                                           evac=eng))
            return join(toks)
        return self._tgrp(srcs, evacs, deps, fp32)

    def _tgrp(self, srcs, evacs, deps, fp32=False):
        assert len(srcs) <= 4
        r = self.tr_rr % 4
        self.tr_rr += 1
        ps = self.psT[r]
        dd = list(deps) + ([self.psT_last[r]] if self.psT_last[r] is not None
                           else [])
        tt = None
        for j, s in enumerate(srcs):
            n = s.shape[1]
            tt = self._tp(ps[0:n, j * 128:(j + 1) * 128], s, fp32,
                          dd if j == 0 else ())
        toks = []
        prev = None
        for eng, dst, j0, j1, npart in evacs:
            if len(dst.shape) == 3:
                sv = ps.rearrange("p (a s) -> p a s", s=128)[0:npart, j0:j1, :]
            else:
                sv = ps[0:npart, j0 * 128:j1 * 128]
            dd2 = [tt] if prev is None else [tt, prev]
            if eng == "act":
                prev = self.act(lambda E, d=dst, s=sv: E.copy(d, s), dd2)
            else:
                prev = self.dve(lambda E, d=dst, s=sv: E.tensor_copy(d, s), dd2)
            toks.append(prev)
        jt = join(toks)
        self.psT_last[r] = jt
        return jt

    def tgrp_add(self, srcs, dst3, deps, fp32=True):
        assert len(srcs) <= 4
        if SPLIT_TGRP:
            last = None
            for j, s in enumerate(srcs):
                r = self.tr_rr % 4
                self.tr_rr += 1
                ps = self.psT[r]
                dd = list(deps) + ([self.psT_last[r]]
                                   if self.psT_last[r] is not None else [])
                tt = self._tp(ps[:, 0:128], s, fp32, dd)
                ps3 = ps.rearrange("p (a s) -> p a s", s=128)[:, 0:1, :]
                ta = self.dve(lambda E, d=dst3[:, j:j + 1, :], s3=ps3:
                              E.tensor_add(d, d, s3), [tt])
                self.psT_last[r] = ta
                last = ta
            return last
        r = self.tr_rr % 4
        self.tr_rr += 1
        ps = self.psT[r]
        dd = list(deps) + ([self.psT_last[r]] if self.psT_last[r] is not None
                           else [])
        tt = None
        for j, s in enumerate(srcs):
            tt = self._tp(ps[:, j * 128:(j + 1) * 128], s, fp32,
                          dd if j == 0 else ())
        ps3 = ps.rearrange("p (a s) -> p a s", s=128)[:, 0:len(srcs), :]
        ta = self.dve(lambda E, d=dst3, s=ps3: E.tensor_add(d, d, s), [tt])
        self.psT_last[r] = ta
        return ta

    def stream_cast(self, dst_ap, src2d, total_rows, c0, c1, deps):
        tl = total_rows // 128
        src = src2d.rearrange("(t p) n -> p t n", p=128)
        slot = self.qidx["qpool"] % NSLOT
        self.qidx["qpool"] += 1
        dom = f"qpool{slot}"
        dd = list(deps)
        if self.qlast[dom] is not None:
            dd.append(self.qlast[dom])
        t = self.prog.op("pool",
                         lambda E, d=dst_ap, s=src, tl=tl, c0=c0, c1=c1:
                         E.dma_start(out=d.rearrange("p (t n) -> p t n", t=tl),
                                     in_=s[:, :, c0:c1]),
                         dd, queue=dom)
        self.qlast[dom] = t
        return t


    def stream_f32(self, dst_ap, src2d, total_rows, c0, c1, deps, q="qsp"):
        tl = total_rows // 128
        src = src2d.rearrange("(t p) n -> p t n", p=128)
        return self.dma(dst_ap.rearrange("p (t n) -> p t n", t=tl),
                        src[:, :, c0:c1], deps, q=q)


def tile_ln(kb, src, dst, scr, eps_col, deps, sqdump=None):
    s1, s2, sd, bb = scr
    sq = sqdump if sqdump is not None else kb.sqdump
    t1 = kb.dve(lambda E: E.reduce_sum(s1, src, axis=AX.X, negate=True), deps)
    t2 = kb.act(lambda E: E.activation(out=sq, in_=src, func=AF.Square,
                                       accum_out=s2), deps)
    t3 = kb.act(lambda E: E.activation(out=s1, in_=s1, func=AF.Copy,
                                       scale=1.0 / D), [t1])
    t4 = kb.dve(lambda E: E.tensor_scalar(bb, s1, s1, -1.0, ALU.mult,
                                          ALU.mult), [t3])
    t5 = kb.dve(lambda E: E.tensor_scalar_add(bb, bb, eps_col), [t4])
    t6 = kb.act(lambda E: E.activation(out=sd, in_=s2, func=AF.Sqrt,
                                       bias=bb, scale=1.0 / D), [t2, t5])
    t7 = kb.dve(lambda E: E.reciprocal(sd, sd), [t6])
    t8 = kb.dve(lambda E: E.tensor_scalar(dst, src, s1, sd, ALU.add,
                                          ALU.mult), [t7, t3])
    return t8


def transpose16(kb, src_bf, dst3, deps, nf=16):
    toks = []
    for gi, g0 in enumerate(range(0, nf, 4)):
        gn = min(4, nf - g0)
        srcs = [src_bf[:, (g0 + j) * 128:(g0 + j + 1) * 128] for j in range(gn)]
        eng = "act" if gi % 2 == 0 else "dve"
        toks.append(kb.tgrp(srcs, [(eng, dst3[:, g0:g0 + gn, :], 0, gn, 128)],
                            deps, site="t16"))
    return toks, join(toks)


def fview(buf, nf, stride, c0, c1):
    return buf.rearrange("p (f s) -> p f s", s=stride)[:, 0:nf, c0:c1]



def emit_l0_segment(kb, xseg, maskseg, x1_dst):
    ar, prog = kb.ar, kb.prog

    xniT = ar.alloc("xniT", 16 * SEGT, BF16)
    xnT = ar.alloc("xnT", 16 * TOWN, BF16)
    mask_sb = ar.alloc("mask_sb", NTO * 16, F32)
    kb.ln0s = ar.alloc("ln0s", D, BF16)
    kb.ln0b = ar.alloc("ln0b", D, BF16)
    tln0 = [kb.dma(kb.ln0s, kb.inp["ln0s"][:, :], [], q="qpool"),
            kb.dma(kb.ln0b, kb.inp["ln0b"][:, :], [], q="qpool")]
    xt = [ar.alloc(f"xt{i}", D, F32) for i in range(2)]
    kb.sqdump = ar.alloc("sqs", D, F32)
    lnscr = [tuple(ar.alloc(f"lns{i}_{j}", 1, F32) for j in range(4))
             for i in range(2)]
    xnv = ar.alloc("xnA", D, F32)
    xn = [xnv, xnv]
    xciv = kb.sqdump
    xci = [xciv, xciv]
    xnib = [ar.alloc(f"xnibA{i}", D, BF16) for i in range(2)]
    xnb = [ar.alloc(f"xnbA{i}", D, BF16) for i in range(2)]
    zscr = ar.alloc("zscr", 16, F32)
    zwork = ar.alloc("zwork", 16, F32)
    cwork = ar.alloc("cwork", 16, F32)
    m1 = ar.alloc("m1", 1, F32)
    tile_done = {}
    prev_router = None
    prev_t7b = None
    prev_inner = None
    for i in range(NT):
        b = i % 2
        dd = [tile_done[i - 2]] if i - 2 in tile_done else []
        tl = kb.dma(xt[b], xseg[i * 128:(i + 1) * 128, :], dd, q="qsp")
        t7 = tile_ln(kb, xt[b], xn[b], lnscr[b], kb.eps5,
                     [tl, prev_t7b, prev_inner])
        if i >= 2:
            ti = i - 2
            tacc = kb.dve(lambda E, o=kb.acc[:, ti * D:(ti + 1) * D],
                          a=xt[b], c=xn[b]: E.tensor_add(o, a, c), [t7])
            t7b = kb.act(lambda E, o=xnb[b], a=xn[b]: E.activation(
                out=o, in_=a, func=AF.Copy), [t7])
            prev_t7b = t7b
            xnt_toks, _ = transpose16(
                kb, xnb[b], fview(xnT, 16, TOWN, ti * 128, (ti + 1) * 128),
                [t7b])
            bi, bk, bd = kb.bank([join(xnt_toks), kb.wrT_tok])
            tm = kb.mm16(bk[:, 0:16],
                         lambda f, ti=ti: xnT[:, f * TOWN + ti * 128:
                                              f * TOWN + (ti + 1) * 128],
                         lambda f: kb.wrTb[:, f * 16:(f + 1) * 16], bd)
            tz = kb.dve(lambda E, bk=bk: E.tensor_add(zscr, bk[:, 0:16], kb.brb),
                        [tm, prev_router])
            kb.bank_last[bi] = tz
            tz2 = kb.dve(lambda E: E.tensor_copy(zwork, zscr), [tz])
            tprev = tz2
            for it in range(3):
                ta = kb.dve(lambda E: E.reduce_max(m1, zwork, axis=AX.X), [tprev])
                tb = kb.dve(lambda E: E.tensor_single_scalar(cwork, zwork, m1,
                                                             ALU.is_ge), [ta])
                tc = kb.dve(lambda E: E.tensor_scalar_mul(cwork, cwork, -BIGC),
                            [tb])
                tprev = kb.dve(lambda E: E.tensor_add(zwork, zwork, cwork), [tc])
            ta = kb.dve(lambda E: E.reduce_max(m1, zwork, axis=AX.X), [tprev])
            prev_router = kb.dve(lambda E, ti=ti: E.tensor_single_scalar(
                mask_sb[:, ti * 16:(ti + 1) * 16], zscr, m1, ALU.is_ge), [ta])
        else:
            tacc = t7
        t8 = tile_ln(kb, xn[b], xci[b], lnscr[b], kb.eps5, [t7, tacc])
        t9 = kb.dve(lambda E, a=xci[b], s=kb.ln0s: E.tensor_mul(a, a, s),
                    [t8] + tln0)
        t10 = kb.dve(lambda E, o=xnib[b], a=xci[b], s=kb.ln0b:
                     E.tensor_add(o, a, s), [t9])
        prev_inner = t10
        _, tlast = transpose16(
            kb, xnib[b], fview(xniT, 16, SEGT, i * 128, (i + 1) * 128), [t10])
        tile_done[i] = join([tlast, prev_router]) if i >= 2 else tlast
    kb.dma(kb.xnT_scr, xnT, [tile_done[NT - 1], tile_done[NT - 2]], q="qsp")
    for n in ("xt0", "xt1", "xnA", "xnibA0", "xnibA1", "sqs",
              "xnbA0", "xnbA1", "zscr", "zwork", "cwork", "m1",
              "ln0s", "ln0b", "xnT",
              "lns0_0", "lns0_1", "lns0_2", "lns0_3",
              "lns1_0", "lns1_1", "lns1_2", "lns1_3"):
        ar.free(n)
    prog.barrier(exclude=QACT_DOMS)

    OTb = ar.alloc("OTb", 16 * TOWN, BF16)
    QTb = ar.alloc("QTb", TOWN, BF16)
    KTb = ar.alloc("KTb", G + SEGT, BF16)
    V4b = ar.alloc("V4b", NT * 512, BF16)
    Vg4 = ar.alloc("Vg4", 512, BF16)
    wv4 = ar.alloc("wv4", 16 * 512, BF16)
    wqh = [ar.alloc(f"wqh{i}", 16 * 128, BF16) for i in range(2)]
    wkh = [ar.alloc(f"wkh{i}", 16 * 128, BF16) for i in range(2)]
    smscr_ = [ar.alloc(f"smscr{i}", 512, F32) for i in range(2)]
    pscr_ = [ar.alloc(f"pscr{i}", 512, F32) for i in range(2)]
    Ab_ = [ar.alloc(f"Ab{i}", 512, BF16) for i in range(2)]
    ATb_ = [ar.alloc(f"ATb{i}", 512, BF16) for i in range(2)]
    mxs_ = [ar.alloc(f"mxs{i}", 1, F32) for i in range(2)]
    dens_ = [ar.alloc(f"dens{i}", 1, F32) for i in range(2)]
    maskf = ar.alloc("maskf", 3 * 400, BF16)
    tmask = kb.dma(maskf.rearrange("p (q n) -> p q n", q=3),
                   maskseg.rearrange("q p n -> p q n"), [], q="qpool")

    wv_tok = None
    wq_tok = [None, None]
    wk_tok = [None, None]
    head_done = {}
    for g in range(4):
        vdeps = [wv_tok] if wv_tok is not None else []
        if g >= 1:
            vdeps += [head_done[g * 4 - 1]]
        wv_tok = kb.stream_cast(wv4, kb.inp["wvT"], D, g * 512, (g + 1) * 512, vdeps)
        bi, bk, bd = kb.bank([wv_tok, kb.gT_tok])
        tv = kb.mm16(bk[0:16, :], lambda f: kb.gTb[:, f * 16:(f + 1) * 16],
                     lambda f: wv4[:, f * 512:(f + 1) * 512], bd)
        te = kb.act(lambda E, bk=bk: E.copy(Vg4[0:16, :], bk[0:16, :]), [tv])
        kb.bank_last[bi] = te
        vg_tok = te
        vt_tok = []
        for kt in range(NT):
            bi, bk, bd = kb.bank([wv_tok])
            tv = kb.mm16(bk, lambda f, kt=kt: xniT[:, f * SEGT + kt * 128:
                                                   f * SEGT + (kt + 1) * 128],
                         lambda f: wv4[:, f * 512:(f + 1) * 512], bd)
            te = kb.act(lambda E, bk=bk, kt=kt: E.copy(
                V4b[:, kt * 512:(kt + 1) * 512], bk), [tv])
            kb.bank_last[bi] = te
            vt_tok.append(te)
        for hh in range(4):
            h = g * 4 + hh
            if getattr(kb, "gamma", None):
                kb.gamma.step(6)
            wb = h % 2
            wdeps = [wq_tok[wb]] if wq_tok[wb] is not None else []
            if h >= 2:
                wdeps += [head_done[h - 2]]
            wq_tok[wb] = kb.stream_cast(wqh[wb], kb.inp["wqT"], D, h * 128,
                                        (h + 1) * 128, wdeps)
            wk_tok[wb] = kb.stream_cast(wkh[wb], kb.inp["wkT"], D, h * 128,
                                        (h + 1) * 128, wdeps)
            prev_h = [head_done[h - 1]] if h >= 1 else []
            qt_toks = []
            for c in range(2):
                bi, bk, bd = kb.bank([wq_tok[wb]] + prev_h)
                tm = kb.mm16(bk, lambda f, wb=wb: wqh[wb][:, f * 128:(f + 1) * 128],
                             lambda f, c=c: xniT[:, f * SEGT + HALO + c * 512:
                                                 f * SEGT + HALO + (c + 1) * 512], bd)
                te = kb.act(lambda E, bk=bk, c=c: E.activation(
                    out=QTb[:, c * 512:(c + 1) * 512], in_=bk, func=AF.Copy,
                    scale=INVSQ), [tm])
                kb.bank_last[bi] = te
                qt_toks.append(te)
            bi, bk, bd = kb.bank([wk_tok[wb], kb.gT_tok] + prev_h)
            tm = kb.mm16(bk[:, 0:16], lambda f, wb=wb: wkh[wb][:, f * 128:(f + 1) * 128],
                         lambda f: kb.gTb[:, f * 16:(f + 1) * 16], bd)
            te = kb.act(lambda E, bk=bk: E.copy(KTb[:, 0:16], bk[:, 0:16]), [tm])
            kb.bank_last[bi] = te
            kt_toks = [te]
            for c, cw in ((0, 512), (1, 512), (2, 256)):
                bi, bk, bd = kb.bank([wk_tok[wb]] + prev_h)
                tm = kb.mm16(bk[:, 0:cw], lambda f, wb=wb: wkh[wb][:, f * 128:(f + 1) * 128],
                             lambda f, c=c, cw=cw: xniT[:, f * SEGT + c * 512:
                                                        f * SEGT + c * 512 + cw], bd)
                te = kb.act(lambda E, bk=bk, c=c, cw=cw: E.copy(
                    KTb[:, G + c * 512:G + c * 512 + cw], bk[:, 0:cw]), [tm])
                kb.bank_last[bi] = te
                kt_toks.append(te)
            last_ot = None
            par_ot = [None, None]
            smm = {}

            def emit_score_mm(qt):
                bi, bk, bd = kb.bank(qt_toks + kt_toks + [tmask])
                t1 = kb.pe(lambda E, bk=bk, qt=qt: E.matmul(
                    bk[:, 0:384], QTb[:, qt * 128:(qt + 1) * 128],
                    KTb[:, G + qt * 128:G + qt * 128 + 384],
                    start=True, stop=True), bd)
                t2 = kb.pe(lambda E, bk=bk, qt=qt: E.matmul(
                    bk[:, 384:400], QTb[:, qt * 128:(qt + 1) * 128],
                    KTb[:, 0:16], start=True, stop=True), ())
                smm[qt] = (bi, bk, t1, t2)

            emit_score_mm(0)
            for qt in range(NTO):
                if qt + 1 < NTO:
                    emit_score_mm(qt + 1)
                pb = qt % 2
                smscr, pscr, mxs, dens = (smscr_[pb], pscr_[pb], mxs_[pb],
                                          dens_[pb])
                Ab, ATb = Ab_[pb], ATb_[pb]
                bi, bk, t1, t2 = smm.pop(qt)
                mi = min(qt, 2)
                t3 = kb.dve(lambda E, bk=bk, mi=mi, sm=smscr: E.tensor_add(
                    sm[:, 0:400], bk[:, 0:400],
                    maskf[:, mi * 400:(mi + 1) * 400]), [t1, t2])
                kb.bank_last[bi] = t3
                t5 = kb.dve(lambda E, sm=smscr, mx=mxs: E.reduce_max(
                    mx, sm[:, 0:400], axis=AX.X, negate=True), [t3])
                t7 = kb.act(lambda E, sm=smscr, mx=mxs, p=pscr, dn=dens:
                            E.activation(out=p[:, 0:400], in_=sm[:, 0:400],
                                         func=AF.Exp, bias=mx, accum_out=dn),
                            [t5, par_ot[pb]])
                t8 = kb.dve(lambda E, dn=dens: E.reciprocal(dn, dn), [t7])
                t9 = kb.dve(lambda E, A=Ab, p=pscr, dn=dens:
                            E.tensor_scalar_mul(A[:, 0:400], p[:, 0:400], dn),
                            [t8])
                soft = {qt: t9}
                treg = [kb.tgrp(
                    [Ab[:, j * 128:(j + 1) * 128] for j in range(3)]
                    + [Ab[:, 384:400]],
                    [("act", ATb[:, 0:384], 0, 3, 128),
                     ("act", ATb[0:16, 384:512], 3, 4, 16)], [soft[qt]],
                    site="atb")]
                bi2, bk2, bd2 = kb.bank(treg + vt_tok + [vg_tok])
                tm = None
                for j in range(3):
                    tm = kb.pe(lambda E, bk2=bk2,
                               lt=V4b[:, (qt + j) * 512 + hh * 128:
                                      (qt + j) * 512 + (hh + 1) * 128],
                               rh=ATb[:, j * 128:(j + 1) * 128],
                               s=(j == 0): E.matmul(bk2[:, 0:128], lt, rh,
                                                    start=s, stop=False),
                               bd2 if j == 0 else ())
                tm = kb.pe(lambda E, bk2=bk2,
                           lt=Vg4[0:16, hh * 128:(hh + 1) * 128],
                           rh=ATb[0:16, 384:512]: E.matmul(
                               bk2[:, 0:128], lt, rh, start=False, stop=True),
                           ())
                te = kb.act(lambda E, bk2=bk2, h=h, qt=qt: E.copy(
                    OTb[:, h * TOWN + qt * 128:h * TOWN + (qt + 1) * 128],
                    bk2[:, 0:128]), [tm])
                kb.bank_last[bi2] = te
                last_ot = te
                par_ot[pb] = te
            head_done[h] = last_ot
    ar.free("QTb", "KTb", "V4b", "Vg4", "wv4", "wqh0", "wqh1", "wkh0", "wkh1",
            "smscr0", "smscr1", "pscr0", "pscr1", "Ab0", "Ab1", "ATb0", "ATb1",
            "mxs0", "mxs1", "dens0", "dens1", "maskf", "xniT")
    prog.barrier(exclude=QACT_DOMS)

    woc = [ar.alloc(f"woc{i}", 16 * 512, BF16) for i in range(2)]
    wo_tok = [None, None]
    wo_mmd = {}

    def wo_stream(c2):
        if c2 < 4:
            b2 = c2 % 2
            dd = [wo_tok[b2], wo_mmd.get(c2 - 2)]
            wo_tok[b2] = kb.stream_cast(woc[b2], kb.inp["woT"], D, c2 * 512,
                                        (c2 + 1) * 512, dd)

    wo_stream(0)
    wo_stream(1)
    for ch in range(4):
        b = ch % 2
        if getattr(kb, "gamma", None):
            kb.gamma.step(4)
        for qt in range(NTO):
            bi, bk, bd = kb.bank([wo_tok[b]])
            tm = kb.mm16(bk, lambda f, qt=qt: OTb[:, f * TOWN + qt * 128:
                                                  f * TOWN + (qt + 1) * 128],
                         lambda f, b=b: woc[b][:, f * 512:(f + 1) * 512], bd)
            if qt == NTO - 1:
                wo_mmd[ch] = tm
                wo_stream(ch + 2)
            ta = kb.dve(lambda E, bk=bk, qt=qt, ch=ch: E.tensor_add(
                kb.acc[:, qt * D + ch * 512:qt * D + (ch + 1) * 512],
                kb.acc[:, qt * D + ch * 512:qt * D + (ch + 1) * 512], bk), [tm])
            kb.bank_last[bi] = ta
    ar.free("OTb", "woc0", "woc1")
    prog.barrier(exclude=QACT_DOMS)

    hTb = ar.alloc("hTb", 64 * 512, BF16)
    xnT = ar.alloc("xnT", 16 * TOWN, BF16)
    kb.dma(xnT, kb.xnT_scr, [], q="qsp")
    prog.barrier(exclude=QACT_DOMS)
    gel = ar.alloc("gel", 512, BF16)
    gmb = ar.alloc("gmb", 512, BF16)
    yts = ar.alloc("ytsE", 512, BF16)
    prev_tmk = prev_tlast = None
    for half in range(2):
        w1c = [ar.alloc(f"w1c{i}", 16 * 512, BF16) for i in range(2)]
        w1_tok = [None, None]
        w1_mmd = {}

        def w1_stream(i2):
            if i2 < 16:
                b2 = i2 % 2
                dd = [w1_tok[b2], w1_mmd.get(i2 - 2)]
                w1_tok[b2] = kb.stream_cast(w1c[b2], kb.inp["w1T0"], D,
                                            i2 * 512, (i2 + 1) * 512, dd)

        w1_stream(0)
        w1_stream(1)
        seq = [(ibc, t4) for ibc in range(16) for t4 in range(4)]
        mmq = {}

        def w1_mm(k):
            ibc, t4 = seq[k]
            b = ibc % 2
            ti = half * 4 + t4
            bi, bk, bd = kb.bank([w1_tok[b]])
            tm = kb.mm16(bk, lambda f, ti=ti: xnT[:, f * TOWN + ti * 128:
                                                  f * TOWN + (ti + 1) * 128],
                         lambda f, b=b: w1c[b][:, f * 512:(f + 1) * 512], bd)
            if t4 == 3:
                w1_mmd[ibc] = tm
                w1_stream(ibc + 2)
            mmq[k] = (bi, bk, tm)

        w1_mm(0)
        for k, (ibc, t4) in enumerate(seq):
            if k + 1 < len(seq):
                w1_mm(k + 1)
            ti = half * 4 + t4
            bi, bk, tm = mmq.pop(k)
            if t4 == 0 and getattr(kb, "gamma", None):
                kb.gamma.step(3)
            tg = kb.act(lambda E, bk=bk: E.activation(out=gel, in_=bk,
                                                      func=AF.Gelu),
                        [tm, prev_tmk])
            kb.bank_last[bi] = tg
            tmk = kb.dve(lambda E, ti=ti, ibc=ibc: E.tensor_scalar_mul(
                gmb, gel, mask_sb[:, ti * 16 + ibc:ti * 16 + ibc + 1]),
                [tg, prev_tlast])
            _, tlast = transpose16(
                kb, gmb,
                hTb.rearrange("p (f s) -> p f s", s=512)
                [:, ibc * 4:ibc * 4 + 4, t4 * 128:(t4 + 1) * 128],
                [tmk], nf=4)
            prev_tmk, prev_tlast = tmk, tlast
        prog.barrier(exclude=QACT_DOMS)
        ar.free("w1c0", "w1c1")
        w2c = [ar.alloc(f"w2c{i}", 64 * 128, BF16) for i in range(2)]
        w2_tok = [None, None]
        w2_mmd = {}

        def w2_stream(f2):
            if f2 < 16:
                b2 = f2 % 2
                dd = [w2_tok[b2], w2_mmd.get(f2 - 2)]
                w2_tok[b2] = kb.stream_cast(w2c[b2], kb.inp["w2T0"], FF0,
                                            f2 * 128, (f2 + 1) * 128, dd)

        w2_stream(0)
        w2_stream(1)
        for f in range(16):
            b = f % 2
            if getattr(kb, "gamma", None):
                kb.gamma.step(3)
            bi, bk, bd = kb.bank([w2_tok[b]])
            tm = None
            for it in range(64):
                tm = kb.pe(lambda E, bk=bk, lt=w2c[b][:, it * 128:(it + 1) * 128],
                           rh=hTb[:, it * 512:(it + 1) * 512],
                           s=(it == 0), p=(it == 63):
                           E.matmul(bk, lt, rh, start=s, stop=p),
                           bd if it == 0 else ())
            w2_mmd[f] = tm
            w2_stream(f + 2)
            tev = kb.act(lambda E, bk=bk: E.copy(yts, bk), [tm])
            kb.bank_last[bi] = tev
            kb.tgrp_add(
                [yts[:, j * 128:(j + 1) * 128] for j in range(4)],
                fview(kb.acc, 8, D, f * 128, (f + 1) * 128)
                [:, half * 4:half * 4 + 4, :], [tev], fp32=False)
        prog.barrier(exclude=QACT_DOMS)
        ar.free("w2c0", "w2c1")
    ar.free("hTb", "gel", "gmb", "ytsE", "xnT", "mask_sb")
    prog.barrier(exclude=QACT_DOMS)

    dump_tok = None
    if x1_dst is not None:
        for ti in range(NTO):
            dump_tok = kb.dma(x1_dst[ti * 128:(ti + 1) * 128, :],
                              kb.acc[:, ti * D:(ti + 1) * D], [], q="qsp")
        prog.barrier(exclude=QACT_DOMS)
    return dump_tok


BIGC = 1.0e9


def declare_l0(kb):
    for name, shape in (("wqT", (D, D)), ("wkT", (D, D)), ("wvT", (D, D)),
                        ("woT", (D, D)), ("wrT", (D, G)), ("w1T0", (D, FF0)),
                        ("w2T0", (FF0, D)), ("gT", (D, G)),
                        ("ln0s", (128, D)), ("ln0b", (128, D)),
                        ("brb", (128, G))):
        kb.din(name, shape, F32)


def load_l0_consts(kb):
    ar = kb.ar
    kb.brb = ar.alloc("brb", G, F32)
    kb.gTb = ar.alloc("gTb", 16 * G, BF16)
    kb.wrTb = ar.alloc("wrTb", 16 * G, BF16)
    kb.dma(kb.brb, kb.inp["brb"][:, :], [], q="qsp")
    kb.gT_tok = kb.stream_cast(kb.gTb, kb.inp["gT"], D, 0, G, [])
    kb.wrT_tok = kb.stream_cast(kb.wrTb, kb.inp["wrT"], D, 0, G, [])
    kb.prog.barrier(exclude=QACT_DOMS)


def build_l0():
    kb = KB()
    setup_min(kb)
    declare_l0(kb)
    load_l0_consts(kb)
    xO = kb.din("xO", (SEGT, D))
    maskO = kb.din("maskO", (3, 128, 400))
    x1 = kb.dout("x1", (TOWN, D))
    emit_l0_segment(kb, xO, maskO, x1)
    finish(kb)
    return kb


def finish(kb):
    nc = kb.nc
    from contextlib import ExitStack
    es = ExitStack()
    sems = {d: es.enter_context(nc.semaphore(f"sem_{d}")) for d in DOMS}
    bodies = kb.prog.emitters(sems)
    final_waits = {d: kb.prog.cnt[d] for d in DOMS if kb.prog.cnt[d] > 0}

    def sp_body(e):
        bodies["sp"](e)
        for dname in sorted(final_waits):
            e.wait_ge(sems[dname], final_waits[dname])

    with es, nc.Block() as block:
        block.sync(sp_body)
        block.tensor(bodies["pe"])
        block.scalar(bodies["act"])
        block.vector(bodies["dve"])
        block.gpsimd(bodies["pool"])
    return kb




def declare_l1(kb):
    for name, shape in (("wqkv", (D, 3 * D)), ("wgate", (D, D)), ("wout", (D, D)),
                        ("rmsb", (128, D)), ("slopeb", (128, G)),
                        ("fw1T", (D, FF1)), ("fw2T", (D, FF1)), ("fw3T", (FF1, D)),
                        ("iotaQ", (128, 2)), ("iotaK", (128, 2)),
                        ("diffT", (128, 512)), ("onesf", (1, 128)),
                        ("selc", (128, 1))):
        kb.din(name, shape, F32)


def load_l1_consts(kb):
    ar = kb.ar
    kb.rmsb = ar.alloc("rmsb", D, F32)
    kb.slopeb = ar.alloc("slopeb", G, F32)
    kb.iotaQ = ar.alloc("iotaQ", 2, F32)
    kb.iotaK = ar.alloc("iotaK", 2, F32)
    kb.diffT = ar.alloc("diffT", 512, F32)
    kb.onesf = ar.alloc("onesf", 128, F32)
    kb.selc = ar.alloc("selc", 1, F32)
    for n in ("rmsb", "slopeb", "iotaQ", "iotaK", "diffT", "selc"):
        kb.dma(getattr(kb, n), kb.inp[n][:, :], [], q="qsp")
    kb.dma(kb.onesf[0:1, :], kb.inp["onesf"][:, :], [], q="qsp")
    kb.prog.barrier(exclude=QACT_DOMS)


class GammaStream:

    def __init__(self, kb):
        self.kb = kb
        ar = kb.ar
        kb.gacc = ar.alloc("gacc", 4, F32)
        self.pcolg = ar.alloc("pcolg", 1, F32)
        self.wst = [ar.alloc(f"wst{i}", 704, F32) for i in range(2)]
        self.prev_add = kb.dve(lambda E: E.memset(kb.gacc, 0.0), [])
        self.wtok = [None] * 2
        self.pieces = []
        for t, nm in (((0, "fw1T")), ((1, "fw2T")), ((2, "fw3T"))):
            rows, cols = ((D, FF1), (D, FF1), (FF1, D))[t]
            piece = 704 if cols % 704 == 0 else 512
            for rt in range(rows // 128):
                for c0 in range(0, cols, piece):
                    self.pieces.append((t, nm, rt, c0, piece))
        self.i = 0
        kb.gamma_bufs = ("wst0", "wst1", "pcolg")

    def step(self, k):
        kb = self.kb
        for _ in range(k):
            if self.i >= len(self.pieces):
                return
            t, nm, rt, c0, piece = self.pieces[self.i]
            b = self.i % 2
            src = kb.inp[nm].rearrange("(t p) n -> p t n", p=128)
            tl = kb.dma(self.wst[b][:, 0:piece], src[:, rt, c0:c0 + piece],
                        [self.wtok[b], self.prev_add], q="qact")
            self.wtok[b] = tl
            tr = kb.dve(lambda E, a=self.wst[b][:, 0:piece]: E.reduce_sum(
                self.pcolg, a, axis=AX.X, apply_absolute_value=True),
                [tl, self.prev_add])
            self.prev_add = kb.dve(
                lambda E, o=kb.gacc[:, t:t + 1]: E.tensor_add(o, o, self.pcolg),
                [tr])
            self.i += 1

    def finishup(self):
        self.step(len(self.pieces))
        self.kb.gamma_tok = self.prev_add


def emit_l1_consts(kb):
    ar = kb.ar
    kb.qdc = ar.alloc("qdc", 32, F32)
    kb.kdc = ar.alloc("kdc", 32, F32)
    kb.bdc = ar.alloc("bdc", G, F32)
    kb.Sst = ar.alloc("Sst", 16 * 128, F32)
    scr = ar.alloc("cscr", 2, F32)
    tz = kb.dve(lambda E: E.memset(kb.Sst, 0.0), [])
    prev = None
    for h in range(16):
        sc = kb.slopeb[:, h:h + 1]
        t1 = kb.dve(lambda E, sc=sc: E.tensor_scalar_mul(scr, kb.iotaQ, sc),
                    [prev] if prev else [])
        t2 = kb.act(lambda E, o=kb.qdc[:, h * 2:(h + 1) * 2]: E.activation(
            out=o, in_=scr, func=AF.Exp, scale=-1.0), [t1])
        t3 = kb.dve(lambda E, sc=sc: E.tensor_scalar_mul(scr, kb.iotaK, sc), [t2])
        t4 = kb.act(lambda E, o=kb.kdc[:, h * 2:(h + 1) * 2]: E.activation(
            out=o, in_=scr, func=AF.Exp, scale=-1.0), [t3])
        t5 = kb.act(lambda E, o=kb.bdc[:, h:h + 1], sc=sc: E.activation(
            out=o, in_=sc, func=AF.Exp, scale=-256.0), [t4])
        prev = t5
    ar.free("cscr")
    kb.prog.barrier(exclude=QACT_DOMS)
    return tz


def _rms_norm(kb, o_tm, dst_bf, sq, vv, deps):
    t1 = kb.act(lambda E: E.activation(out=sq, in_=o_tm, func=AF.Square,
                                       accum_out=vv), deps)
    t2 = kb.act(lambda E: E.activation(out=vv, in_=vv, func=AF.Sqrt,
                                       bias=kb.eps6, scale=1.0 / D), [t1])
    t3 = kb.dve(lambda E: E.reciprocal(vv, vv), [t2])
    t5 = kb.dve(lambda E: E.scalar_tensor_tensor(dst_bf, o_tm, vv, kb.rmsb,
                                                 ALU.mult, ALU.mult), [t3])
    return t5


def emit_prefix_chain(kb, x1p_ap):
    ar, prog = kb.ar, kb.prog
    xnPT = ar.alloc("xnPT", 16 * TOWN, BF16)
    xt = [ar.alloc(f"xtP{i}", D, F32) for i in range(2)]
    kb.sqdump = ar.alloc("sqsP", D, F32)
    lnscr = [tuple(ar.alloc(f"lnsP{i}_{j}", 1, F32) for j in range(4))
             for i in range(2)]
    xnv = ar.alloc("xnP", D, F32)
    xnb = [ar.alloc(f"xnbP{i}", D, BF16) for i in range(2)]
    tile_done = {}
    for i in range(NTO):
        b = i % 2
        dd = [tile_done.get(i - 2)]
        tl = kb.dma(xt[b], x1p_ap[i * 128:(i + 1) * 128, :], dd, q="qsp")
        t7 = tile_ln(kb, xt[b], xnv, lnscr[b], kb.eps5, [tl])
        t8 = kb.act(lambda E, o=xnb[b]: E.activation(out=o, in_=xnv,
                                                     func=AF.Copy), [t7])
        _, tlast = transpose16(
            kb, xnb[b], fview(xnPT, 16, TOWN, i * 128, (i + 1) * 128), [t8])
        tile_done[i] = tlast
    for n in ("xtP0", "xtP1", "sqsP", "xnP", "xnbP0", "xnbP1",
              "lnsP0_0", "lnsP0_1", "lnsP0_2", "lnsP0_3",
              "lnsP1_0", "lnsP1_1", "lnsP1_2", "lnsP1_3"):
        ar.free(n)
    prog.barrier(exclude=QACT_DOMS)

    wbuf = ar.alloc("wbufP", 16 * 512, BF16)
    k4 = ar.alloc("k4P", 8 * 512, BF16)
    v4 = ar.alloc("v4P", 8 * 512, BF16)
    kscr = ar.alloc("kscrP", 256, BF16)
    wtok = None
    grp_done = None
    for g in range(4):
        wtok = kb.stream_cast(wbuf, kb.inp["wqkv"], D, D + g * 512,
                              D + (g + 1) * 512, [wtok, grp_done])
        ktoks = []
        for ti in range(NTO):
            bi, bk, bd = kb.bank([wtok])
            tm = kb.mm16(bk, lambda f, ti=ti: xnPT[:, f * TOWN + ti * 128:
                                                   f * TOWN + (ti + 1) * 128],
                         lambda f: wbuf[:, f * 512:(f + 1) * 512], bd)
            te = kb.act(lambda E, bk=bk, o=k4[:, ti * 512:(ti + 1) * 512]:
                        E.activation(out=o, in_=bk, func=AF.Silu), [tm])
            kb.bank_last[bi] = te
            ktoks.append(te)
        wtok = kb.stream_cast(wbuf, kb.inp["wqkv"], D, 2 * D + g * 512,
                              2 * D + (g + 1) * 512, [wtok, join(ktoks)])
        vtoks = []
        for ti in range(NTO):
            bi, bk, bd = kb.bank([wtok])
            tm = kb.mm16(bk, lambda f, ti=ti: xnPT[:, f * TOWN + ti * 128:
                                                   f * TOWN + (ti + 1) * 128],
                         lambda f: wbuf[:, f * 512:(f + 1) * 512], bd)
            te = kb.act(lambda E, bk=bk, o=v4[:, ti * 512:(ti + 1) * 512]:
                        E.activation(out=o, in_=bk, func=AF.Silu), [tm])
            kb.bank_last[bi] = te
            vtoks.append(te)
        gtok = join(ktoks + vtoks)
        hlast = None
        for hh in range(4):
            h = g * 4 + hh
            for b4 in range(4):
                kst = []
                for t in range(2):
                    kst.append(kb.dve(
                        lambda E, o=kscr[:, t * 128:(t + 1) * 128],
                        a=k4[:, (b4 * 2 + t) * 512 + hh * 128:
                             (b4 * 2 + t) * 512 + (hh + 1) * 128],
                        s=kb.kdc[:, h * 2 + t:h * 2 + t + 1]:
                        E.tensor_scalar_mul(o, a, s), [gtok, hlast]))
                bi, bk, bd = kb.bank(kst)
                tm = None
                for t in range(2):
                    tm = kb.pe(lambda E, bk=bk,
                               lt=kscr[:, t * 128:(t + 1) * 128],
                               rh=v4[:, (b4 * 2 + t) * 512 + hh * 128:
                                     (b4 * 2 + t) * 512 + (hh + 1) * 128],
                               s=(t == 0), p=(t == 1):
                               E.matmul(bk[:, 0:128], lt, rh, start=s, stop=p),
                               bd if t == 0 else ())
                ts1 = kb.dve(lambda E, o=kb.Sst[:, h * 128:(h + 1) * 128],
                             s=kb.bdc[:, h:h + 1]:
                             E.tensor_scalar_mul(o, o, s), [tm, hlast])
                ts2 = kb.dve(lambda E, o=kb.Sst[:, h * 128:(h + 1) * 128], bk=bk:
                             E.tensor_add(o, o, bk[:, 0:128]), [ts1])
                kb.bank_last[bi] = ts2
                hlast = ts2
        grp_done = hlast
    tsel = kb.dve(lambda E: E.tensor_scalar_mul(kb.Sst, kb.Sst, kb.selc),
                  [grp_done])
    ar.free("xnPT", "wbufP", "k4P", "v4P", "kscrP")
    prog.barrier(exclude=QACT_DOMS)
    return tsel


def emit_l1_own(kb, dbg=None):
    ar, prog = kb.ar, kb.prog
    xn1T = ar.alloc("xn1T", 16 * TOWN, BF16)
    kb.sqdump = ar.alloc("sqsG", D, F32)
    lnscr = [tuple(ar.alloc(f"lnsG{i}_{j}", 1, F32) for j in range(4))
             for i in range(2)]
    xnv = ar.alloc("xnG", D, F32)
    xnb = [ar.alloc(f"xnbG{i}", D, BF16) for i in range(2)]
    tile_done = {}
    for i in range(NTO):
        b = i % 2
        dd = [tile_done.get(i - 2)]
        t7 = tile_ln(kb, kb.acc[:, i * D:(i + 1) * D], xnv, lnscr[b], kb.eps5, dd)
        t8 = kb.act(lambda E, o=xnb[b]: E.activation(out=o, in_=xnv,
                                                     func=AF.Copy), [t7])
        _, tlast = transpose16(
            kb, xnb[b], fview(xn1T, 16, TOWN, i * 128, (i + 1) * 128), [t8])
        tile_done[i] = tlast
    for n in ("sqsG", "xnG", "xnbG0", "xnbG1",
              "lnsG0_0", "lnsG0_1", "lnsG0_2", "lnsG0_3",
              "lnsG1_0", "lnsG1_1", "lnsG1_2", "lnsG1_3"):
        ar.free(n)
    prog.barrier(exclude=QACT_DOMS)
    kb.xn1T = xn1T

    OT = ar.alloc("OTl1", 16 * TOWN, BF16)
    wbuf = ar.alloc("wbufG", 16 * 512, BF16)
    q4 = ar.alloc("q4G", 8 * 512, BF16)
    k4 = ar.alloc("k4G", 8 * 512, BF16)
    v4 = ar.alloc("v4G", 8 * 512, BF16)
    kscr_ = [ar.alloc(f"kscrG{i}", 256, BF16) for i in range(2)]
    qscr_ = [ar.alloc(f"qscrG{i}", 256, BF16) for i in range(2)]
    qTb_ = [ar.alloc(f"qTbG{i}", 256, BF16) for i in range(2)]
    qsTb_ = [ar.alloc(f"qsTbG{i}", 256, BF16) for i in range(2)]
    kTb_ = [ar.alloc(f"kTbG{i}", 256, BF16) for i in range(2)]
    ATs_ = [ar.alloc(f"ATsG{i}", 512, BF16) for i in range(2)]
    Sb_ = [ar.alloc(f"SbG{i}", 128, BF16) for i in range(2)]
    dT4 = ar.alloc("dT4G", 512, F32)
    dscr = ar.alloc("dscrG", 512, F32)
    wtok = None
    grp_done = None
    trsq, stm, omm, tcm = {}, {}, {}, {}
    td2_prev = [None]
    at_prev = [None]
    for g in range(4):
        projs = []
        for which, dst, cbase in (("q", q4, g * 512), ("k", k4, D + g * 512),
                                  ("v", v4, 2 * D + g * 512)):
            wtok = kb.stream_cast(wbuf, kb.inp["wqkv"], D, cbase, cbase + 512,
                                  [wtok, grp_done] + projs[-1:])
            toks = []
            for ti in range(NTO):
                bi, bk, bd = kb.bank([wtok])
                tm = kb.mm16(bk, lambda f, ti=ti: xn1T[:, f * TOWN + ti * 128:
                                                       f * TOWN + (ti + 1) * 128],
                             lambda f: wbuf[:, f * 512:(f + 1) * 512], bd)
                te = kb.act(lambda E, bk=bk, o=dst[:, ti * 512:(ti + 1) * 512]:
                            E.activation(out=o, in_=bk, func=AF.Silu), [tm])
                kb.bank_last[bi] = te
                toks.append(te)
            projs.append(join(toks))
        gtok = join(projs)
        hlast = None
        for hh in range(4):
            h = g * 4 + hh
            s0 = (g * 4 + hh) * 4
            td1 = kb.dve(lambda E, s=kb.slopeb[:, h:h + 1]:
                         E.tensor_scalar_mul(dscr, kb.diffT, s),
                         [gtok, td2_prev[0]])
            td2 = kb.act(lambda E: E.activation(out=dT4, in_=dscr, func=AF.Exp,
                                                scale=-1.0),
                         [td1, at_prev[0]])
            td2_prev[0] = td2
            state_prev = None
            for b4 in range(4):
                s = s0 + b4
                pb = b4 % 2
                kscr, qscr, qTb, qsTb, kTb, ATs, Sb = (
                    kscr_[pb], qscr_[pb], qTb_[pb], qsTb_[pb], kTb_[pb],
                    ATs_[pb], Sb_[pb])
                base = lambda t, col: (b4 * 2 + t) * 512 + col * 128
                qst = []
                for t in range(2):
                    qst.append(kb.dve(
                        lambda E, o=qscr[:, t * 128:(t + 1) * 128],
                        a=q4[:, base(t, hh):base(t, hh) + 128],
                        s=kb.qdc[:, h * 2 + t:h * 2 + t + 1]:
                        E.tensor_scalar_mul(o, a, s),
                        [gtok, trsq.get(s - 2)]))
                trs = [kb.tgrp(
                    [q4[:, base(t, hh):base(t, hh) + 128] for t in range(2)]
                    + [qscr[:, t * 128:(t + 1) * 128] for t in range(2)],
                    [("act", qTb[:, 0:256], 0, 2, 128),
                     ("act", qsTb[:, 0:256], 2, 4, 128)],
                    [gtok, qst[0], qst[1], stm.get(s - 2), omm.get(s - 2)],
                    site="g1"),
                    kb.tgrp(
                    [k4[:, base(t, hh):base(t, hh) + 128] for t in range(2)],
                    [("dve", kTb[:, 0:256], 0, 2, 128)],
                    [gtok, stm.get(s - 2)], site="g1")]
                trsq[s] = trs[0]
                tq = join(trs)
                bi, bk, bd = kb.bank([tq])
                st_toks = []
                for t in range(2):
                    st_toks.append(kb.pe(
                        lambda E, bk=bk, lt=kTb[:, t * 128:(t + 1) * 128],
                        rh=qTb, o=(0 if t == 0 else 256):
                        E.matmul(bk[:, o:o + 256], lt, rh, start=True,
                                 stop=True), bd if t == 0 else ()))
                stm[s] = st_toks[1]
                at = []
                for t in range(2):
                    at.append(kb.dve(
                        lambda E, o=ATs[:, t * 256:(t + 1) * 256],
                        a=bk[:, t * 256:(t + 1) * 256],
                        dd=dT4[:, t * 256:(t + 1) * 256]:
                        E.tensor_mul(o, a, dd),
                        [st_toks[1], td2, omm.get(s - 2)]))
                kb.bank_last[bi] = join(at)
                at_prev[0] = at[1]
                tSb = kb.act(lambda E, a=kb.Sst[:, h * 128:(h + 1) * 128],
                             o=Sb: E.copy(o, a),
                             [state_prev, omm.get(s - 2)])
                bi2, bk2, bd2 = kb.bank(at + [tSb])
                tm = None
                for t in range(2):
                    tm = kb.pe(lambda E, bk2=bk2,
                               lt=v4[:, base(t, hh):base(t, hh) + 128],
                               rh=ATs[:, t * 256:(t + 1) * 256],
                               s=(t == 0):
                               E.matmul(bk2[:, 0:256], lt, rh, start=s,
                                        stop=False), bd2 if t == 0 else ())
                tm = kb.pe(lambda E, bk2=bk2, lt=Sb, rh=qsTb: E.matmul(
                    bk2[:, 0:256], lt, rh, start=False, stop=True), ())
                omm[s] = tm
                tev = kb.act(lambda E, bk2=bk2,
                             o=OT[:, h * TOWN + b4 * 256:h * TOWN + (b4 + 1) * 256]:
                             E.copy(o, bk2[:, 0:256]), [tm])
                kb.bank_last[bi2] = tev
                kst = []
                for t in range(2):
                    kst.append(kb.dve(
                        lambda E, o=kscr[:, t * 128:(t + 1) * 128],
                        a=k4[:, base(t, hh):base(t, hh) + 128],
                        s=kb.kdc[:, h * 2 + t:h * 2 + t + 1]:
                        E.tensor_scalar_mul(o, a, s),
                        [gtok, tcm.get(s - 2)]))
                bi3, bk3, bd3 = kb.bank(kst + [tSb])
                tc = None
                for t in range(2):
                    tc = kb.pe(lambda E, bk3=bk3,
                               lt=kscr[:, t * 128:(t + 1) * 128],
                               rh=v4[:, base(t, hh):base(t, hh) + 128],
                               s=(t == 0), p=(t == 1):
                               E.matmul(bk3[:, 0:128], lt, rh, start=s, stop=p),
                               bd3 if t == 0 else ())
                tcm[s] = tc
                ts1 = kb.dve(lambda E, o=kb.Sst[:, h * 128:(h + 1) * 128],
                             s=kb.bdc[:, h:h + 1]:
                             E.tensor_scalar_mul(o, o, s), [tc, tSb])
                ts2 = kb.dve(lambda E, o=kb.Sst[:, h * 128:(h + 1) * 128], bk3=bk3:
                             E.tensor_add(o, o, bk3[:, 0:128]), [ts1])
                kb.bank_last[bi3] = ts2
                state_prev = ts2
                hlast = join([ts2, tev])
        grp_done = hlast
    ar.free("wbufG", "q4G", "k4G", "v4G", "dT4G", "dscrG",
            *[f"{n}{i}" for n in ("kscrG", "qscrG", "qTbG", "qsTbG", "kTbG",
                                  "ATsG", "SbG") for i in range(2)])
    prog.barrier(exclude=QACT_DOMS)
    if dbg is not None:
        kb.dma(dbg[0], OT, [], q="qsp")
        prog.barrier(exclude=QACT_DOMS)

    o_nb = ar.alloc("o_nb", 8 * D, BF16)
    o_tm = ar.alloc("o_tm", D, F32)
    sq = ar.alloc("sqG2", D, F32)
    vv = ar.alloc("vvG2", 1, F32)
    prev = None
    for ti in range(NTO):
        toks = []
        for g in range(4):
            srcs = [OT[:, (g * 4 + j) * TOWN + ti * 128:
                       (g * 4 + j) * TOWN + (ti + 1) * 128] for j in range(4)]
            toks.append(kb.tgrp(
                srcs, [("dve" if g % 2 else "act",
                        o_tm[:, g * 512:(g + 1) * 512], 0, 4, 128)],
                [prev] if prev else [], site="g2a"))
        t5 = _rms_norm(kb, o_tm, o_nb[:, ti * D:(ti + 1) * D], sq, vv, [join(toks)])
        prev = t5
    if dbg is not None:
        prog.barrier(exclude=QACT_DOMS)
        kb.dma(dbg[1], o_nb, [], q="qsp")
    ar.free("OTl1", "o_tm", "sqG2", "vvG2")
    prog.barrier(exclude=QACT_DOMS)

    ogT = ar.alloc("ogT", 16 * TOWN, BF16)
    wgc = ar.alloc("wgcG", 16 * 512, BF16)
    sgel = ar.alloc("sgelG", 512, F32)
    ogc = ar.alloc("ogcG", 512, BF16)
    wtok = None
    prev_mul = None
    prev_tr = None
    for ch in range(4):
        wtok = kb.stream_cast(wgc, kb.inp["wgate"], D, ch * 512, (ch + 1) * 512,
                              [wtok, prev_tr])
        for ti in range(NTO):
            bi, bk, bd = kb.bank([wtok])
            tm = kb.mm16(bk, lambda f, ti=ti: xn1T[:, f * TOWN + ti * 128:
                                                   f * TOWN + (ti + 1) * 128],
                         lambda f: wgc[:, f * 512:(f + 1) * 512], bd)
            tsg = kb.act(lambda E, bk=bk: E.activation(out=sgel, in_=bk,
                                                       func=AF.Sigmoid),
                         [tm, prev_mul])
            kb.bank_last[bi] = tsg
            tmul = kb.dve(lambda E, ti=ti, ch=ch: E.tensor_mul(
                ogc, sgel, o_nb[:, ti * D + ch * 512:ti * D + (ch + 1) * 512]),
                [tsg, prev_tr])
            prev_tr = kb.tgrp(
                [ogc[:, j * 128:(j + 1) * 128] for j in range(4)],
                [("act" if ti % 2 else "dve",
                  fview(ogT, 16, TOWN, ti * 128, (ti + 1) * 128)
                  [:, ch * 4:ch * 4 + 4, :], 0, 4, 128)], [tmul],
                site="g2b")
            prev_mul = tmul
    ar.free("o_nb", "wgcG", "sgelG", "ogcG")
    prog.barrier(exclude=QACT_DOMS)

    woc = ar.alloc("wocG", 16 * 512, BF16)
    wtok = None
    chunk_done = {}
    for ch in range(4):
        wtok = kb.stream_cast(woc, kb.inp["wout"], D, ch * 512, (ch + 1) * 512,
                              [wtok, chunk_done.get(ch - 1)])
        last = None
        for ti in range(NTO):
            bi, bk, bd = kb.bank([wtok])
            tm = kb.mm16(bk, lambda f, ti=ti: ogT[:, f * TOWN + ti * 128:
                                                  f * TOWN + (ti + 1) * 128],
                         lambda f: woc[:, f * 512:(f + 1) * 512], bd)
            ta = kb.dve(lambda E, bk=bk, o=kb.acc[:, ti * D + ch * 512:
                                                  ti * D + (ch + 1) * 512]:
                        E.tensor_add(o, o, bk), [tm])
            kb.bank_last[bi] = ta
            last = ta
        chunk_done[ch] = last
    ar.free("ogT", "wocG")
    prog.barrier(exclude=QACT_DOMS)
    if dbg is not None:
        kb.dma(dbg[2], kb.acc, [], q="qsp")
        prog.barrier(exclude=QACT_DOMS)


def emit_l1_ffn(kb, dbg=None):
    ar, prog = kb.ar, kb.prog
    xn1T = kb.xn1T
    ar.free("Sst", "qdc", "kdc", "bdc", "diffT", "iotaQ", "iotaK", "rmsb",
            "slopeb", "selc")
    NW = FF1 * D
    gacc = kb.gacc
    pcol = ar.alloc("pcol", 1, F32)
    invg = ar.alloc("invg", 4, F32)
    prev_add = kb.gamma_tok
    thr3 = invg
    nthr3 = ar.alloc("nthr3", 4, F32)
    prevb = prev_add
    grow = ar.alloc("grow", 128, F32)
    for t in range(3):
        tt = kb.ttile(gacc[:, t:t + 1], grow[0:1, 0:128], [prevb], fp32=True,
                      evac="dve")
        t2 = kb.dve(lambda E: E.reduce_sum(pcol[0:1, :], grow[0:1, 0:128],
                                           axis=AX.X), [tt])
        t3 = kb.dve(lambda E: E.tensor_scalar(pcol[0:1, :], pcol[0:1, :],
                                              0.5 / NW, 0.5e-8, ALU.mult,
                                              ALU.add), [t2])
        bi, bk, bd = kb.bank([t3])
        t5 = kb.pe(lambda E, bk=bk: E.matmul(bk[:, 0:1], kb.onesf[0:1, :],
                                             pcol[0:1, 0:1], start=True,
                                             stop=True), bd)
        t6 = kb.dve(lambda E, o=thr3[:, t:t + 1], bk=bk: E.tensor_copy(
            o, bk[:, 0:1]), [t5])
        t7 = kb.dve(lambda E, o=nthr3[:, t:t + 1], a=thr3[:, t:t + 1]:
                    E.tensor_scalar_mul(o, a, -1.0), [t6])
        kb.bank_last[bi] = t7
        prevb = t7
    if dbg is not None:
        kb.dma(dbg, thr3, [prevb], q="qsp")
    ar.free(*kb.gamma_bufs)
    ar.free("pcol", "gacc", "grow")
    prog.barrier(exclude=QACT_DOMS)

    NWFC = 4
    wfc_ = [ar.alloc(f"wfcH{i}", 1024, F32) for i in range(NWFC)]
    cbuf_ = [ar.alloc(f"cbufH{i}", 1024, BF16) for i in range(2)]
    w1q_ = [ar.alloc(f"w1qH{i}", 16 * 512, BF16) for i in range(2)]
    w2q_ = [ar.alloc(f"w2qH{i}", 16 * 512, BF16) for i in range(2)]
    w3q = ar.alloc("w3qH", 4 * D, BF16)
    sgel_ = [ar.alloc(f"sgelH{i}", 512, F32) for i in range(2)]
    hidb_ = [ar.alloc(f"hidbH{i}", 512, BF16) for i in range(2)]
    hTt_ = [ar.alloc(f"hTtH{i}", 4 * 128, BF16) for i in range(2)]
    NCH = FF1 // 512
    srcs = {nm: kb.inp[nm].rearrange("(t p) n -> p t n", p=128)
            for nm in ("fw1T", "fw2T", "fw3T")}

    def pieces(hc):
        pl = []
        for tnum, nm in ((0, "fw1T"), (1, "fw2T")):
            for j in range(8):
                pl.append((tnum, 2 * j, 2, hc * 512, 512, 2 * j * 512))
        for j in range(8):
            pl.append((2, hc * 4 + j // 2, 1, (j % 2) * 1024, 1024,
                       (j // 2) * D + (j % 2) * 1024))
        return pl

    all_pieces = [p for hc in range(NCH) for p in pieces(hc)]
    wfc_readers = [None] * NWFC
    cbuf_reader = [None, None]
    users12 = {}
    user3 = [None]
    dma_toks = {}
    st = {"di": 0}

    def pump_dma(upto):
        while st["di"] < min(upto, len(all_pieces)):
            i = st["di"]
            st["di"] += 1
            tnum, rbase, ntl, c0, ncols, dbase = all_pieces[i]
            buf = wfc_[i % NWFC]
            nm = ("fw1T", "fw2T", "fw3T")[tnum]
            dma_toks[i] = kb.dma(
                buf[:, 0:ntl * ncols].rearrange("p (t n) -> p t n", t=ntl),
                srcs[nm][:, rbase:rbase + ntl, c0:c0 + ncols],
                [wfc_readers[i % NWFC]], q="qsp")

    def emit_cmp(i):
        pump_dma(i + NWFC)
        hc_i, tnum = i // 24, all_pieces[i][0]
        _, rbase, ntl, c0, ncols, dbase = all_pieces[i]
        buf, cb = wfc_[i % NWFC], cbuf_[i % 2]
        dstq = (w1q_[hc_i % 2], w2q_[hc_i % 2], w3q)[tnum]
        user = user3[0] if tnum == 2 else users12.get((tnum, hc_i % 2))
        wtok = dma_toks.pop(i)
        tq1 = kb.dve(lambda E, a=buf[:, 0:ntl * ncols],
                     o=dstq[:, dbase:dbase + ntl * ncols],
                     s=thr3[:, tnum:tnum + 1]:
                     E.tensor_single_scalar(o, a, s, ALU.is_gt),
                     [wtok, user])
        tq1b = kb.dve(lambda E, a=buf[:, 0:ntl * ncols], o=cb[:, 0:ntl * ncols],
                      s=nthr3[:, tnum:tnum + 1]:
                      E.tensor_single_scalar(o, a, s, ALU.is_lt),
                      [wtok, cbuf_reader[i % 2]])
        tq2 = kb.pool_op(lambda E, o=dstq[:, dbase:dbase + ntl * ncols],
                         a=cb[:, 0:ntl * ncols]: E.tensor_sub(o, o, a),
                         [tq1, tq1b, user])
        wfc_readers[i % NWFC] = join([tq1, tq1b])
        cbuf_reader[i % 2] = tq2
        return tq2

    q12done = {}
    q3done = {}
    for i in range(16):
        q12done[0] = emit_cmp(i)
    for i in range(16, 24):
        q3done[0] = emit_cmp(i)
    hc_done = None
    prev_hid = None
    prev_htr = None
    if dbg is not None:
        td1 = kb.dma(kb.dbgQ1, w1q_[0], [q12done[0]], q="qsp")
        td3 = kb.dma(kb.dbgQ3, w3q, [q3done[0]], q="qsp")
        q12done[0] = join([q12done[0], td1])
        q3done[0] = join([q3done[0], td3])
        hc_done = join([td1, td3])
    for hc in range(NCH):
        w1q, w2q = w1q_[hc % 2], w2q_[hc % 2]
        last = None
        for ti in range(NTO):
            if hc + 1 < NCH:
                for i in range(24 * (hc + 1) + 2 * ti,
                               24 * (hc + 1) + 2 * ti + 2):
                    q12done[hc + 1] = emit_cmp(i)
            sgel, hidb, hTt = sgel_[ti % 2], hidb_[ti % 2], hTt_[ti % 2]
            bi1, bk1, bd1 = kb.bank([q12done[hc]])
            tm1 = kb.mm16(bk1, lambda f, ti=ti: xn1T[:, f * TOWN + ti * 128:
                                                     f * TOWN + (ti + 1) * 128],
                          lambda f, w1q=w1q: w1q[:, f * 512:(f + 1) * 512], bd1)
            bi2, bk2, bd2 = kb.bank([q12done[hc]])
            tm2 = kb.mm16(bk2, lambda f, ti=ti: xn1T[:, f * TOWN + ti * 128:
                                                     f * TOWN + (ti + 1) * 128],
                          lambda f, w2q=w2q: w2q[:, f * 512:(f + 1) * 512], bd2)
            tsg = kb.act(lambda E, bk1=bk1, o=sgel: E.activation(
                out=o, in_=bk1, func=AF.Sigmoid), [tm1, prev_hid])
            kb.bank_last[bi1] = tsg
            thid = kb.dve(lambda E, bk2=bk2, o=hidb, sg=sgel:
                          E.tensor_mul(o, sg, bk2), [tsg, tm2, prev_htr])
            kb.bank_last[bi2] = thid
            last_tm = {"fw1T": tm1, "fw2T": tm2}
            if dbg is not None and hc == 0 and ti == 0:
                thid = join([thid, kb.dma(kb.dbgHid, hidb, [thid], q="qsp")])
            htr = kb.tgrp([hidb[:, j * 128:(j + 1) * 128] for j in range(4)],
                          [("act" if ti % 2 else "dve", hTt[:, 0:512],
                            0, 4, 128)], [thid], site="h1")
            for ch in range(4):
                bi3, bk3, bd3 = kb.bank([htr, q3done[hc]])
                tm3 = None
                for jt in range(4):
                    tm3 = kb.pe(lambda E, bk3=bk3,
                                lt=hTt[:, jt * 128:(jt + 1) * 128],
                                rh=w3q[:, jt * D + ch * 512:jt * D + (ch + 1) * 512],
                                s=(jt == 0), p=(jt == 3):
                                E.matmul(bk3, lt, rh, start=s, stop=p),
                                bd3 if jt == 0 else ())
                ta = kb.dve(lambda E, bk3=bk3,
                            o=kb.acc[:, ti * D + ch * 512:ti * D + (ch + 1) * 512]:
                            E.tensor_add(o, o, bk3), [tm3])
                kb.bank_last[bi3] = ta
                last = ta
            last_tm["fw3T"] = tm3
            prev_hid = tsg
            prev_htr = htr
        users12[(0, hc % 2)] = last_tm["fw1T"]
        users12[(1, hc % 2)] = last_tm["fw2T"]
        user3[0] = last_tm["fw3T"]
        if hc + 1 < NCH:
            for i in range(24 * (hc + 1) + 16, 24 * (hc + 1) + 24):
                q3done[hc + 1] = emit_cmp(i)
        hc_done = last
    ar.free("w3qH", "invg", "nthr3",
            *[f"wfcH{i}" for i in range(NWFC)],
            *[f"{n}{i}" for n in ("cbufH", "w1qH", "w2qH",
                                  "sgelH", "hidbH", "hTtH") for i in range(2)])
    prog.barrier(exclude=QACT_DOMS)


def emit_out(kb, out_ap):
    tok = None
    for ti in range(NTO):
        tok = kb.dma(out_ap[ti * 128:(ti + 1) * 128, :],
                     kb.acc[:, ti * D:(ti + 1) * D], [], q="qsp")
    kb.prog.barrier(exclude=QACT_DOMS)
    return tok


def setup_min(kb):
    for name, shape in (("identb", (128, 128)), ("identf", (128, 128)),
                        ("eps5", (128, 1)), ("eps6", (128, 1))):
        dt = BF16 if name == "identb" else F32
        kb.din(name, shape, dt)
    ar = kb.ar
    kb.identb = ar.alloc("identb", 128, BF16)
    kb.identf = ar.alloc("identf", 128, F32)
    kb.eps5 = ar.alloc("eps5", 1, F32)
    kb.eps6 = ar.alloc("eps6", 1, F32)
    for n in ("identb", "identf", "eps5", "eps6"):
        kb.dma(getattr(kb, n), kb.inp[n][:, :], [], q="qsp")


DEBUG_L1 = False


def build_l1():
    kb = KB()
    setup_min(kb)
    declare_l1(kb)
    load_l1_consts(kb)
    kb.gamma = GammaStream(kb)
    x1own = kb.din("x1own", (TOWN, D))
    x1pre = kb.din("x1pre", (TOWN, D))
    out = kb.dout("out", (TOWN, D))
    if DEBUG_L1:
        kb.dbgQ1 = kb.dout("dbgQ1", (128, 16 * 512), BF16)
        kb.dbgQ3 = kb.dout("dbgQ3", (128, 4 * D), BF16)
        kb.dbgHid = kb.dout("dbgHid", (128, 512), BF16)
        dbgS = kb.dout("dbgS", (128, 16 * 128))
        dbgOT = kb.dout("dbgOT", (128, 16 * TOWN), BF16)
        dbgON = kb.dout("dbgON", (128, 8 * D), BF16)
        dbgA = kb.dout("dbgA", (128, 8 * D))
        dbgI = kb.dout("dbgI", (128, 4))
    for ti in range(NTO):
        kb.dma(kb.acc[:, ti * D:(ti + 1) * D], x1own[ti * 128:(ti + 1) * 128, :],
               [], q="qsp")
    kb.prog.barrier(exclude=QACT_DOMS)
    kb.gamma.finishup()
    emit_l1_consts(kb)
    emit_prefix_chain(kb, x1pre)
    if DEBUG_L1:
        kb.prog.barrier(exclude=QACT_DOMS)
        kb.dma(dbgS, kb.Sst, [], q="qsp")
        kb.prog.barrier(exclude=QACT_DOMS)
    emit_l1_own(kb, dbg=(dbgOT, dbgON, dbgA) if DEBUG_L1 else None)
    emit_l1_ffn(kb, dbg=dbgI if DEBUG_L1 else None)
    emit_out(kb, out)
    finish(kb)
    return kb



def _seg_x(x, b, start):
    out = np.zeros((SEGT, D), np.float32)
    lo = start - HALO
    src_lo = max(lo, 0)
    out[src_lo - lo:HALO] = x[b, src_lo:start]
    out[HALO:] = x[b, start:start + TOWN]
    return out


def _seg_mask(start):
    kmin = HALO - start if start < HALO else 0
    i = np.arange(128)[:, None]
    c = np.arange(384)[None, :]
    band = (c >= i + 1) & (c <= i + 256)
    out = np.zeros((3, 128, 400), np.float32)
    for q in range(3):
        out[q, :, 0:384] = np.where(band & ((c + q * 128) >= kmin), 0.0, NEG)
    return np.ascontiguousarray(out)


def _bcast(v):
    v = np.asarray(v, np.float32).reshape(1, -1)
    return np.ascontiguousarray(np.broadcast_to(v, (128, v.shape[1])))


def stage_shared(ins):
    t = lambda a: np.ascontiguousarray(np.asarray(a, np.float32).T)
    sh = {
        "wqT": t(ins["L0_attn_Wq"]), "wkT": t(ins["L0_attn_Wk"]),
        "wvT": t(ins["L0_attn_Wv"]), "woT": t(ins["L0_attn_Wo"]),
        "wrT": t(ins["L0_ffn_Wr"]),
        "w1T0": t(ins["L0_ffn_W1"]), "w2T0": t(ins["L0_ffn_W2"]),
        "gT": t(np.asarray(ins["L0_attn_global_tokens"], np.float32)[0]),
        "ln0s": _bcast(ins["L0_attn_ln_scale"]),
        "ln0b": _bcast(ins["L0_attn_ln_bias"]),
        "brb": _bcast(ins["L0_ffn_br"]),
        "identb": np.eye(128, dtype=ml_dtypes.bfloat16),
        "identf": np.eye(128, dtype=np.float32),
        "eps5": np.full((128, 1), 1e-5, np.float32),
        "eps6": np.full((128, 1), 1e-6, np.float32),
    }
    return sh


def stage_core_l0(x, core):
    b, h = core // 2, core % 2
    return {"xO": _seg_x(x, b, h * TOWN), "maskO": _seg_mask(h * TOWN)}


def stage_shared_l1(ins):
    t = lambda a: np.ascontiguousarray(np.asarray(a, np.float32).T)
    c = lambda a: np.ascontiguousarray(np.asarray(a, np.float32))
    m = np.arange(256)[:, None]
    l = np.arange(256)[None, :]
    dd = np.where(l >= m, (l - m).astype(np.float32), 1.0e5)
    diffT = np.ascontiguousarray(
        dd.reshape(2, 128, 256).transpose(1, 0, 2).reshape(128, 512))
    return {
        "wqkv": c(ins["L1_attn_w_qkv"]), "wgate": c(ins["L1_attn_w_gate"]),
        "wout": c(ins["L1_attn_w_out"]),
        "rmsb": _bcast(ins["L1_attn_rms_weight"]),
        "slopeb": _bcast(np.asarray(ins["L1_attn_slope_rate"], np.float32).reshape(-1)),
        "fw1T": t(ins["L1_ffn_w1"]), "fw2T": t(ins["L1_ffn_w2"]),
        "fw3T": t(ins["L1_ffn_w3"]),
        "iotaQ": np.ascontiguousarray(
            (np.arange(256).reshape(2, 128).T + 1).astype(np.float32)),
        "iotaK": np.ascontiguousarray(
            (255 - np.arange(256).reshape(2, 128).T).astype(np.float32)),
        "diffT": diffT,
        "onesf": np.ones((1, 128), np.float32),
        "identb": np.eye(128, dtype=ml_dtypes.bfloat16),
        "identf": np.eye(128, dtype=np.float32),
        "eps5": np.full((128, 1), 1e-5, np.float32),
        "eps6": np.full((128, 1), 1e-6, np.float32),
    }


def build_fused():
    kb = KB()
    setup_min(kb)
    declare_l0(kb)
    declare_l1(kb)
    xO = kb.din("xO", (SEGT, D))
    xP = kb.din("xP", (SEGT, D))
    maskO = kb.din("maskO", (3, 128, 400))
    maskP = kb.din("maskP", (3, 128, 400))
    out = kb.dout("out", (TOWN, D))
    x1P = kb.nc.dram_tensor("x1Pscr", [TOWN, D], F32).ap()
    load_l0_consts(kb)
    kb.gamma = GammaStream(kb)
    emit_l0_segment(kb, xP, maskP, x1P)
    emit_l0_segment(kb, xO, maskO, None)
    kb.ar.free("brb", "gTb", "wrTb")
    load_l1_consts(kb)
    kb.gamma.finishup()
    emit_l1_consts(kb)
    emit_prefix_chain(kb, x1P)
    emit_l1_own(kb)
    emit_l1_ffn(kb)
    emit_out(kb, out)
    finish(kb)
    return kb


FUSED = True


def kernel(**ins):
    if FUSED:
        return kernel_fused(ins)
    return kernel_2launch(ins)


def kernel_fused(ins):
    x = np.asarray(ins["L0_attn_x"], np.float32)
    sh = stage_shared(ins)
    sh.update(stage_shared_l1(ins))
    kb = build_fused()
    maps = []
    for c in range(8):
        b, h = c // 2, c % 2
        m = dict(sh)
        m["xO"] = _seg_x(x, b, h * TOWN)
        m["maskO"] = _seg_mask(h * TOWN)
        m["xP"] = _seg_x(x, b, (1 - h) * TOWN)
        m["maskP"] = _seg_mask((1 - h) * TOWN)
        m["selc"] = np.full((128, 1), float(h), np.float32)
        maps.append({k: m[k] for k in kb.inp})
    r = run_bass_kernel_spmd(kb.nc, maps, core_ids=list(range(8)))
    out = np.empty((4, 2048, D), np.float32)
    for c in range(8):
        b, h = c // 2, c % 2
        out[b, h * TOWN:(h + 1) * TOWN] = r.results[c]["out"]
    return out


def kernel_2launch(ins):
    x = np.asarray(ins["L0_attn_x"], np.float32)
    sh = stage_shared(ins)
    kb1 = build_l0()
    maps1 = []
    for c in range(8):
        m = dict(sh)
        m.update(stage_core_l0(x, c))
        maps1.append({k: m[k] for k in kb1.inp})
    r1 = run_bass_kernel_spmd(kb1.nc, maps1, core_ids=list(range(8)))
    sh1 = stage_shared_l1(ins)
    kb2 = build_l1()
    maps2 = []
    for c in range(8):
        h = c % 2
        m = dict(sh1)
        m["x1own"] = np.ascontiguousarray(r1.results[c]["x1"])
        m["x1pre"] = np.ascontiguousarray(r1.results[c ^ 1]["x1"])
        m["selc"] = np.full((128, 1), float(h), np.float32)
        maps2.append({k: m[k] for k in kb2.inp})
    r2 = run_bass_kernel_spmd(kb2.nc, maps2, core_ids=list(range(8)))
    out = np.empty((4, 2048, D), np.float32)
    for c in range(8):
        b, h = c // 2, c % 2
        out[b, h * TOWN:(h + 1) * TOWN] = r2.results[c]["out"]
    return out


# version 29
# speedup vs baseline: 1.0033x; 1.0033x over previous
import math
import numpy as np
import ml_dtypes
import concourse.bass as bass
import concourse.mybir as mybir
from concourse.bass_utils import run_bass_kernel_spmd

F32 = mybir.dt.float32
BF16 = mybir.dt.bfloat16
I8 = mybir.dt.int8
AF = mybir.ActivationFunctionType
ALU = mybir.AluOpType
AX = mybir.AxisListType

D = 2048
TOWN = 1024
HALO = 256
SEGT = HALO + TOWN
NT = SEGT // 128
NTO = TOWN // 128
G = 16
FF0 = 8192
FF1 = 5632
NEG = -1.0e30
INVSQ = 1.0 / math.sqrt(128.0)

COMPUTE = ("pe", "act", "dve", "sp", "pool")
QACT_DOMS = ("qact0", "qact1", "qact2", "qact3")
NSLOT = 4
QUEUES = ("qsp", "qact", "qpool")
SPLIT_TGRP = True
BATCH_SITES = {"g2a", "g2b", "h1"}
DRAIN_K = 10**9
DOMS = COMPUTE + tuple(f"{q}{s}" for q in QUEUES for s in range(NSLOT))


class Tok:
    __slots__ = ("dom", "val", "fr")

    def __init__(self, dom, val, fr):
        self.dom, self.val, self.fr = dom, val, fr


def _merge(dst, src):
    for k, v in src.items():
        if dst.get(k, 0) < v:
            dst[k] = v


class JTok:
    __slots__ = ("doms", "fr")

    def __init__(self, doms, fr):
        self.doms = doms
        self.fr = fr


def join(toks):
    toks = [t for t in toks if t is not None]
    fr = {}
    doms = {}
    for t in toks:
        _merge(fr, t.fr)
        if isinstance(t, JTok):
            _merge(doms, t.doms)
        else:
            if doms.get(t.dom, 0) < t.val:
                doms[t.dom] = t.val
    return JTok(doms, fr)


class Prog:
    def __init__(self):
        self.ops = {e: [] for e in COMPUTE}
        self.cnt = {d: 0 for d in DOMS}
        self.base = {}

    def op(self, eng, fn, deps=(), queue=None):
        dom = queue if queue is not None else eng
        inc = 16 if queue is not None else 1
        fr = dict(self.base)
        direct = 0
        for t in deps:
            if t is not None:
                _merge(fr, t.fr)
                if isinstance(t, JTok):
                    v = t.doms.get(eng, 0)
                    if v > direct:
                        direct = v
                elif t.dom == eng and t.val > direct:
                    direct = t.val
        val = self.cnt[dom] + inc
        self.cnt[dom] = val
        fr2 = dict(fr)
        fr2[dom] = val
        self.ops[eng].append((fr, fn, dom, inc, direct))
        return Tok(dom, val, fr2)

    def barrier(self, exclude=()):
        self.base = {d: self.cnt[d] for d in DOMS
                     if self.cnt[d] > 0 and d not in exclude}

    def emitters(self, sems):
        out = {}
        for eng in COMPUTE:
            oplist = self.ops[eng]

            def body(e, eng=eng, oplist=oplist):
                guaranteed = {}
                semcnt = 0
                drainmark = 0
                for fr, fn, dom, inc, direct in oplist:
                    if eng != "pe" and direct > drainmark and \
                            direct > semcnt - DRAIN_K:
                        e.drain()
                        drainmark = semcnt
                    for d in sorted(fr):
                        if d == eng:
                            continue
                        v = fr[d]
                        if v > guaranteed.get(d, 0):
                            e.wait_ge(sems[d], v)
                            guaranteed[d] = v
                    inst = fn(e)
                    if inst is not None:
                        inst.then_inc(sems[dom], inc)
                    if dom == eng:
                        semcnt += inc
            out[eng] = body
        return out


class Arena:
    def __init__(self, pool_ap, nbytes):
        self.pool = pool_ap
        self.live = {}
        self.free_list = [(0, nbytes)]
        self.hw = 0

    def alloc(self, name, cols, dtype):
        size = (cols * mybir.dt.size(dtype) + 63) & ~63
        cands = [(sz, i) for i, (off, sz) in enumerate(self.free_list)
                 if sz >= size]
        if cands:
            _, i = min(cands)
            off, sz = self.free_list[i]
            if size <= 4096:
                start = off + sz - size
                if sz == size:
                    del self.free_list[i]
                else:
                    self.free_list[i] = (off, sz - size)
            else:
                start = off
                if sz == size:
                    del self.free_list[i]
                else:
                    self.free_list[i] = (off + size, sz - size)
            self.live[name] = (start, size)
            self.hw = max(self.hw, start + size)
            return self.pool[:, start:start + cols * mybir.dt.size(dtype)].bitcast(dtype)
        raise MemoryError(f"arena: no room for {name} ({size}B) "
                          f"live={sorted((k, v[1]) for k, v in self.live.items())}")

    def free(self, *names):
        for name in names:
            off, sz = self.live.pop(name)
            self.free_list.append((off, sz))
        self.free_list.sort()
        out = []
        for off, sz in self.free_list:
            if out and out[-1][0] + out[-1][1] == off:
                out[-1][1] += sz
            else:
                out.append([off, sz])
        self.free_list = [tuple(x) for x in out]



class KB:
    def __init__(self):
        nc = bass.Bass("TRN2", target_bir_lowering=False, debug=False)
        self.nc = nc
        self.inp = {}
        self.prog = Prog()
        self.pool_h = nc.alloc_sbuf_tensor("pool", [128, 146176], mybir.dt.uint8)
        self.acc = nc.alloc_sbuf_tensor("accb", [128, 8 * D], F32).ap()
        self.ar = Arena(self.pool_h.ap(), 146176)
        self.psum = [nc.alloc_psum_tensor(f"ps{i}", [128, 512], F32).ap() for i in range(4)]
        self.psT = [nc.alloc_psum_tensor(f"psTf{i}", [128, 512], F32).ap() for i in range(4)]
        self.psT_last = [None] * 4
        self.bank_last = {}
        self.ps_rr = 0
        self.tr_rr = 0
        self.persist = {}
        self.xnT_scr = nc.dram_tensor("xnT_scr", [128, 16 * TOWN], BF16).ap()
        self.qidx = {q: 0 for q in QUEUES}
        self.qlast = {f"{q}{s}": None for q in QUEUES for s in range(NSLOT)}

    def din(self, name, shape, dtype=F32):
        ap = self.nc.dram_tensor(name, list(shape), dtype, kind="ExternalInput").ap()
        self.inp[name] = ap
        return ap

    def dout(self, name, shape, dtype=F32):
        return self.nc.dram_tensor(name, list(shape), dtype, kind="ExternalOutput").ap()

    def dma(self, out, in_, deps=(), q="qsp"):
        engmap = {"qsp": "sp", "qact": "act", "qpool": "pool"}
        slot = self.qidx[q] % NSLOT
        self.qidx[q] += 1
        dom = f"{q}{slot}"
        dd = list(deps)
        if self.qlast[dom] is not None:
            dd.append(self.qlast[dom])
        t = self.prog.op(engmap[q],
                         lambda E, o=out, i=in_: E.dma_start(out=o, in_=i),
                         dd, queue=dom)
        self.qlast[dom] = t
        return t

    def pe(self, fn, deps=()):
        return self.prog.op("pe", fn, deps)

    def act(self, fn, deps=()):
        return self.prog.op("act", fn, deps)

    def dve(self, fn, deps=()):
        return self.prog.op("dve", fn, deps)

    def pool_op(self, fn, deps=()):
        return self.prog.op("pool", fn, deps)

    def bank(self, deps):
        i = self.ps_rr % len(self.psum)
        self.ps_rr += 1
        dd = list(deps)
        if i in self.bank_last:
            dd.append(self.bank_last[i])
        return i, self.psum[i], dd

    def bankT(self, deps):
        r = self.tr_rr % 4
        self.tr_rr += 1
        dd = list(deps)
        if self.psT_last[r] is not None:
            dd.append(self.psT_last[r])
        return r, self.psT[r], dd

    def mm16(self, bank_ap, lhsT_fn, rhs_fn, deps, nk=16):
        t = None
        for f in range(nk):
            dd = deps if f == 0 else ()
            lt, rh = lhsT_fn(f), rhs_fn(f)
            t = self.pe(lambda E, lt=lt, rh=rh, o=bank_ap, s=(f == 0),
                        p=(f == nk - 1): E.matmul(o, lt, rh, start=s, stop=p),
                        dd)
        return t

    def _tp(self, out_ap, src_ap, fp32, dd):
        if fp32:
            return self.pe(lambda E, o=out_ap, s=src_ap:
                           E.transpose(o, s, self.identf), dd)
        return self.pe(lambda E, o=out_ap, s=src_ap:
                       E.matmul(o, s, self.identb, start=True, stop=True), dd)

    def ttile(self, src_ap, dst_ap, deps, fp32=False, evac="act"):
        r = self.tr_rr % 4
        self.tr_rr += 1
        ps = self.psT[r]
        dd = list(deps) + ([self.psT_last[r]] if self.psT_last[r] is not None
                           else [])
        n = src_ap.shape[1]
        tt = self._tp(ps[0:n, 0:128], src_ap, fp32, dd)
        if len(dst_ap.shape) == 3:
            sv = ps.rearrange("p (a s) -> p a s", s=128)[0:n, 0:1, :]
        else:
            sv = ps[0:n, 0:128]
        if evac == "act":
            te = self.act(lambda E, d=dst_ap, s=sv: E.copy(d, s), [tt])
        else:
            te = self.dve(lambda E, d=dst_ap, s=sv: E.tensor_copy(d, s), [tt])
        self.psT_last[r] = te
        return te

    def tgrp(self, srcs, evacs, deps, fp32=False, site=""):
        if SPLIT_TGRP and site not in BATCH_SITES:
            toks = []
            for eng, dst, j0, j1, npart in evacs:
                for k in range(j0, j1):
                    if len(dst.shape) == 3:
                        d = dst[:, k - j0:k - j0 + 1, :]
                    else:
                        d = dst[:, (k - j0) * 128:(k - j0 + 1) * 128]
                    toks.append(self.ttile(srcs[k], d, deps, fp32=fp32,
                                           evac=eng))
            return join(toks)
        return self._tgrp(srcs, evacs, deps, fp32)

    def _tgrp(self, srcs, evacs, deps, fp32=False):
        assert len(srcs) <= 4
        r = self.tr_rr % 4
        self.tr_rr += 1
        ps = self.psT[r]
        dd = list(deps) + ([self.psT_last[r]] if self.psT_last[r] is not None
                           else [])
        tt = None
        for j, s in enumerate(srcs):
            n = s.shape[1]
            tt = self._tp(ps[0:n, j * 128:(j + 1) * 128], s, fp32,
                          dd if j == 0 else ())
        toks = []
        prev = None
        for eng, dst, j0, j1, npart in evacs:
            if len(dst.shape) == 3:
                sv = ps.rearrange("p (a s) -> p a s", s=128)[0:npart, j0:j1, :]
            else:
                sv = ps[0:npart, j0 * 128:j1 * 128]
            dd2 = [tt] if prev is None else [tt, prev]
            if eng == "act":
                prev = self.act(lambda E, d=dst, s=sv: E.copy(d, s), dd2)
            else:
                prev = self.dve(lambda E, d=dst, s=sv: E.tensor_copy(d, s), dd2)
            toks.append(prev)
        jt = join(toks)
        self.psT_last[r] = jt
        return jt

    def tgrp_add(self, srcs, dst3, deps, fp32=True):
        assert len(srcs) <= 4
        if SPLIT_TGRP:
            last = None
            for j, s in enumerate(srcs):
                r = self.tr_rr % 4
                self.tr_rr += 1
                ps = self.psT[r]
                dd = list(deps) + ([self.psT_last[r]]
                                   if self.psT_last[r] is not None else [])
                tt = self._tp(ps[:, 0:128], s, fp32, dd)
                ps3 = ps.rearrange("p (a s) -> p a s", s=128)[:, 0:1, :]
                ta = self.dve(lambda E, d=dst3[:, j:j + 1, :], s3=ps3:
                              E.tensor_add(d, d, s3), [tt])
                self.psT_last[r] = ta
                last = ta
            return last
        r = self.tr_rr % 4
        self.tr_rr += 1
        ps = self.psT[r]
        dd = list(deps) + ([self.psT_last[r]] if self.psT_last[r] is not None
                           else [])
        tt = None
        for j, s in enumerate(srcs):
            tt = self._tp(ps[:, j * 128:(j + 1) * 128], s, fp32,
                          dd if j == 0 else ())
        ps3 = ps.rearrange("p (a s) -> p a s", s=128)[:, 0:len(srcs), :]
        ta = self.dve(lambda E, d=dst3, s=ps3: E.tensor_add(d, d, s), [tt])
        self.psT_last[r] = ta
        return ta

    def stream_cast(self, dst_ap, src2d, total_rows, c0, c1, deps):
        tl = total_rows // 128
        src = src2d.rearrange("(t p) n -> p t n", p=128)
        slot = self.qidx["qpool"] % NSLOT
        self.qidx["qpool"] += 1
        dom = f"qpool{slot}"
        dd = list(deps)
        if self.qlast[dom] is not None:
            dd.append(self.qlast[dom])
        t = self.prog.op("pool",
                         lambda E, d=dst_ap, s=src, tl=tl, c0=c0, c1=c1:
                         E.dma_start(out=d.rearrange("p (t n) -> p t n", t=tl),
                                     in_=s[:, :, c0:c1]),
                         dd, queue=dom)
        self.qlast[dom] = t
        return t


    def stream_f32(self, dst_ap, src2d, total_rows, c0, c1, deps, q="qsp"):
        tl = total_rows // 128
        src = src2d.rearrange("(t p) n -> p t n", p=128)
        return self.dma(dst_ap.rearrange("p (t n) -> p t n", t=tl),
                        src[:, :, c0:c1], deps, q=q)


def tile_ln(kb, src, dst, scr, eps_col, deps, sqdump=None):
    s1, s2, sd, bb = scr
    sq = sqdump if sqdump is not None else kb.sqdump
    t1 = kb.dve(lambda E: E.reduce_sum(s1, src, axis=AX.X, negate=True), deps)
    t2 = kb.act(lambda E: E.activation(out=sq, in_=src, func=AF.Square,
                                       accum_out=s2), deps)
    t3 = kb.act(lambda E: E.activation(out=s1, in_=s1, func=AF.Copy,
                                       scale=1.0 / D), [t1])
    t4 = kb.dve(lambda E: E.tensor_scalar(bb, s1, s1, -1.0, ALU.mult,
                                          ALU.mult), [t3])
    t5 = kb.dve(lambda E: E.tensor_scalar_add(bb, bb, eps_col), [t4])
    t6 = kb.act(lambda E: E.activation(out=sd, in_=s2, func=AF.Sqrt,
                                       bias=bb, scale=1.0 / D), [t2, t5])
    t7 = kb.dve(lambda E: E.reciprocal(sd, sd), [t6])
    t8 = kb.dve(lambda E: E.tensor_scalar(dst, src, s1, sd, ALU.add,
                                          ALU.mult), [t7, t3])
    return t8


def transpose16(kb, src_bf, dst3, deps, nf=16):
    toks = []
    for gi, g0 in enumerate(range(0, nf, 4)):
        gn = min(4, nf - g0)
        srcs = [src_bf[:, (g0 + j) * 128:(g0 + j + 1) * 128] for j in range(gn)]
        eng = "act" if gi % 2 == 0 else "dve"
        toks.append(kb.tgrp(srcs, [(eng, dst3[:, g0:g0 + gn, :], 0, gn, 128)],
                            deps, site="t16"))
    return toks, join(toks)


def fview(buf, nf, stride, c0, c1):
    return buf.rearrange("p (f s) -> p f s", s=stride)[:, 0:nf, c0:c1]



def emit_l0_segment(kb, xseg, maskseg, x1_dst):
    ar, prog = kb.ar, kb.prog

    xniT = ar.alloc("xniT", 16 * SEGT, BF16)
    xnT = ar.alloc("xnT", 16 * TOWN, BF16)
    mask_sb = ar.alloc("mask_sb", NTO * 16, F32)
    kb.ln0s = ar.alloc("ln0s", D, BF16)
    kb.ln0b = ar.alloc("ln0b", D, BF16)
    tln0 = [kb.dma(kb.ln0s, kb.inp["ln0s"][:, :], [], q="qpool"),
            kb.dma(kb.ln0b, kb.inp["ln0b"][:, :], [], q="qpool")]
    xt = [ar.alloc(f"xt{i}", D, F32) for i in range(2)]
    kb.sqdump = ar.alloc("sqs", D, F32)
    lnscr = [tuple(ar.alloc(f"lns{i}_{j}", 1, F32) for j in range(4))
             for i in range(2)]
    xnv = ar.alloc("xnA", D, F32)
    xn = [xnv, xnv]
    xciv = kb.sqdump
    xci = [xciv, xciv]
    xnib = [ar.alloc(f"xnibA{i}", D, BF16) for i in range(2)]
    xnb = [ar.alloc(f"xnbA{i}", D, BF16) for i in range(2)]
    zscr = ar.alloc("zscr", 16, F32)
    zwork = ar.alloc("zwork", 16, F32)
    cwork = ar.alloc("cwork", 16, F32)
    m1 = ar.alloc("m1", 1, F32)
    tile_done = {}
    prev_router = None
    prev_t7b = None
    prev_inner = None
    for i in range(NT):
        b = i % 2
        dd = [tile_done[i - 2]] if i - 2 in tile_done else []
        tl = kb.dma(xt[b], xseg[i * 128:(i + 1) * 128, :], dd, q="qsp")
        t7 = tile_ln(kb, xt[b], xn[b], lnscr[b], kb.eps5,
                     [tl, prev_t7b, prev_inner])
        if i >= 2:
            ti = i - 2
            tacc = kb.dve(lambda E, o=kb.acc[:, ti * D:(ti + 1) * D],
                          a=xt[b], c=xn[b]: E.tensor_add(o, a, c), [t7])
            t7b = kb.act(lambda E, o=xnb[b], a=xn[b]: E.activation(
                out=o, in_=a, func=AF.Copy), [t7])
            prev_t7b = t7b
            xnt_toks, _ = transpose16(
                kb, xnb[b], fview(xnT, 16, TOWN, ti * 128, (ti + 1) * 128),
                [t7b])
            bi, bk, bd = kb.bank([join(xnt_toks), kb.wrT_tok])
            tm = kb.mm16(bk[:, 0:16],
                         lambda f, ti=ti: xnT[:, f * TOWN + ti * 128:
                                              f * TOWN + (ti + 1) * 128],
                         lambda f: kb.wrTb[:, f * 16:(f + 1) * 16], bd)
            tz = kb.dve(lambda E, bk=bk: E.tensor_add(zscr, bk[:, 0:16], kb.brb),
                        [tm, prev_router])
            kb.bank_last[bi] = tz
            tz2 = kb.dve(lambda E: E.tensor_copy(zwork, zscr), [tz])
            tprev = tz2
            for it in range(3):
                ta = kb.dve(lambda E: E.reduce_max(m1, zwork, axis=AX.X), [tprev])
                tb = kb.dve(lambda E: E.tensor_single_scalar(cwork, zwork, m1,
                                                             ALU.is_ge), [ta])
                tc = kb.dve(lambda E: E.tensor_scalar_mul(cwork, cwork, -BIGC),
                            [tb])
                tprev = kb.dve(lambda E: E.tensor_add(zwork, zwork, cwork), [tc])
            ta = kb.dve(lambda E: E.reduce_max(m1, zwork, axis=AX.X), [tprev])
            prev_router = kb.dve(lambda E, ti=ti: E.tensor_single_scalar(
                mask_sb[:, ti * 16:(ti + 1) * 16], zscr, m1, ALU.is_ge), [ta])
        else:
            tacc = t7
        t8 = tile_ln(kb, xn[b], xci[b], lnscr[b], kb.eps5, [t7, tacc])
        t9 = kb.dve(lambda E, a=xci[b], s=kb.ln0s: E.tensor_mul(a, a, s),
                    [t8] + tln0)
        t10 = kb.dve(lambda E, o=xnib[b], a=xci[b], s=kb.ln0b:
                     E.tensor_add(o, a, s), [t9])
        prev_inner = t10
        _, tlast = transpose16(
            kb, xnib[b], fview(xniT, 16, SEGT, i * 128, (i + 1) * 128), [t10])
        tile_done[i] = join([tlast, prev_router]) if i >= 2 else tlast
    kb.dma(kb.xnT_scr, xnT, [tile_done[NT - 1], tile_done[NT - 2]], q="qsp")
    for n in ("xt0", "xt1", "xnA", "xnibA0", "xnibA1", "sqs",
              "xnbA0", "xnbA1", "zscr", "zwork", "cwork", "m1",
              "ln0s", "ln0b", "xnT",
              "lns0_0", "lns0_1", "lns0_2", "lns0_3",
              "lns1_0", "lns1_1", "lns1_2", "lns1_3"):
        ar.free(n)
    prog.barrier(exclude=QACT_DOMS)

    OTb = ar.alloc("OTb", 16 * TOWN, BF16)
    QTb = ar.alloc("QTb", TOWN, BF16)
    KTb = ar.alloc("KTb", G + SEGT, BF16)
    V4b = ar.alloc("V4b", NT * 512, BF16)
    Vg4 = ar.alloc("Vg4", 512, BF16)
    wv4 = ar.alloc("wv4", 16 * 512, BF16)
    wqh = [ar.alloc(f"wqh{i}", 16 * 128, BF16) for i in range(2)]
    wkh = [ar.alloc(f"wkh{i}", 16 * 128, BF16) for i in range(2)]
    smscr_ = [ar.alloc(f"smscr{i}", 512, F32) for i in range(2)]
    pscr_ = [ar.alloc(f"pscr{i}", 512, F32) for i in range(2)]
    Ab_ = [ar.alloc(f"Ab{i}", 512, BF16) for i in range(2)]
    ATb_ = [ar.alloc(f"ATb{i}", 512, BF16) for i in range(2)]
    mxs_ = [ar.alloc(f"mxs{i}", 1, F32) for i in range(2)]
    dens_ = [ar.alloc(f"dens{i}", 1, F32) for i in range(2)]
    maskf = ar.alloc("maskf", 3 * 400, BF16)
    tmask = kb.dma(maskf.rearrange("p (q n) -> p q n", q=3),
                   maskseg.rearrange("q p n -> p q n"), [], q="qpool")

    wv_tok = None
    wq_tok = [None, None]
    wk_tok = [None, None]
    head_done = {}
    for g in range(4):
        vdeps = [wv_tok] if wv_tok is not None else []
        if g >= 1:
            vdeps += [head_done[g * 4 - 1]]
        wv_tok = kb.stream_cast(wv4, kb.inp["wvT"], D, g * 512, (g + 1) * 512, vdeps)
        bi, bk, bd = kb.bank([wv_tok, kb.gT_tok])
        tv = kb.mm16(bk[0:16, :], lambda f: kb.gTb[:, f * 16:(f + 1) * 16],
                     lambda f: wv4[:, f * 512:(f + 1) * 512], bd)
        te = kb.act(lambda E, bk=bk: E.copy(Vg4[0:16, :], bk[0:16, :]), [tv])
        kb.bank_last[bi] = te
        vg_tok = te
        vt_tok = []
        for kt in range(NT):
            bi, bk, bd = kb.bank([wv_tok])
            tv = kb.mm16(bk, lambda f, kt=kt: xniT[:, f * SEGT + kt * 128:
                                                   f * SEGT + (kt + 1) * 128],
                         lambda f: wv4[:, f * 512:(f + 1) * 512], bd)
            te = kb.act(lambda E, bk=bk, kt=kt: E.copy(
                V4b[:, kt * 512:(kt + 1) * 512], bk), [tv])
            kb.bank_last[bi] = te
            vt_tok.append(te)
        for hh in range(4):
            h = g * 4 + hh
            if getattr(kb, "gamma", None):
                kb.gamma.step(6)
            wb = h % 2
            wdeps = [wq_tok[wb]] if wq_tok[wb] is not None else []
            if h >= 2:
                wdeps += [head_done[h - 2]]
            wq_tok[wb] = kb.stream_cast(wqh[wb], kb.inp["wqT"], D, h * 128,
                                        (h + 1) * 128, wdeps)
            wk_tok[wb] = kb.stream_cast(wkh[wb], kb.inp["wkT"], D, h * 128,
                                        (h + 1) * 128, wdeps)
            prev_h = [head_done[h - 1]] if h >= 1 else []
            qt_toks = []
            for c in range(2):
                bi, bk, bd = kb.bank([wq_tok[wb]] + prev_h)
                tm = kb.mm16(bk, lambda f, wb=wb: wqh[wb][:, f * 128:(f + 1) * 128],
                             lambda f, c=c: xniT[:, f * SEGT + HALO + c * 512:
                                                 f * SEGT + HALO + (c + 1) * 512], bd)
                te = kb.act(lambda E, bk=bk, c=c: E.activation(
                    out=QTb[:, c * 512:(c + 1) * 512], in_=bk, func=AF.Copy,
                    scale=INVSQ), [tm])
                kb.bank_last[bi] = te
                qt_toks.append(te)
            bi, bk, bd = kb.bank([wk_tok[wb], kb.gT_tok] + prev_h)
            tm = kb.mm16(bk[:, 0:16], lambda f, wb=wb: wkh[wb][:, f * 128:(f + 1) * 128],
                         lambda f: kb.gTb[:, f * 16:(f + 1) * 16], bd)
            te = kb.act(lambda E, bk=bk: E.copy(KTb[:, 0:16], bk[:, 0:16]), [tm])
            kb.bank_last[bi] = te
            kt_toks = [te]
            for c, cw in ((0, 512), (1, 512), (2, 256)):
                bi, bk, bd = kb.bank([wk_tok[wb]] + prev_h)
                tm = kb.mm16(bk[:, 0:cw], lambda f, wb=wb: wkh[wb][:, f * 128:(f + 1) * 128],
                             lambda f, c=c, cw=cw: xniT[:, f * SEGT + c * 512:
                                                        f * SEGT + c * 512 + cw], bd)
                te = kb.act(lambda E, bk=bk, c=c, cw=cw: E.copy(
                    KTb[:, G + c * 512:G + c * 512 + cw], bk[:, 0:cw]), [tm])
                kb.bank_last[bi] = te
                kt_toks.append(te)
            last_ot = None
            par_ot = [None, None]
            smm = {}

            def emit_score_mm(qt):
                bi, bk, bd = kb.bank(qt_toks + kt_toks + [tmask])
                t1 = kb.pe(lambda E, bk=bk, qt=qt: E.matmul(
                    bk[:, 0:384], QTb[:, qt * 128:(qt + 1) * 128],
                    KTb[:, G + qt * 128:G + qt * 128 + 384],
                    start=True, stop=True), bd)
                t2 = kb.pe(lambda E, bk=bk, qt=qt: E.matmul(
                    bk[:, 384:400], QTb[:, qt * 128:(qt + 1) * 128],
                    KTb[:, 0:16], start=True, stop=True), ())
                smm[qt] = (bi, bk, t1, t2)

            emit_score_mm(0)
            for qt in range(NTO):
                if qt + 1 < NTO:
                    emit_score_mm(qt + 1)
                pb = qt % 2
                smscr, pscr, mxs, dens = (smscr_[pb], pscr_[pb], mxs_[pb],
                                          dens_[pb])
                Ab, ATb = Ab_[pb], ATb_[pb]
                bi, bk, t1, t2 = smm.pop(qt)
                mi = min(qt, 2)
                t3 = kb.dve(lambda E, bk=bk, mi=mi, sm=smscr: E.tensor_add(
                    sm[:, 0:400], bk[:, 0:400],
                    maskf[:, mi * 400:(mi + 1) * 400]), [t1, t2])
                kb.bank_last[bi] = t3
                t5 = kb.dve(lambda E, sm=smscr, mx=mxs: E.reduce_max(
                    mx, sm[:, 0:400], axis=AX.X, negate=True), [t3])
                t7 = kb.act(lambda E, sm=smscr, mx=mxs, p=pscr, dn=dens:
                            E.activation(out=p[:, 0:400], in_=sm[:, 0:400],
                                         func=AF.Exp, bias=mx, accum_out=dn),
                            [t5, par_ot[pb]])
                t8 = kb.dve(lambda E, dn=dens: E.reciprocal(dn, dn), [t7])
                t9 = kb.dve(lambda E, A=Ab, p=pscr, dn=dens:
                            E.tensor_scalar_mul(A[:, 0:400], p[:, 0:400], dn),
                            [t8])
                soft = {qt: t9}
                treg = [kb.tgrp(
                    [Ab[:, j * 128:(j + 1) * 128] for j in range(3)]
                    + [Ab[:, 384:400]],
                    [("act", ATb[:, 0:384], 0, 3, 128),
                     ("act", ATb[0:16, 384:512], 3, 4, 16)], [soft[qt]],
                    site="atb")]
                bi2, bk2, bd2 = kb.bank(treg + vt_tok + [vg_tok])
                tm = None
                for j in range(3):
                    tm = kb.pe(lambda E, bk2=bk2,
                               lt=V4b[:, (qt + j) * 512 + hh * 128:
                                      (qt + j) * 512 + (hh + 1) * 128],
                               rh=ATb[:, j * 128:(j + 1) * 128],
                               s=(j == 0): E.matmul(bk2[:, 0:128], lt, rh,
                                                    start=s, stop=False),
                               bd2 if j == 0 else ())
                tm = kb.pe(lambda E, bk2=bk2,
                           lt=Vg4[0:16, hh * 128:(hh + 1) * 128],
                           rh=ATb[0:16, 384:512]: E.matmul(
                               bk2[:, 0:128], lt, rh, start=False, stop=True),
                           ())
                te = kb.act(lambda E, bk2=bk2, h=h, qt=qt: E.copy(
                    OTb[:, h * TOWN + qt * 128:h * TOWN + (qt + 1) * 128],
                    bk2[:, 0:128]), [tm])
                kb.bank_last[bi2] = te
                last_ot = te
                par_ot[pb] = te
            head_done[h] = last_ot
    ar.free("QTb", "KTb", "V4b", "Vg4", "wv4", "wqh0", "wqh1", "wkh0", "wkh1",
            "smscr0", "smscr1", "pscr0", "pscr1", "Ab0", "Ab1", "ATb0", "ATb1",
            "mxs0", "mxs1", "dens0", "dens1", "maskf", "xniT")
    prog.barrier(exclude=QACT_DOMS)

    woc = [ar.alloc(f"woc{i}", 16 * 512, BF16) for i in range(2)]
    wo_tok = [None, None]
    wo_mmd = {}

    def wo_stream(c2):
        if c2 < 4:
            b2 = c2 % 2
            dd = [wo_tok[b2], wo_mmd.get(c2 - 2)]
            wo_tok[b2] = kb.stream_cast(woc[b2], kb.inp["woT"], D, c2 * 512,
                                        (c2 + 1) * 512, dd)

    wo_stream(0)
    wo_stream(1)
    for ch in range(4):
        b = ch % 2
        if getattr(kb, "gamma", None):
            kb.gamma.step(4)
        for qt in range(NTO):
            bi, bk, bd = kb.bank([wo_tok[b]])
            tm = kb.mm16(bk, lambda f, qt=qt: OTb[:, f * TOWN + qt * 128:
                                                  f * TOWN + (qt + 1) * 128],
                         lambda f, b=b: woc[b][:, f * 512:(f + 1) * 512], bd)
            if qt == NTO - 1:
                wo_mmd[ch] = tm
                wo_stream(ch + 2)
            ta = kb.dve(lambda E, bk=bk, qt=qt, ch=ch: E.tensor_add(
                kb.acc[:, qt * D + ch * 512:qt * D + (ch + 1) * 512],
                kb.acc[:, qt * D + ch * 512:qt * D + (ch + 1) * 512], bk), [tm])
            kb.bank_last[bi] = ta
    ar.free("OTb", "woc0", "woc1")
    prog.barrier(exclude=QACT_DOMS)

    hTb = ar.alloc("hTb", 64 * 512, BF16)
    xnT = ar.alloc("xnT", 16 * TOWN, BF16)
    kb.dma(xnT, kb.xnT_scr, [], q="qsp")
    prog.barrier(exclude=QACT_DOMS)
    gel = ar.alloc("gel", 512, BF16)
    gmb = ar.alloc("gmb", 512, BF16)
    yts_ = [ar.alloc(f"ytsE{i}", 512, BF16) for i in range(2)]
    prev_tmk = prev_tlast = None
    for half in range(2):
        w1c = [ar.alloc(f"w1c{i}", 16 * 512, BF16) for i in range(2)]
        w1_tok = [None, None]
        w1_mmd = {}

        def w1_stream(i2):
            if i2 < 16:
                b2 = i2 % 2
                dd = [w1_tok[b2], w1_mmd.get(i2 - 2)]
                w1_tok[b2] = kb.stream_cast(w1c[b2], kb.inp["w1T0"], D,
                                            i2 * 512, (i2 + 1) * 512, dd)

        w1_stream(0)
        w1_stream(1)
        seq = [(ibc, t4) for ibc in range(16) for t4 in range(4)]
        mmq = {}

        def w1_mm(k):
            ibc, t4 = seq[k]
            b = ibc % 2
            ti = half * 4 + t4
            bi, bk, bd = kb.bank([w1_tok[b]])
            tm = kb.mm16(bk, lambda f, ti=ti: xnT[:, f * TOWN + ti * 128:
                                                  f * TOWN + (ti + 1) * 128],
                         lambda f, b=b: w1c[b][:, f * 512:(f + 1) * 512], bd)
            if t4 == 3:
                w1_mmd[ibc] = tm
                w1_stream(ibc + 2)
            mmq[k] = (bi, bk, tm)

        w1_mm(0)
        for k, (ibc, t4) in enumerate(seq):
            if k + 1 < len(seq):
                w1_mm(k + 1)
            ti = half * 4 + t4
            bi, bk, tm = mmq.pop(k)
            if t4 == 0 and getattr(kb, "gamma", None):
                kb.gamma.step(3)
            tg = kb.act(lambda E, bk=bk: E.activation(out=gel, in_=bk,
                                                      func=AF.Gelu),
                        [tm, prev_tmk])
            kb.bank_last[bi] = tg
            tmk = kb.dve(lambda E, ti=ti, ibc=ibc: E.tensor_scalar_mul(
                gmb, gel, mask_sb[:, ti * 16 + ibc:ti * 16 + ibc + 1]),
                [tg, prev_tlast])
            _, tlast = transpose16(
                kb, gmb,
                hTb.rearrange("p (f s) -> p f s", s=512)
                [:, ibc * 4:ibc * 4 + 4, t4 * 128:(t4 + 1) * 128],
                [tmk], nf=4)
            prev_tmk, prev_tlast = tmk, tlast
        prog.barrier(exclude=QACT_DOMS)
        ar.free("w1c0", "w1c1")
        w2c = [ar.alloc(f"w2c{i}", 64 * 128, BF16) for i in range(2)]
        w2_tok = [None, None]
        w2_mmd = {}

        def w2_stream(f2):
            if f2 < 16:
                b2 = f2 % 2
                dd = [w2_tok[b2], w2_mmd.get(f2 - 2)]
                w2_tok[b2] = kb.stream_cast(w2c[b2], kb.inp["w2T0"], FF0,
                                            f2 * 128, (f2 + 1) * 128, dd)

        w2_stream(0)
        w2_stream(1)
        w2q_pend = {}

        def w2_mm(f):
            b = f % 2
            bi, bk, bd = kb.bank([w2_tok[b]])
            tm = None
            for it in range(64):
                tm = kb.pe(lambda E, bk=bk, lt=w2c[b][:, it * 128:(it + 1) * 128],
                           rh=hTb[:, it * 512:(it + 1) * 512],
                           s=(it == 0), p=(it == 63):
                           E.matmul(bk, lt, rh, start=s, stop=p),
                           bd if it == 0 else ())
            w2_mmd[f] = tm
            w2_stream(f + 2)
            w2q_pend[f] = (bi, bk, tm)

        w2_mm(0)
        for f in range(16):
            if f + 1 < 16:
                w2_mm(f + 1)
            if getattr(kb, "gamma", None):
                kb.gamma.step(3)
            bi, bk, tm = w2q_pend.pop(f)
            yts = yts_[f % 2]
            tev = kb.act(lambda E, bk=bk, o=yts: E.copy(o, bk), [tm])
            kb.bank_last[bi] = tev
            kb.tgrp_add(
                [yts[:, j * 128:(j + 1) * 128] for j in range(4)],
                fview(kb.acc, 8, D, f * 128, (f + 1) * 128)
                [:, half * 4:half * 4 + 4, :], [tev], fp32=False)
        prog.barrier(exclude=QACT_DOMS)
        ar.free("w2c0", "w2c1")
    ar.free("hTb", "gel", "gmb", "ytsE0", "ytsE1", "xnT", "mask_sb")
    prog.barrier(exclude=QACT_DOMS)

    dump_tok = None
    if x1_dst is not None:
        for ti in range(NTO):
            dump_tok = kb.dma(x1_dst[ti * 128:(ti + 1) * 128, :],
                              kb.acc[:, ti * D:(ti + 1) * D], [], q="qsp")
        prog.barrier(exclude=QACT_DOMS)
    return dump_tok


BIGC = 1.0e9


def declare_l0(kb):
    for name, shape in (("wqT", (D, D)), ("wkT", (D, D)), ("wvT", (D, D)),
                        ("woT", (D, D)), ("wrT", (D, G)), ("w1T0", (D, FF0)),
                        ("w2T0", (FF0, D)), ("gT", (D, G)),
                        ("ln0s", (128, D)), ("ln0b", (128, D)),
                        ("brb", (128, G))):
        kb.din(name, shape, F32)


def load_l0_consts(kb):
    ar = kb.ar
    kb.brb = ar.alloc("brb", G, F32)
    kb.gTb = ar.alloc("gTb", 16 * G, BF16)
    kb.wrTb = ar.alloc("wrTb", 16 * G, BF16)
    kb.dma(kb.brb, kb.inp["brb"][:, :], [], q="qsp")
    kb.gT_tok = kb.stream_cast(kb.gTb, kb.inp["gT"], D, 0, G, [])
    kb.wrT_tok = kb.stream_cast(kb.wrTb, kb.inp["wrT"], D, 0, G, [])
    kb.prog.barrier(exclude=QACT_DOMS)


def build_l0():
    kb = KB()
    setup_min(kb)
    declare_l0(kb)
    load_l0_consts(kb)
    xO = kb.din("xO", (SEGT, D))
    maskO = kb.din("maskO", (3, 128, 400))
    x1 = kb.dout("x1", (TOWN, D))
    emit_l0_segment(kb, xO, maskO, x1)
    finish(kb)
    return kb


def finish(kb):
    nc = kb.nc
    from contextlib import ExitStack
    es = ExitStack()
    sems = {d: es.enter_context(nc.semaphore(f"sem_{d}")) for d in DOMS}
    bodies = kb.prog.emitters(sems)
    final_waits = {d: kb.prog.cnt[d] for d in DOMS if kb.prog.cnt[d] > 0}

    def sp_body(e):
        bodies["sp"](e)
        for dname in sorted(final_waits):
            e.wait_ge(sems[dname], final_waits[dname])

    with es, nc.Block() as block:
        block.sync(sp_body)
        block.tensor(bodies["pe"])
        block.scalar(bodies["act"])
        block.vector(bodies["dve"])
        block.gpsimd(bodies["pool"])
    return kb




def declare_l1(kb):
    for name, shape in (("wqkv", (D, 3 * D)), ("wgate", (D, D)), ("wout", (D, D)),
                        ("rmsb", (128, D)), ("slopeb", (128, G)),
                        ("fw1T", (D, FF1)), ("fw2T", (D, FF1)), ("fw3T", (FF1, D)),
                        ("iotaQ", (128, 2)), ("iotaK", (128, 2)),
                        ("diffT", (128, 512)), ("onesf", (1, 128)),
                        ("selc", (128, 1))):
        kb.din(name, shape, F32)


def load_l1_consts(kb):
    ar = kb.ar
    kb.rmsb = ar.alloc("rmsb", D, F32)
    kb.slopeb = ar.alloc("slopeb", G, F32)
    kb.iotaQ = ar.alloc("iotaQ", 2, F32)
    kb.iotaK = ar.alloc("iotaK", 2, F32)
    kb.diffT = ar.alloc("diffT", 512, F32)
    kb.onesf = ar.alloc("onesf", 128, F32)
    kb.selc = ar.alloc("selc", 1, F32)
    for n in ("rmsb", "slopeb", "iotaQ", "iotaK", "diffT", "selc"):
        kb.dma(getattr(kb, n), kb.inp[n][:, :], [], q="qsp")
    kb.dma(kb.onesf[0:1, :], kb.inp["onesf"][:, :], [], q="qsp")
    kb.prog.barrier(exclude=QACT_DOMS)


class GammaStream:

    def __init__(self, kb):
        self.kb = kb
        ar = kb.ar
        kb.gacc = ar.alloc("gacc", 4, F32)
        self.pcolg = ar.alloc("pcolg", 1, F32)
        self.wst = [ar.alloc(f"wst{i}", 704, F32) for i in range(2)]
        self.prev_add = kb.dve(lambda E: E.memset(kb.gacc, 0.0), [])
        self.wtok = [None] * 2
        self.pieces = []
        for t, nm in (((0, "fw1T")), ((1, "fw2T")), ((2, "fw3T"))):
            rows, cols = ((D, FF1), (D, FF1), (FF1, D))[t]
            piece = 704 if cols % 704 == 0 else 512
            for rt in range(rows // 128):
                for c0 in range(0, cols, piece):
                    self.pieces.append((t, nm, rt, c0, piece))
        self.i = 0
        kb.gamma_bufs = ("wst0", "wst1", "pcolg")

    def step(self, k):
        kb = self.kb
        for _ in range(k):
            if self.i >= len(self.pieces):
                return
            t, nm, rt, c0, piece = self.pieces[self.i]
            b = self.i % 2
            src = kb.inp[nm].rearrange("(t p) n -> p t n", p=128)
            tl = kb.dma(self.wst[b][:, 0:piece], src[:, rt, c0:c0 + piece],
                        [self.wtok[b], self.prev_add], q="qact")
            self.wtok[b] = tl
            tr = kb.dve(lambda E, a=self.wst[b][:, 0:piece]: E.reduce_sum(
                self.pcolg, a, axis=AX.X, apply_absolute_value=True),
                [tl, self.prev_add])
            self.prev_add = kb.dve(
                lambda E, o=kb.gacc[:, t:t + 1]: E.tensor_add(o, o, self.pcolg),
                [tr])
            self.i += 1

    def finishup(self):
        self.step(len(self.pieces))
        self.kb.gamma_tok = self.prev_add


def emit_l1_consts(kb):
    ar = kb.ar
    kb.qdc = ar.alloc("qdc", 32, F32)
    kb.kdc = ar.alloc("kdc", 32, F32)
    kb.bdc = ar.alloc("bdc", G, F32)
    kb.Sst = ar.alloc("Sst", 16 * 128, F32)
    scr = ar.alloc("cscr", 2, F32)
    tz = kb.dve(lambda E: E.memset(kb.Sst, 0.0), [])
    prev = None
    for h in range(16):
        sc = kb.slopeb[:, h:h + 1]
        t1 = kb.dve(lambda E, sc=sc: E.tensor_scalar_mul(scr, kb.iotaQ, sc),
                    [prev] if prev else [])
        t2 = kb.act(lambda E, o=kb.qdc[:, h * 2:(h + 1) * 2]: E.activation(
            out=o, in_=scr, func=AF.Exp, scale=-1.0), [t1])
        t3 = kb.dve(lambda E, sc=sc: E.tensor_scalar_mul(scr, kb.iotaK, sc), [t2])
        t4 = kb.act(lambda E, o=kb.kdc[:, h * 2:(h + 1) * 2]: E.activation(
            out=o, in_=scr, func=AF.Exp, scale=-1.0), [t3])
        t5 = kb.act(lambda E, o=kb.bdc[:, h:h + 1], sc=sc: E.activation(
            out=o, in_=sc, func=AF.Exp, scale=-256.0), [t4])
        prev = t5
    ar.free("cscr")
    kb.prog.barrier(exclude=QACT_DOMS)
    return tz


def _rms_norm(kb, o_tm, dst_bf, sq, vv, deps):
    t1 = kb.act(lambda E: E.activation(out=sq, in_=o_tm, func=AF.Square,
                                       accum_out=vv), deps)
    t2 = kb.act(lambda E: E.activation(out=vv, in_=vv, func=AF.Sqrt,
                                       bias=kb.eps6, scale=1.0 / D), [t1])
    t3 = kb.dve(lambda E: E.reciprocal(vv, vv), [t2])
    t5 = kb.dve(lambda E: E.scalar_tensor_tensor(dst_bf, o_tm, vv, kb.rmsb,
                                                 ALU.mult, ALU.mult), [t3])
    return t5


def emit_prefix_chain(kb, x1p_ap):
    ar, prog = kb.ar, kb.prog
    xnPT = ar.alloc("xnPT", 16 * TOWN, BF16)
    xt = [ar.alloc(f"xtP{i}", D, F32) for i in range(2)]
    kb.sqdump = ar.alloc("sqsP", D, F32)
    lnscr = [tuple(ar.alloc(f"lnsP{i}_{j}", 1, F32) for j in range(4))
             for i in range(2)]
    xnv = ar.alloc("xnP", D, F32)
    xnb = [ar.alloc(f"xnbP{i}", D, BF16) for i in range(2)]
    tile_done = {}
    for i in range(NTO):
        b = i % 2
        dd = [tile_done.get(i - 2)]
        tl = kb.dma(xt[b], x1p_ap[i * 128:(i + 1) * 128, :], dd, q="qsp")
        t7 = tile_ln(kb, xt[b], xnv, lnscr[b], kb.eps5, [tl])
        t8 = kb.act(lambda E, o=xnb[b]: E.activation(out=o, in_=xnv,
                                                     func=AF.Copy), [t7])
        _, tlast = transpose16(
            kb, xnb[b], fview(xnPT, 16, TOWN, i * 128, (i + 1) * 128), [t8])
        tile_done[i] = tlast
    for n in ("xtP0", "xtP1", "sqsP", "xnP", "xnbP0", "xnbP1",
              "lnsP0_0", "lnsP0_1", "lnsP0_2", "lnsP0_3",
              "lnsP1_0", "lnsP1_1", "lnsP1_2", "lnsP1_3"):
        ar.free(n)
    prog.barrier(exclude=QACT_DOMS)

    wbuf = ar.alloc("wbufP", 16 * 512, BF16)
    k4 = ar.alloc("k4P", 8 * 512, BF16)
    v4 = ar.alloc("v4P", 8 * 512, BF16)
    kscr = ar.alloc("kscrP", 256, BF16)
    wtok = None
    grp_done = None
    for g in range(4):
        wtok = kb.stream_cast(wbuf, kb.inp["wqkv"], D, D + g * 512,
                              D + (g + 1) * 512, [wtok, grp_done])
        ktoks = []
        for ti in range(NTO):
            bi, bk, bd = kb.bank([wtok])
            tm = kb.mm16(bk, lambda f, ti=ti: xnPT[:, f * TOWN + ti * 128:
                                                   f * TOWN + (ti + 1) * 128],
                         lambda f: wbuf[:, f * 512:(f + 1) * 512], bd)
            te = kb.act(lambda E, bk=bk, o=k4[:, ti * 512:(ti + 1) * 512]:
                        E.activation(out=o, in_=bk, func=AF.Silu), [tm])
            kb.bank_last[bi] = te
            ktoks.append(te)
        wtok = kb.stream_cast(wbuf, kb.inp["wqkv"], D, 2 * D + g * 512,
                              2 * D + (g + 1) * 512, [wtok, join(ktoks)])
        vtoks = []
        for ti in range(NTO):
            bi, bk, bd = kb.bank([wtok])
            tm = kb.mm16(bk, lambda f, ti=ti: xnPT[:, f * TOWN + ti * 128:
                                                   f * TOWN + (ti + 1) * 128],
                         lambda f: wbuf[:, f * 512:(f + 1) * 512], bd)
            te = kb.act(lambda E, bk=bk, o=v4[:, ti * 512:(ti + 1) * 512]:
                        E.activation(out=o, in_=bk, func=AF.Silu), [tm])
            kb.bank_last[bi] = te
            vtoks.append(te)
        gtok = join(ktoks + vtoks)
        hlast = None
        for hh in range(4):
            h = g * 4 + hh
            for b4 in range(4):
                kst = []
                for t in range(2):
                    kst.append(kb.dve(
                        lambda E, o=kscr[:, t * 128:(t + 1) * 128],
                        a=k4[:, (b4 * 2 + t) * 512 + hh * 128:
                             (b4 * 2 + t) * 512 + (hh + 1) * 128],
                        s=kb.kdc[:, h * 2 + t:h * 2 + t + 1]:
                        E.tensor_scalar_mul(o, a, s), [gtok, hlast]))
                bi, bk, bd = kb.bank(kst)
                tm = None
                for t in range(2):
                    tm = kb.pe(lambda E, bk=bk,
                               lt=kscr[:, t * 128:(t + 1) * 128],
                               rh=v4[:, (b4 * 2 + t) * 512 + hh * 128:
                                     (b4 * 2 + t) * 512 + (hh + 1) * 128],
                               s=(t == 0), p=(t == 1):
                               E.matmul(bk[:, 0:128], lt, rh, start=s, stop=p),
                               bd if t == 0 else ())
                ts1 = kb.dve(lambda E, o=kb.Sst[:, h * 128:(h + 1) * 128],
                             s=kb.bdc[:, h:h + 1]:
                             E.tensor_scalar_mul(o, o, s), [tm, hlast])
                ts2 = kb.dve(lambda E, o=kb.Sst[:, h * 128:(h + 1) * 128], bk=bk:
                             E.tensor_add(o, o, bk[:, 0:128]), [ts1])
                kb.bank_last[bi] = ts2
                hlast = ts2
        grp_done = hlast
    tsel = kb.dve(lambda E: E.tensor_scalar_mul(kb.Sst, kb.Sst, kb.selc),
                  [grp_done])
    ar.free("xnPT", "wbufP", "k4P", "v4P", "kscrP")
    prog.barrier(exclude=QACT_DOMS)
    return tsel


def emit_l1_own(kb, dbg=None):
    ar, prog = kb.ar, kb.prog
    xn1T = ar.alloc("xn1T", 16 * TOWN, BF16)
    kb.sqdump = ar.alloc("sqsG", D, F32)
    lnscr = [tuple(ar.alloc(f"lnsG{i}_{j}", 1, F32) for j in range(4))
             for i in range(2)]
    xnv = ar.alloc("xnG", D, F32)
    xnb = [ar.alloc(f"xnbG{i}", D, BF16) for i in range(2)]
    tile_done = {}
    for i in range(NTO):
        b = i % 2
        dd = [tile_done.get(i - 2)]
        t7 = tile_ln(kb, kb.acc[:, i * D:(i + 1) * D], xnv, lnscr[b], kb.eps5, dd)
        t8 = kb.act(lambda E, o=xnb[b]: E.activation(out=o, in_=xnv,
                                                     func=AF.Copy), [t7])
        _, tlast = transpose16(
            kb, xnb[b], fview(xn1T, 16, TOWN, i * 128, (i + 1) * 128), [t8])
        tile_done[i] = tlast
    for n in ("sqsG", "xnG", "xnbG0", "xnbG1",
              "lnsG0_0", "lnsG0_1", "lnsG0_2", "lnsG0_3",
              "lnsG1_0", "lnsG1_1", "lnsG1_2", "lnsG1_3"):
        ar.free(n)
    prog.barrier(exclude=QACT_DOMS)
    kb.xn1T = xn1T

    OT = ar.alloc("OTl1", 16 * TOWN, BF16)
    wbuf = ar.alloc("wbufG", 16 * 512, BF16)
    q4 = ar.alloc("q4G", 8 * 512, BF16)
    k4 = ar.alloc("k4G", 8 * 512, BF16)
    v4 = ar.alloc("v4G", 8 * 512, BF16)
    kscr_ = [ar.alloc(f"kscrG{i}", 256, BF16) for i in range(2)]
    qscr_ = [ar.alloc(f"qscrG{i}", 256, BF16) for i in range(2)]
    qTb_ = [ar.alloc(f"qTbG{i}", 256, BF16) for i in range(2)]
    qsTb_ = [ar.alloc(f"qsTbG{i}", 256, BF16) for i in range(2)]
    kTb_ = [ar.alloc(f"kTbG{i}", 256, BF16) for i in range(2)]
    ATs_ = [ar.alloc(f"ATsG{i}", 512, BF16) for i in range(2)]
    Sb_ = [ar.alloc(f"SbG{i}", 128, BF16) for i in range(2)]
    dT4 = ar.alloc("dT4G", 512, F32)
    dscr = ar.alloc("dscrG", 512, F32)
    wtok = None
    grp_done = None
    trsq, stm, omm, tcm = {}, {}, {}, {}
    td2_prev = [None]
    at_prev = [None]
    for g in range(4):
        projs = []
        for which, dst, cbase in (("q", q4, g * 512), ("k", k4, D + g * 512),
                                  ("v", v4, 2 * D + g * 512)):
            wtok = kb.stream_cast(wbuf, kb.inp["wqkv"], D, cbase, cbase + 512,
                                  [wtok, grp_done] + projs[-1:])
            toks = []
            for ti in range(NTO):
                bi, bk, bd = kb.bank([wtok])
                tm = kb.mm16(bk, lambda f, ti=ti: xn1T[:, f * TOWN + ti * 128:
                                                       f * TOWN + (ti + 1) * 128],
                             lambda f: wbuf[:, f * 512:(f + 1) * 512], bd)
                te = kb.act(lambda E, bk=bk, o=dst[:, ti * 512:(ti + 1) * 512]:
                            E.activation(out=o, in_=bk, func=AF.Silu), [tm])
                kb.bank_last[bi] = te
                toks.append(te)
            projs.append(join(toks))
        gtok = join(projs)
        hlast = None
        for hh in range(4):
            h = g * 4 + hh
            s0 = (g * 4 + hh) * 4
            td1 = kb.dve(lambda E, s=kb.slopeb[:, h:h + 1]:
                         E.tensor_scalar_mul(dscr, kb.diffT, s),
                         [gtok, td2_prev[0]])
            td2 = kb.act(lambda E: E.activation(out=dT4, in_=dscr, func=AF.Exp,
                                                scale=-1.0),
                         [td1, at_prev[0]])
            td2_prev[0] = td2
            state_prev = None
            for b4 in range(4):
                s = s0 + b4
                pb = b4 % 2
                kscr, qscr, qTb, qsTb, kTb, ATs, Sb = (
                    kscr_[pb], qscr_[pb], qTb_[pb], qsTb_[pb], kTb_[pb],
                    ATs_[pb], Sb_[pb])
                base = lambda t, col: (b4 * 2 + t) * 512 + col * 128
                qst = []
                for t in range(2):
                    qst.append(kb.dve(
                        lambda E, o=qscr[:, t * 128:(t + 1) * 128],
                        a=q4[:, base(t, hh):base(t, hh) + 128],
                        s=kb.qdc[:, h * 2 + t:h * 2 + t + 1]:
                        E.tensor_scalar_mul(o, a, s),
                        [gtok, trsq.get(s - 2)]))
                trs = [kb.tgrp(
                    [q4[:, base(t, hh):base(t, hh) + 128] for t in range(2)]
                    + [qscr[:, t * 128:(t + 1) * 128] for t in range(2)],
                    [("act", qTb[:, 0:256], 0, 2, 128),
                     ("act", qsTb[:, 0:256], 2, 4, 128)],
                    [gtok, qst[0], qst[1], stm.get(s - 2), omm.get(s - 2)],
                    site="g1"),
                    kb.tgrp(
                    [k4[:, base(t, hh):base(t, hh) + 128] for t in range(2)],
                    [("dve", kTb[:, 0:256], 0, 2, 128)],
                    [gtok, stm.get(s - 2)], site="g1")]
                trsq[s] = trs[0]
                tq = join(trs)
                bi, bk, bd = kb.bank([tq])
                st_toks = []
                for t in range(2):
                    st_toks.append(kb.pe(
                        lambda E, bk=bk, lt=kTb[:, t * 128:(t + 1) * 128],
                        rh=qTb, o=(0 if t == 0 else 256):
                        E.matmul(bk[:, o:o + 256], lt, rh, start=True,
                                 stop=True), bd if t == 0 else ()))
                stm[s] = st_toks[1]
                at = []
                for t in range(2):
                    at.append(kb.dve(
                        lambda E, o=ATs[:, t * 256:(t + 1) * 256],
                        a=bk[:, t * 256:(t + 1) * 256],
                        dd=dT4[:, t * 256:(t + 1) * 256]:
                        E.tensor_mul(o, a, dd),
                        [st_toks[1], td2, omm.get(s - 2)]))
                kb.bank_last[bi] = join(at)
                at_prev[0] = at[1]
                tSb = kb.act(lambda E, a=kb.Sst[:, h * 128:(h + 1) * 128],
                             o=Sb: E.copy(o, a),
                             [state_prev, omm.get(s - 2)])
                bi2, bk2, bd2 = kb.bank(at + [tSb])
                tm = None
                for t in range(2):
                    tm = kb.pe(lambda E, bk2=bk2,
                               lt=v4[:, base(t, hh):base(t, hh) + 128],
                               rh=ATs[:, t * 256:(t + 1) * 256],
                               s=(t == 0):
                               E.matmul(bk2[:, 0:256], lt, rh, start=s,
                                        stop=False), bd2 if t == 0 else ())
                tm = kb.pe(lambda E, bk2=bk2, lt=Sb, rh=qsTb: E.matmul(
                    bk2[:, 0:256], lt, rh, start=False, stop=True), ())
                omm[s] = tm
                tev = kb.act(lambda E, bk2=bk2,
                             o=OT[:, h * TOWN + b4 * 256:h * TOWN + (b4 + 1) * 256]:
                             E.copy(o, bk2[:, 0:256]), [tm])
                kb.bank_last[bi2] = tev
                kst = []
                for t in range(2):
                    kst.append(kb.dve(
                        lambda E, o=kscr[:, t * 128:(t + 1) * 128],
                        a=k4[:, base(t, hh):base(t, hh) + 128],
                        s=kb.kdc[:, h * 2 + t:h * 2 + t + 1]:
                        E.tensor_scalar_mul(o, a, s),
                        [gtok, tcm.get(s - 2)]))
                bi3, bk3, bd3 = kb.bank(kst + [tSb])
                tc = None
                for t in range(2):
                    tc = kb.pe(lambda E, bk3=bk3,
                               lt=kscr[:, t * 128:(t + 1) * 128],
                               rh=v4[:, base(t, hh):base(t, hh) + 128],
                               s=(t == 0), p=(t == 1):
                               E.matmul(bk3[:, 0:128], lt, rh, start=s, stop=p),
                               bd3 if t == 0 else ())
                tcm[s] = tc
                ts1 = kb.dve(lambda E, o=kb.Sst[:, h * 128:(h + 1) * 128],
                             s=kb.bdc[:, h:h + 1]:
                             E.tensor_scalar_mul(o, o, s), [tc, tSb])
                ts2 = kb.dve(lambda E, o=kb.Sst[:, h * 128:(h + 1) * 128], bk3=bk3:
                             E.tensor_add(o, o, bk3[:, 0:128]), [ts1])
                kb.bank_last[bi3] = ts2
                state_prev = ts2
                hlast = join([ts2, tev])
        grp_done = hlast
    ar.free("wbufG", "q4G", "k4G", "v4G", "dT4G", "dscrG",
            *[f"{n}{i}" for n in ("kscrG", "qscrG", "qTbG", "qsTbG", "kTbG",
                                  "ATsG", "SbG") for i in range(2)])
    prog.barrier(exclude=QACT_DOMS)
    if dbg is not None:
        kb.dma(dbg[0], OT, [], q="qsp")
        prog.barrier(exclude=QACT_DOMS)

    o_nb = ar.alloc("o_nb", 8 * D, BF16)
    o_tm = ar.alloc("o_tm", D, F32)
    sq = ar.alloc("sqG2", D, F32)
    vv = ar.alloc("vvG2", 1, F32)
    prev = None
    for ti in range(NTO):
        toks = []
        for g in range(4):
            srcs = [OT[:, (g * 4 + j) * TOWN + ti * 128:
                       (g * 4 + j) * TOWN + (ti + 1) * 128] for j in range(4)]
            toks.append(kb.tgrp(
                srcs, [("dve" if g % 2 else "act",
                        o_tm[:, g * 512:(g + 1) * 512], 0, 4, 128)],
                [prev] if prev else [], site="g2a"))
        t5 = _rms_norm(kb, o_tm, o_nb[:, ti * D:(ti + 1) * D], sq, vv, [join(toks)])
        prev = t5
    if dbg is not None:
        prog.barrier(exclude=QACT_DOMS)
        kb.dma(dbg[1], o_nb, [], q="qsp")
    ar.free("OTl1", "o_tm", "sqG2", "vvG2")
    prog.barrier(exclude=QACT_DOMS)

    ogT = ar.alloc("ogT", 16 * TOWN, BF16)
    wgc = ar.alloc("wgcG", 16 * 512, BF16)
    sgel = ar.alloc("sgelG", 512, F32)
    ogc = ar.alloc("ogcG", 512, BF16)
    wtok = None
    prev_mul = None
    prev_tr = None
    for ch in range(4):
        wtok = kb.stream_cast(wgc, kb.inp["wgate"], D, ch * 512, (ch + 1) * 512,
                              [wtok, prev_tr])
        for ti in range(NTO):
            bi, bk, bd = kb.bank([wtok])
            tm = kb.mm16(bk, lambda f, ti=ti: xn1T[:, f * TOWN + ti * 128:
                                                   f * TOWN + (ti + 1) * 128],
                         lambda f: wgc[:, f * 512:(f + 1) * 512], bd)
            tsg = kb.act(lambda E, bk=bk: E.activation(out=sgel, in_=bk,
                                                       func=AF.Sigmoid),
                         [tm, prev_mul])
            kb.bank_last[bi] = tsg
            tmul = kb.dve(lambda E, ti=ti, ch=ch: E.tensor_mul(
                ogc, sgel, o_nb[:, ti * D + ch * 512:ti * D + (ch + 1) * 512]),
                [tsg, prev_tr])
            prev_tr = kb.tgrp(
                [ogc[:, j * 128:(j + 1) * 128] for j in range(4)],
                [("act" if ti % 2 else "dve",
                  fview(ogT, 16, TOWN, ti * 128, (ti + 1) * 128)
                  [:, ch * 4:ch * 4 + 4, :], 0, 4, 128)], [tmul],
                site="g2b")
            prev_mul = tmul
    ar.free("o_nb", "wgcG", "sgelG", "ogcG")
    prog.barrier(exclude=QACT_DOMS)

    woc = ar.alloc("wocG", 16 * 512, BF16)
    wtok = None
    chunk_done = {}
    for ch in range(4):
        wtok = kb.stream_cast(woc, kb.inp["wout"], D, ch * 512, (ch + 1) * 512,
                              [wtok, chunk_done.get(ch - 1)])
        last = None
        for ti in range(NTO):
            bi, bk, bd = kb.bank([wtok])
            tm = kb.mm16(bk, lambda f, ti=ti: ogT[:, f * TOWN + ti * 128:
                                                  f * TOWN + (ti + 1) * 128],
                         lambda f: woc[:, f * 512:(f + 1) * 512], bd)
            ta = kb.dve(lambda E, bk=bk, o=kb.acc[:, ti * D + ch * 512:
                                                  ti * D + (ch + 1) * 512]:
                        E.tensor_add(o, o, bk), [tm])
            kb.bank_last[bi] = ta
            last = ta
        chunk_done[ch] = last
    ar.free("ogT", "wocG")
    prog.barrier(exclude=QACT_DOMS)
    if dbg is not None:
        kb.dma(dbg[2], kb.acc, [], q="qsp")
        prog.barrier(exclude=QACT_DOMS)


def emit_l1_ffn(kb, dbg=None):
    ar, prog = kb.ar, kb.prog
    xn1T = kb.xn1T
    ar.free("Sst", "qdc", "kdc", "bdc", "diffT", "iotaQ", "iotaK", "rmsb",
            "slopeb", "selc")
    NW = FF1 * D
    gacc = kb.gacc
    pcol = ar.alloc("pcol", 1, F32)
    invg = ar.alloc("invg", 4, F32)
    prev_add = kb.gamma_tok
    thr3 = invg
    nthr3 = ar.alloc("nthr3", 4, F32)
    prevb = prev_add
    grow = ar.alloc("grow", 128, F32)
    for t in range(3):
        tt = kb.ttile(gacc[:, t:t + 1], grow[0:1, 0:128], [prevb], fp32=True,
                      evac="dve")
        t2 = kb.dve(lambda E: E.reduce_sum(pcol[0:1, :], grow[0:1, 0:128],
                                           axis=AX.X), [tt])
        t3 = kb.dve(lambda E: E.tensor_scalar(pcol[0:1, :], pcol[0:1, :],
                                              0.5 / NW, 0.5e-8, ALU.mult,
                                              ALU.add), [t2])
        bi, bk, bd = kb.bank([t3])
        t5 = kb.pe(lambda E, bk=bk: E.matmul(bk[:, 0:1], kb.onesf[0:1, :],
                                             pcol[0:1, 0:1], start=True,
                                             stop=True), bd)
        t6 = kb.dve(lambda E, o=thr3[:, t:t + 1], bk=bk: E.tensor_copy(
            o, bk[:, 0:1]), [t5])
        t7 = kb.dve(lambda E, o=nthr3[:, t:t + 1], a=thr3[:, t:t + 1]:
                    E.tensor_scalar_mul(o, a, -1.0), [t6])
        kb.bank_last[bi] = t7
        prevb = t7
    if dbg is not None:
        kb.dma(dbg, thr3, [prevb], q="qsp")
    ar.free(*kb.gamma_bufs)
    ar.free("pcol", "gacc", "grow")
    prog.barrier(exclude=QACT_DOMS)

    NWFC = 4
    wfc_ = [ar.alloc(f"wfcH{i}", 1024, F32) for i in range(NWFC)]
    cbuf_ = [ar.alloc(f"cbufH{i}", 1024, BF16) for i in range(2)]
    w1q_ = [ar.alloc(f"w1qH{i}", 16 * 512, BF16) for i in range(2)]
    w2q_ = [ar.alloc(f"w2qH{i}", 16 * 512, BF16) for i in range(2)]
    w3q = ar.alloc("w3qH", 4 * D, BF16)
    sgel_ = [ar.alloc(f"sgelH{i}", 512, F32) for i in range(2)]
    hidb_ = [ar.alloc(f"hidbH{i}", 512, BF16) for i in range(2)]
    hTt_ = [ar.alloc(f"hTtH{i}", 4 * 128, BF16) for i in range(2)]
    NCH = FF1 // 512
    srcs = {nm: kb.inp[nm].rearrange("(t p) n -> p t n", p=128)
            for nm in ("fw1T", "fw2T", "fw3T")}

    def pieces(hc):
        pl = []
        for tnum, nm in ((0, "fw1T"), (1, "fw2T")):
            for j in range(8):
                pl.append((tnum, 2 * j, 2, hc * 512, 512, 2 * j * 512))
        for j in range(8):
            pl.append((2, hc * 4 + j // 2, 1, (j % 2) * 1024, 1024,
                       (j // 2) * D + (j % 2) * 1024))
        return pl

    all_pieces = [p for hc in range(NCH) for p in pieces(hc)]
    wfc_readers = [None] * NWFC
    cbuf_reader = [None, None]
    users12 = {}
    user3 = [None]
    dma_toks = {}
    st = {"di": 0}

    def pump_dma(upto):
        while st["di"] < min(upto, len(all_pieces)):
            i = st["di"]
            st["di"] += 1
            tnum, rbase, ntl, c0, ncols, dbase = all_pieces[i]
            buf = wfc_[i % NWFC]
            nm = ("fw1T", "fw2T", "fw3T")[tnum]
            dma_toks[i] = kb.dma(
                buf[:, 0:ntl * ncols].rearrange("p (t n) -> p t n", t=ntl),
                srcs[nm][:, rbase:rbase + ntl, c0:c0 + ncols],
                [wfc_readers[i % NWFC]], q="qsp")

    def emit_cmp(i):
        pump_dma(i + NWFC)
        hc_i, tnum = i // 24, all_pieces[i][0]
        _, rbase, ntl, c0, ncols, dbase = all_pieces[i]
        buf, cb = wfc_[i % NWFC], cbuf_[i % 2]
        dstq = (w1q_[hc_i % 2], w2q_[hc_i % 2], w3q)[tnum]
        user = user3[0] if tnum == 2 else users12.get((tnum, hc_i % 2))
        wtok = dma_toks.pop(i)
        tq1 = kb.dve(lambda E, a=buf[:, 0:ntl * ncols],
                     o=dstq[:, dbase:dbase + ntl * ncols],
                     s=thr3[:, tnum:tnum + 1]:
                     E.tensor_single_scalar(o, a, s, ALU.is_gt),
                     [wtok, user])
        tq1b = kb.dve(lambda E, a=buf[:, 0:ntl * ncols], o=cb[:, 0:ntl * ncols],
                      s=nthr3[:, tnum:tnum + 1]:
                      E.tensor_single_scalar(o, a, s, ALU.is_lt),
                      [wtok, cbuf_reader[i % 2]])
        tq2 = kb.pool_op(lambda E, o=dstq[:, dbase:dbase + ntl * ncols],
                         a=cb[:, 0:ntl * ncols]: E.tensor_sub(o, o, a),
                         [tq1, tq1b, user])
        wfc_readers[i % NWFC] = join([tq1, tq1b])
        cbuf_reader[i % 2] = tq2
        return tq2

    q12done = {}
    q3done = {}
    for i in range(16):
        q12done[0] = emit_cmp(i)
    for i in range(16, 24):
        q3done[0] = emit_cmp(i)
    hc_done = None
    prev_hid = None
    prev_htr = None
    if dbg is not None:
        td1 = kb.dma(kb.dbgQ1, w1q_[0], [q12done[0]], q="qsp")
        td3 = kb.dma(kb.dbgQ3, w3q, [q3done[0]], q="qsp")
        q12done[0] = join([q12done[0], td1])
        q3done[0] = join([q3done[0], td3])
        hc_done = join([td1, td3])
    for hc in range(NCH):
        w1q, w2q = w1q_[hc % 2], w2q_[hc % 2]
        last = None
        mmq = {}

        def h1_mm(ti, hc=hc, w1q=w1q, w2q=w2q):
            if hc + 1 < NCH:
                for i in range(24 * (hc + 1) + 2 * ti,
                               24 * (hc + 1) + 2 * ti + 2):
                    q12done[hc + 1] = emit_cmp(i)
            bi1, bk1, bd1 = kb.bank([q12done[hc]])
            tm1 = kb.mm16(bk1, lambda f, ti=ti: xn1T[:, f * TOWN + ti * 128:
                                                     f * TOWN + (ti + 1) * 128],
                          lambda f, w1q=w1q: w1q[:, f * 512:(f + 1) * 512], bd1)
            bi2, bk2, bd2 = kb.bank([q12done[hc]])
            tm2 = kb.mm16(bk2, lambda f, ti=ti: xn1T[:, f * TOWN + ti * 128:
                                                     f * TOWN + (ti + 1) * 128],
                          lambda f, w2q=w2q: w2q[:, f * 512:(f + 1) * 512], bd2)
            mmq[ti] = (bi1, bk1, tm1, bi2, bk2, tm2)

        h1_mm(0)
        for ti in range(NTO):
            if ti + 1 < NTO:
                h1_mm(ti + 1)
            sgel, hidb, hTt = sgel_[ti % 2], hidb_[ti % 2], hTt_[ti % 2]
            bi1, bk1, tm1, bi2, bk2, tm2 = mmq.pop(ti)
            tsg = kb.act(lambda E, bk1=bk1, o=sgel: E.activation(
                out=o, in_=bk1, func=AF.Sigmoid), [tm1, prev_hid])
            kb.bank_last[bi1] = tsg
            thid = kb.dve(lambda E, bk2=bk2, o=hidb, sg=sgel:
                          E.tensor_mul(o, sg, bk2), [tsg, tm2, prev_htr])
            kb.bank_last[bi2] = thid
            last_tm = {"fw1T": tm1, "fw2T": tm2}
            if dbg is not None and hc == 0 and ti == 0:
                thid = join([thid, kb.dma(kb.dbgHid, hidb, [thid], q="qsp")])
            htr = kb.tgrp([hidb[:, j * 128:(j + 1) * 128] for j in range(4)],
                          [("act" if ti % 2 else "dve", hTt[:, 0:512],
                            0, 4, 128)], [thid], site="h1")
            for ch in range(4):
                bi3, bk3, bd3 = kb.bankT([htr, q3done[hc]])
                tm3 = None
                for jt in range(4):
                    tm3 = kb.pe(lambda E, bk3=bk3,
                                lt=hTt[:, jt * 128:(jt + 1) * 128],
                                rh=w3q[:, jt * D + ch * 512:jt * D + (ch + 1) * 512],
                                s=(jt == 0), p=(jt == 3):
                                E.matmul(bk3, lt, rh, start=s, stop=p),
                                bd3 if jt == 0 else ())
                ta = kb.dve(lambda E, bk3=bk3,
                            o=kb.acc[:, ti * D + ch * 512:ti * D + (ch + 1) * 512]:
                            E.tensor_add(o, o, bk3), [tm3])
                kb.psT_last[bi3] = ta
                last = ta
            last_tm["fw3T"] = tm3
            prev_hid = tsg
            prev_htr = htr
        users12[(0, hc % 2)] = last_tm["fw1T"]
        users12[(1, hc % 2)] = last_tm["fw2T"]
        user3[0] = last_tm["fw3T"]
        if hc + 1 < NCH:
            for i in range(24 * (hc + 1) + 16, 24 * (hc + 1) + 24):
                q3done[hc + 1] = emit_cmp(i)
        hc_done = last
    ar.free("w3qH", "invg", "nthr3",
            *[f"wfcH{i}" for i in range(NWFC)],
            *[f"{n}{i}" for n in ("cbufH", "w1qH", "w2qH",
                                  "sgelH", "hidbH", "hTtH") for i in range(2)])
    prog.barrier(exclude=QACT_DOMS)


def emit_out(kb, out_ap):
    tok = None
    for ti in range(NTO):
        tok = kb.dma(out_ap[ti * 128:(ti + 1) * 128, :],
                     kb.acc[:, ti * D:(ti + 1) * D], [], q="qsp")
    kb.prog.barrier(exclude=QACT_DOMS)
    return tok


def setup_min(kb):
    for name, shape in (("identb", (128, 128)), ("identf", (128, 128)),
                        ("eps5", (128, 1)), ("eps6", (128, 1))):
        dt = BF16 if name == "identb" else F32
        kb.din(name, shape, dt)
    ar = kb.ar
    kb.identb = ar.alloc("identb", 128, BF16)
    kb.identf = ar.alloc("identf", 128, F32)
    kb.eps5 = ar.alloc("eps5", 1, F32)
    kb.eps6 = ar.alloc("eps6", 1, F32)
    for n in ("identb", "identf", "eps5", "eps6"):
        kb.dma(getattr(kb, n), kb.inp[n][:, :], [], q="qsp")


DEBUG_L1 = False


def build_l1():
    kb = KB()
    setup_min(kb)
    declare_l1(kb)
    load_l1_consts(kb)
    kb.gamma = GammaStream(kb)
    x1own = kb.din("x1own", (TOWN, D))
    x1pre = kb.din("x1pre", (TOWN, D))
    out = kb.dout("out", (TOWN, D))
    if DEBUG_L1:
        kb.dbgQ1 = kb.dout("dbgQ1", (128, 16 * 512), BF16)
        kb.dbgQ3 = kb.dout("dbgQ3", (128, 4 * D), BF16)
        kb.dbgHid = kb.dout("dbgHid", (128, 512), BF16)
        dbgS = kb.dout("dbgS", (128, 16 * 128))
        dbgOT = kb.dout("dbgOT", (128, 16 * TOWN), BF16)
        dbgON = kb.dout("dbgON", (128, 8 * D), BF16)
        dbgA = kb.dout("dbgA", (128, 8 * D))
        dbgI = kb.dout("dbgI", (128, 4))
    for ti in range(NTO):
        kb.dma(kb.acc[:, ti * D:(ti + 1) * D], x1own[ti * 128:(ti + 1) * 128, :],
               [], q="qsp")
    kb.prog.barrier(exclude=QACT_DOMS)
    kb.gamma.finishup()
    emit_l1_consts(kb)
    emit_prefix_chain(kb, x1pre)
    if DEBUG_L1:
        kb.prog.barrier(exclude=QACT_DOMS)
        kb.dma(dbgS, kb.Sst, [], q="qsp")
        kb.prog.barrier(exclude=QACT_DOMS)
    emit_l1_own(kb, dbg=(dbgOT, dbgON, dbgA) if DEBUG_L1 else None)
    emit_l1_ffn(kb, dbg=dbgI if DEBUG_L1 else None)
    emit_out(kb, out)
    finish(kb)
    return kb



def _seg_x(x, b, start):
    out = np.zeros((SEGT, D), np.float32)
    lo = start - HALO
    src_lo = max(lo, 0)
    out[src_lo - lo:HALO] = x[b, src_lo:start]
    out[HALO:] = x[b, start:start + TOWN]
    return out


def _seg_mask(start):
    kmin = HALO - start if start < HALO else 0
    i = np.arange(128)[:, None]
    c = np.arange(384)[None, :]
    band = (c >= i + 1) & (c <= i + 256)
    out = np.zeros((3, 128, 400), np.float32)
    for q in range(3):
        out[q, :, 0:384] = np.where(band & ((c + q * 128) >= kmin), 0.0, NEG)
    return np.ascontiguousarray(out)


def _bcast(v):
    v = np.asarray(v, np.float32).reshape(1, -1)
    return np.ascontiguousarray(np.broadcast_to(v, (128, v.shape[1])))


def stage_shared(ins):
    t = lambda a: np.ascontiguousarray(np.asarray(a, np.float32).T)
    sh = {
        "wqT": t(ins["L0_attn_Wq"]), "wkT": t(ins["L0_attn_Wk"]),
        "wvT": t(ins["L0_attn_Wv"]), "woT": t(ins["L0_attn_Wo"]),
        "wrT": t(ins["L0_ffn_Wr"]),
        "w1T0": t(ins["L0_ffn_W1"]), "w2T0": t(ins["L0_ffn_W2"]),
        "gT": t(np.asarray(ins["L0_attn_global_tokens"], np.float32)[0]),
        "ln0s": _bcast(ins["L0_attn_ln_scale"]),
        "ln0b": _bcast(ins["L0_attn_ln_bias"]),
        "brb": _bcast(ins["L0_ffn_br"]),
        "identb": np.eye(128, dtype=ml_dtypes.bfloat16),
        "identf": np.eye(128, dtype=np.float32),
        "eps5": np.full((128, 1), 1e-5, np.float32),
        "eps6": np.full((128, 1), 1e-6, np.float32),
    }
    return sh


def stage_core_l0(x, core):
    b, h = core // 2, core % 2
    return {"xO": _seg_x(x, b, h * TOWN), "maskO": _seg_mask(h * TOWN)}


def stage_shared_l1(ins):
    t = lambda a: np.ascontiguousarray(np.asarray(a, np.float32).T)
    c = lambda a: np.ascontiguousarray(np.asarray(a, np.float32))
    m = np.arange(256)[:, None]
    l = np.arange(256)[None, :]
    dd = np.where(l >= m, (l - m).astype(np.float32), 1.0e5)
    diffT = np.ascontiguousarray(
        dd.reshape(2, 128, 256).transpose(1, 0, 2).reshape(128, 512))
    return {
        "wqkv": c(ins["L1_attn_w_qkv"]), "wgate": c(ins["L1_attn_w_gate"]),
        "wout": c(ins["L1_attn_w_out"]),
        "rmsb": _bcast(ins["L1_attn_rms_weight"]),
        "slopeb": _bcast(np.asarray(ins["L1_attn_slope_rate"], np.float32).reshape(-1)),
        "fw1T": t(ins["L1_ffn_w1"]), "fw2T": t(ins["L1_ffn_w2"]),
        "fw3T": t(ins["L1_ffn_w3"]),
        "iotaQ": np.ascontiguousarray(
            (np.arange(256).reshape(2, 128).T + 1).astype(np.float32)),
        "iotaK": np.ascontiguousarray(
            (255 - np.arange(256).reshape(2, 128).T).astype(np.float32)),
        "diffT": diffT,
        "onesf": np.ones((1, 128), np.float32),
        "identb": np.eye(128, dtype=ml_dtypes.bfloat16),
        "identf": np.eye(128, dtype=np.float32),
        "eps5": np.full((128, 1), 1e-5, np.float32),
        "eps6": np.full((128, 1), 1e-6, np.float32),
    }


def build_fused():
    kb = KB()
    setup_min(kb)
    declare_l0(kb)
    declare_l1(kb)
    xO = kb.din("xO", (SEGT, D))
    xP = kb.din("xP", (SEGT, D))
    maskO = kb.din("maskO", (3, 128, 400))
    maskP = kb.din("maskP", (3, 128, 400))
    out = kb.dout("out", (TOWN, D))
    x1P = kb.nc.dram_tensor("x1Pscr", [TOWN, D], F32).ap()
    load_l0_consts(kb)
    kb.gamma = GammaStream(kb)
    emit_l0_segment(kb, xP, maskP, x1P)
    emit_l0_segment(kb, xO, maskO, None)
    kb.ar.free("brb", "gTb", "wrTb")
    load_l1_consts(kb)
    kb.gamma.finishup()
    emit_l1_consts(kb)
    emit_prefix_chain(kb, x1P)
    emit_l1_own(kb)
    emit_l1_ffn(kb)
    emit_out(kb, out)
    finish(kb)
    return kb


FUSED = True


def kernel(**ins):
    if FUSED:
        return kernel_fused(ins)
    return kernel_2launch(ins)


def kernel_fused(ins):
    x = np.asarray(ins["L0_attn_x"], np.float32)
    sh = stage_shared(ins)
    sh.update(stage_shared_l1(ins))
    kb = build_fused()
    maps = []
    for c in range(8):
        b, h = c // 2, c % 2
        m = dict(sh)
        m["xO"] = _seg_x(x, b, h * TOWN)
        m["maskO"] = _seg_mask(h * TOWN)
        m["xP"] = _seg_x(x, b, (1 - h) * TOWN)
        m["maskP"] = _seg_mask((1 - h) * TOWN)
        m["selc"] = np.full((128, 1), float(h), np.float32)
        maps.append({k: m[k] for k in kb.inp})
    r = run_bass_kernel_spmd(kb.nc, maps, core_ids=list(range(8)))
    out = np.empty((4, 2048, D), np.float32)
    for c in range(8):
        b, h = c // 2, c % 2
        out[b, h * TOWN:(h + 1) * TOWN] = r.results[c]["out"]
    return out


def kernel_2launch(ins):
    x = np.asarray(ins["L0_attn_x"], np.float32)
    sh = stage_shared(ins)
    kb1 = build_l0()
    maps1 = []
    for c in range(8):
        m = dict(sh)
        m.update(stage_core_l0(x, c))
        maps1.append({k: m[k] for k in kb1.inp})
    r1 = run_bass_kernel_spmd(kb1.nc, maps1, core_ids=list(range(8)))
    sh1 = stage_shared_l1(ins)
    kb2 = build_l1()
    maps2 = []
    for c in range(8):
        h = c % 2
        m = dict(sh1)
        m["x1own"] = np.ascontiguousarray(r1.results[c]["x1"])
        m["x1pre"] = np.ascontiguousarray(r1.results[c ^ 1]["x1"])
        m["selc"] = np.full((128, 1), float(h), np.float32)
        maps2.append({k: m[k] for k in kb2.inp})
    r2 = run_bass_kernel_spmd(kb2.nc, maps2, core_ids=list(range(8)))
    out = np.empty((4, 2048, D), np.float32)
    for c in range(8):
        b, h = c // 2, c % 2
        out[b, h * TOWN:(h + 1) * TOWN] = r2.results[c]["out"]
    return out


# version 30 (final)
# speedup vs baseline: 1.1477x; 1.1439x over previous
import math
import numpy as np
import ml_dtypes
import concourse.bass as bass
import concourse.mybir as mybir
from concourse.bass_utils import run_bass_kernel_spmd

F32 = mybir.dt.float32
BF16 = mybir.dt.bfloat16
I8 = mybir.dt.int8
AF = mybir.ActivationFunctionType
ALU = mybir.AluOpType
AX = mybir.AxisListType

D = 2048
TOWN = 1024
HALO = 256
SEGT = HALO + TOWN
NT = SEGT // 128
NTO = TOWN // 128
G = 16
FF0 = 8192
FF1 = 5632
NEG = -1.0e30
INVSQ = 1.0 / math.sqrt(128.0)

COMPUTE = ("pe", "act", "dve", "sp", "pool")
QACT_DOMS = ("qact0", "qact1", "qact2", "qact3")
NSLOT = 4
QUEUES = ("qsp", "qact", "qpool")
SPLIT_TGRP = True
BATCH_SITES = {"g2a", "g2b", "h1"}
DRAIN_K = 10**9
DOMS = COMPUTE + tuple(f"{q}{s}" for q in QUEUES for s in range(NSLOT))


class Tok:
    __slots__ = ("dom", "val", "fr")

    def __init__(self, dom, val, fr):
        self.dom, self.val, self.fr = dom, val, fr


def _merge(dst, src):
    for k, v in src.items():
        if dst.get(k, 0) < v:
            dst[k] = v


class JTok:
    __slots__ = ("doms", "fr")

    def __init__(self, doms, fr):
        self.doms = doms
        self.fr = fr


def join(toks):
    toks = [t for t in toks if t is not None]
    fr = {}
    doms = {}
    for t in toks:
        _merge(fr, t.fr)
        if isinstance(t, JTok):
            _merge(doms, t.doms)
        else:
            if doms.get(t.dom, 0) < t.val:
                doms[t.dom] = t.val
    return JTok(doms, fr)


class Prog:
    def __init__(self):
        self.ops = {e: [] for e in COMPUTE}
        self.cnt = {d: 0 for d in DOMS}
        self.base = {}

    def op(self, eng, fn, deps=(), queue=None):
        dom = queue if queue is not None else eng
        inc = 16 if queue is not None else 1
        fr = dict(self.base)
        direct = 0
        for t in deps:
            if t is not None:
                _merge(fr, t.fr)
                if isinstance(t, JTok):
                    v = t.doms.get(eng, 0)
                    if v > direct:
                        direct = v
                elif t.dom == eng and t.val > direct:
                    direct = t.val
        val = self.cnt[dom] + inc
        self.cnt[dom] = val
        fr2 = dict(fr)
        fr2[dom] = val
        self.ops[eng].append((fr, fn, dom, inc, direct))
        return Tok(dom, val, fr2)

    def barrier(self, exclude=()):
        self.base = {d: self.cnt[d] for d in DOMS
                     if self.cnt[d] > 0 and d not in exclude}

    def emitters(self, sems):
        out = {}
        for eng in COMPUTE:
            oplist = self.ops[eng]

            def body(e, eng=eng, oplist=oplist):
                guaranteed = {}
                semcnt = 0
                drainmark = 0
                for fr, fn, dom, inc, direct in oplist:
                    if eng != "pe" and direct > drainmark and \
                            direct > semcnt - DRAIN_K:
                        e.drain()
                        drainmark = semcnt
                    for d in sorted(fr):
                        if d == eng:
                            continue
                        v = fr[d]
                        if v > guaranteed.get(d, 0):
                            e.wait_ge(sems[d], v)
                            guaranteed[d] = v
                    inst = fn(e)
                    if inst is not None:
                        inst.then_inc(sems[dom], inc)
                    if dom == eng:
                        semcnt += inc
            out[eng] = body
        return out


class Arena:
    def __init__(self, pool_ap, nbytes):
        self.pool = pool_ap
        self.live = {}
        self.free_list = [(0, nbytes)]
        self.hw = 0

    def alloc(self, name, cols, dtype):
        size = (cols * mybir.dt.size(dtype) + 63) & ~63
        cands = [(sz, i) for i, (off, sz) in enumerate(self.free_list)
                 if sz >= size]
        if cands:
            _, i = min(cands)
            off, sz = self.free_list[i]
            if size <= 4096:
                start = off + sz - size
                if sz == size:
                    del self.free_list[i]
                else:
                    self.free_list[i] = (off, sz - size)
            else:
                start = off
                if sz == size:
                    del self.free_list[i]
                else:
                    self.free_list[i] = (off + size, sz - size)
            self.live[name] = (start, size)
            self.hw = max(self.hw, start + size)
            return self.pool[:, start:start + cols * mybir.dt.size(dtype)].bitcast(dtype)
        raise MemoryError(f"arena: no room for {name} ({size}B) "
                          f"live={sorted((k, v[1]) for k, v in self.live.items())}")

    def free(self, *names):
        for name in names:
            off, sz = self.live.pop(name)
            self.free_list.append((off, sz))
        self.free_list.sort()
        out = []
        for off, sz in self.free_list:
            if out and out[-1][0] + out[-1][1] == off:
                out[-1][1] += sz
            else:
                out.append([off, sz])
        self.free_list = [tuple(x) for x in out]



class KB:
    def __init__(self):
        nc = bass.Bass("TRN2", target_bir_lowering=False, debug=False)
        self.nc = nc
        self.inp = {}
        self.prog = Prog()
        self.pool_h = nc.alloc_sbuf_tensor("pool", [128, 146176], mybir.dt.uint8)
        self.acc = nc.alloc_sbuf_tensor("accb", [128, 8 * D], F32).ap()
        self.ar = Arena(self.pool_h.ap(), 146176)
        self.psum = [nc.alloc_psum_tensor(f"ps{i}", [128, 512], F32).ap() for i in range(4)]
        self.psT = [nc.alloc_psum_tensor(f"psTf{i}", [128, 512], F32).ap() for i in range(4)]
        self.psT_last = [None] * 4
        self.bank_last = {}
        self.ps_rr = 0
        self.tr_rr = 0
        self.persist = {}
        self.xnT_scr = nc.dram_tensor("xnT_scr", [128, 16 * TOWN], BF16).ap()
        self.qidx = {q: 0 for q in QUEUES}
        self.qlast = {f"{q}{s}": None for q in QUEUES for s in range(NSLOT)}

    def din(self, name, shape, dtype=F32):
        ap = self.nc.dram_tensor(name, list(shape), dtype, kind="ExternalInput").ap()
        self.inp[name] = ap
        return ap

    def dout(self, name, shape, dtype=F32):
        return self.nc.dram_tensor(name, list(shape), dtype, kind="ExternalOutput").ap()

    def dma(self, out, in_, deps=(), q="qsp"):
        engmap = {"qsp": "sp", "qact": "act", "qpool": "pool"}
        slot = self.qidx[q] % NSLOT
        self.qidx[q] += 1
        dom = f"{q}{slot}"
        dd = list(deps)
        if self.qlast[dom] is not None:
            dd.append(self.qlast[dom])
        t = self.prog.op(engmap[q],
                         lambda E, o=out, i=in_: E.dma_start(out=o, in_=i),
                         dd, queue=dom)
        self.qlast[dom] = t
        return t

    def pe(self, fn, deps=()):
        return self.prog.op("pe", fn, deps)

    def act(self, fn, deps=()):
        return self.prog.op("act", fn, deps)

    def dve(self, fn, deps=()):
        return self.prog.op("dve", fn, deps)

    def pool_op(self, fn, deps=()):
        return self.prog.op("pool", fn, deps)

    def bank(self, deps):
        i = self.ps_rr % len(self.psum)
        self.ps_rr += 1
        dd = list(deps)
        if i in self.bank_last:
            dd.append(self.bank_last[i])
        return i, self.psum[i], dd

    def bankT(self, deps):
        r = self.tr_rr % 4
        self.tr_rr += 1
        dd = list(deps)
        if self.psT_last[r] is not None:
            dd.append(self.psT_last[r])
        return r, self.psT[r], dd

    def mm16(self, bank_ap, lhsT_fn, rhs_fn, deps, nk=16):
        t = None
        for f in range(nk):
            dd = deps if f == 0 else ()
            lt, rh = lhsT_fn(f), rhs_fn(f)
            t = self.pe(lambda E, lt=lt, rh=rh, o=bank_ap, s=(f == 0),
                        p=(f == nk - 1): E.matmul(o, lt, rh, start=s, stop=p),
                        dd)
        return t

    def _tp(self, out_ap, src_ap, fp32, dd):
        if fp32:
            return self.pe(lambda E, o=out_ap, s=src_ap:
                           E.transpose(o, s, self.identf), dd)
        return self.pe(lambda E, o=out_ap, s=src_ap:
                       E.matmul(o, s, self.identb, start=True, stop=True), dd)

    def ttile(self, src_ap, dst_ap, deps, fp32=False, evac="act"):
        r = self.tr_rr % 4
        self.tr_rr += 1
        ps = self.psT[r]
        dd = list(deps) + ([self.psT_last[r]] if self.psT_last[r] is not None
                           else [])
        n = src_ap.shape[1]
        tt = self._tp(ps[0:n, 0:128], src_ap, fp32, dd)
        if len(dst_ap.shape) == 3:
            sv = ps.rearrange("p (a s) -> p a s", s=128)[0:n, 0:1, :]
        else:
            sv = ps[0:n, 0:128]
        if evac == "act":
            te = self.act(lambda E, d=dst_ap, s=sv: E.copy(d, s), [tt])
        else:
            te = self.dve(lambda E, d=dst_ap, s=sv: E.tensor_copy(d, s), [tt])
        self.psT_last[r] = te
        return te

    def tgrp(self, srcs, evacs, deps, fp32=False, site=""):
        if SPLIT_TGRP and site not in BATCH_SITES:
            toks = []
            for eng, dst, j0, j1, npart in evacs:
                for k in range(j0, j1):
                    if len(dst.shape) == 3:
                        d = dst[:, k - j0:k - j0 + 1, :]
                    else:
                        d = dst[:, (k - j0) * 128:(k - j0 + 1) * 128]
                    toks.append(self.ttile(srcs[k], d, deps, fp32=fp32,
                                           evac=eng))
            return join(toks)
        return self._tgrp(srcs, evacs, deps, fp32)

    def _tgrp(self, srcs, evacs, deps, fp32=False):
        assert len(srcs) <= 4
        r = self.tr_rr % 4
        self.tr_rr += 1
        ps = self.psT[r]
        dd = list(deps) + ([self.psT_last[r]] if self.psT_last[r] is not None
                           else [])
        tt = None
        for j, s in enumerate(srcs):
            n = s.shape[1]
            tt = self._tp(ps[0:n, j * 128:(j + 1) * 128], s, fp32,
                          dd if j == 0 else ())
        toks = []
        prev = None
        for eng, dst, j0, j1, npart in evacs:
            if len(dst.shape) == 3:
                sv = ps.rearrange("p (a s) -> p a s", s=128)[0:npart, j0:j1, :]
            else:
                sv = ps[0:npart, j0 * 128:j1 * 128]
            dd2 = [tt] if prev is None else [tt, prev]
            if eng == "act":
                prev = self.act(lambda E, d=dst, s=sv: E.copy(d, s), dd2)
            else:
                prev = self.dve(lambda E, d=dst, s=sv: E.tensor_copy(d, s), dd2)
            toks.append(prev)
        jt = join(toks)
        self.psT_last[r] = jt
        return jt

    def tgrp_add(self, srcs, dst3, deps, fp32=True):
        assert len(srcs) <= 4
        if SPLIT_TGRP:
            last = None
            for j, s in enumerate(srcs):
                r = self.tr_rr % 4
                self.tr_rr += 1
                ps = self.psT[r]
                dd = list(deps) + ([self.psT_last[r]]
                                   if self.psT_last[r] is not None else [])
                tt = self._tp(ps[:, 0:128], s, fp32, dd)
                ps3 = ps.rearrange("p (a s) -> p a s", s=128)[:, 0:1, :]
                ta = self.dve(lambda E, d=dst3[:, j:j + 1, :], s3=ps3:
                              E.tensor_add(d, d, s3), [tt])
                self.psT_last[r] = ta
                last = ta
            return last
        r = self.tr_rr % 4
        self.tr_rr += 1
        ps = self.psT[r]
        dd = list(deps) + ([self.psT_last[r]] if self.psT_last[r] is not None
                           else [])
        tt = None
        for j, s in enumerate(srcs):
            tt = self._tp(ps[:, j * 128:(j + 1) * 128], s, fp32,
                          dd if j == 0 else ())
        ps3 = ps.rearrange("p (a s) -> p a s", s=128)[:, 0:len(srcs), :]
        ta = self.dve(lambda E, d=dst3, s=ps3: E.tensor_add(d, d, s), [tt])
        self.psT_last[r] = ta
        return ta

    def stream_cast(self, dst_ap, src2d, total_rows, c0, c1, deps):
        tl = total_rows // 128
        src = src2d.rearrange("(t p) n -> p t n", p=128)
        slot = self.qidx["qpool"] % NSLOT
        self.qidx["qpool"] += 1
        dom = f"qpool{slot}"
        dd = list(deps)
        if self.qlast[dom] is not None:
            dd.append(self.qlast[dom])
        t = self.prog.op("pool",
                         lambda E, d=dst_ap, s=src, tl=tl, c0=c0, c1=c1:
                         E.dma_start(out=d.rearrange("p (t n) -> p t n", t=tl),
                                     in_=s[:, :, c0:c1]),
                         dd, queue=dom)
        self.qlast[dom] = t
        return t


    def stream_f32(self, dst_ap, src2d, total_rows, c0, c1, deps, q="qsp"):
        tl = total_rows // 128
        src = src2d.rearrange("(t p) n -> p t n", p=128)
        return self.dma(dst_ap.rearrange("p (t n) -> p t n", t=tl),
                        src[:, :, c0:c1], deps, q=q)


def tile_ln(kb, src, dst, scr, eps_col, deps, sqdump=None):
    s1, s2, sd, bb = scr
    sq = sqdump if sqdump is not None else kb.sqdump
    t1 = kb.dve(lambda E: E.reduce_sum(s1, src, axis=AX.X, negate=True), deps)
    t2 = kb.act(lambda E: E.activation(out=sq, in_=src, func=AF.Square,
                                       accum_out=s2), deps)
    t3 = kb.act(lambda E: E.activation(out=s1, in_=s1, func=AF.Copy,
                                       scale=1.0 / D), [t1])
    t4 = kb.dve(lambda E: E.tensor_scalar(bb, s1, s1, -1.0, ALU.mult,
                                          ALU.mult), [t3])
    t5 = kb.dve(lambda E: E.tensor_scalar_add(bb, bb, eps_col), [t4])
    t6 = kb.act(lambda E: E.activation(out=sd, in_=s2, func=AF.Sqrt,
                                       bias=bb, scale=1.0 / D), [t2, t5])
    t7 = kb.dve(lambda E: E.reciprocal(sd, sd), [t6])
    t8 = kb.dve(lambda E: E.tensor_scalar(dst, src, s1, sd, ALU.add,
                                          ALU.mult), [t7, t3])
    return t8


def transpose16(kb, src_bf, dst3, deps, nf=16):
    toks = []
    for gi, g0 in enumerate(range(0, nf, 4)):
        gn = min(4, nf - g0)
        srcs = [src_bf[:, (g0 + j) * 128:(g0 + j + 1) * 128] for j in range(gn)]
        eng = "act" if gi % 2 == 0 else "dve"
        toks.append(kb.tgrp(srcs, [(eng, dst3[:, g0:g0 + gn, :], 0, gn, 128)],
                            deps, site="t16"))
    return toks, join(toks)


def fview(buf, nf, stride, c0, c1):
    return buf.rearrange("p (f s) -> p f s", s=stride)[:, 0:nf, c0:c1]



def emit_l0_segment(kb, xseg, maskseg, x1_dst):
    ar, prog = kb.ar, kb.prog

    xniT = ar.alloc("xniT", 16 * SEGT, BF16)
    xnT = ar.alloc("xnT", 16 * TOWN, BF16)
    mask_sb = ar.alloc("mask_sb", NTO * 16, F32)
    kb.ln0s = ar.alloc("ln0s", D, BF16)
    kb.ln0b = ar.alloc("ln0b", D, BF16)
    tln0 = [kb.dma(kb.ln0s, kb.inp["ln0s"][:, :], [], q="qpool"),
            kb.dma(kb.ln0b, kb.inp["ln0b"][:, :], [], q="qpool")]
    xt = [ar.alloc(f"xt{i}", D, F32) for i in range(2)]
    kb.sqdump = ar.alloc("sqs", D, F32)
    lnscr = [tuple(ar.alloc(f"lns{i}_{j}", 1, F32) for j in range(4))
             for i in range(2)]
    xnv = ar.alloc("xnA", D, F32)
    xn = [xnv, xnv]
    xciv = kb.sqdump
    xci = [xciv, xciv]
    xnib = [ar.alloc(f"xnibA{i}", D, BF16) for i in range(2)]
    xnb = [ar.alloc(f"xnbA{i}", D, BF16) for i in range(2)]
    zscr = ar.alloc("zscr", 16, F32)
    zwork = ar.alloc("zwork", 16, F32)
    cwork = ar.alloc("cwork", 16, F32)
    m1 = ar.alloc("m1", 1, F32)
    tile_done = {}
    prev_router = None
    prev_t7b = None
    prev_inner = None
    for i in range(NT):
        b = i % 2
        dd = [tile_done[i - 2]] if i - 2 in tile_done else []
        tl = kb.dma(xt[b], xseg[i * 128:(i + 1) * 128, :], dd, q="qsp")
        t7 = tile_ln(kb, xt[b], xn[b], lnscr[b], kb.eps5,
                     [tl, prev_t7b, prev_inner])
        if i >= 2:
            ti = i - 2
            tacc = kb.dve(lambda E, o=kb.acc[:, ti * D:(ti + 1) * D],
                          a=xt[b], c=xn[b]: E.tensor_add(o, a, c), [t7])
            t7b = kb.act(lambda E, o=xnb[b], a=xn[b]: E.activation(
                out=o, in_=a, func=AF.Copy), [t7])
            prev_t7b = t7b
            xnt_toks, _ = transpose16(
                kb, xnb[b], fview(xnT, 16, TOWN, ti * 128, (ti + 1) * 128),
                [t7b])
            bi, bk, bd = kb.bank([join(xnt_toks), kb.wrT_tok])
            tm = kb.mm16(bk[:, 0:16],
                         lambda f, ti=ti: xnT[:, f * TOWN + ti * 128:
                                              f * TOWN + (ti + 1) * 128],
                         lambda f: kb.wrTb[:, f * 16:(f + 1) * 16], bd)
            tz = kb.dve(lambda E, bk=bk: E.tensor_add(zscr, bk[:, 0:16], kb.brb),
                        [tm, prev_router])
            kb.bank_last[bi] = tz
            tz2 = kb.dve(lambda E: E.tensor_copy(zwork, zscr), [tz])
            tprev = tz2
            for it in range(3):
                ta = kb.dve(lambda E: E.reduce_max(m1, zwork, axis=AX.X), [tprev])
                tb = kb.dve(lambda E: E.tensor_single_scalar(cwork, zwork, m1,
                                                             ALU.is_ge), [ta])
                tc = kb.dve(lambda E: E.tensor_scalar_mul(cwork, cwork, -BIGC),
                            [tb])
                tprev = kb.dve(lambda E: E.tensor_add(zwork, zwork, cwork), [tc])
            ta = kb.dve(lambda E: E.reduce_max(m1, zwork, axis=AX.X), [tprev])
            prev_router = kb.dve(lambda E, ti=ti: E.tensor_single_scalar(
                mask_sb[:, ti * 16:(ti + 1) * 16], zscr, m1, ALU.is_ge), [ta])
        else:
            tacc = t7
        t8 = tile_ln(kb, xn[b], xci[b], lnscr[b], kb.eps5, [t7, tacc])
        t9 = kb.dve(lambda E, a=xci[b], s=kb.ln0s: E.tensor_mul(a, a, s),
                    [t8] + tln0)
        t10 = kb.dve(lambda E, o=xnib[b], a=xci[b], s=kb.ln0b:
                     E.tensor_add(o, a, s), [t9])
        prev_inner = t10
        _, tlast = transpose16(
            kb, xnib[b], fview(xniT, 16, SEGT, i * 128, (i + 1) * 128), [t10])
        tile_done[i] = join([tlast, prev_router]) if i >= 2 else tlast
    kb.dma(kb.xnT_scr, xnT, [tile_done[NT - 1], tile_done[NT - 2]], q="qsp")
    for n in ("xt0", "xt1", "xnA", "xnibA0", "xnibA1", "sqs",
              "xnbA0", "xnbA1", "zscr", "zwork", "cwork", "m1",
              "ln0s", "ln0b", "xnT",
              "lns0_0", "lns0_1", "lns0_2", "lns0_3",
              "lns1_0", "lns1_1", "lns1_2", "lns1_3"):
        ar.free(n)
    prog.barrier(exclude=QACT_DOMS)

    OTb = ar.alloc("OTb", 16 * TOWN, BF16)
    QTb = ar.alloc("QTb", TOWN, BF16)
    KTb = ar.alloc("KTb", G + SEGT, BF16)
    V4b = ar.alloc("V4b", NT * 512, BF16)
    Vg4 = ar.alloc("Vg4", 512, BF16)
    wv4 = ar.alloc("wv4", 16 * 512, BF16)
    wqh = [ar.alloc(f"wqh{i}", 16 * 128, BF16) for i in range(2)]
    wkh = [ar.alloc(f"wkh{i}", 16 * 128, BF16) for i in range(2)]
    smscr_ = [ar.alloc(f"smscr{i}", 512, F32) for i in range(2)]
    pscr_ = [ar.alloc(f"pscr{i}", 512, F32) for i in range(2)]
    Ab_ = [ar.alloc(f"Ab{i}", 512, BF16) for i in range(2)]
    ATb_ = [ar.alloc(f"ATb{i}", 512, BF16) for i in range(2)]
    mxs_ = [ar.alloc(f"mxs{i}", 1, F32) for i in range(2)]
    dens_ = [ar.alloc(f"dens{i}", 1, F32) for i in range(2)]
    maskf = ar.alloc("maskf", 3 * 400, BF16)
    tmask = kb.dma(maskf.rearrange("p (q n) -> p q n", q=3),
                   maskseg.rearrange("q p n -> p q n"), [], q="qpool")

    wv_tok = None
    wq_tok = [None, None]
    wk_tok = [None, None]
    head_done = {}
    for g in range(4):
        vdeps = [wv_tok] if wv_tok is not None else []
        if g >= 1:
            vdeps += [head_done[g * 4 - 1]]
        wv_tok = kb.stream_cast(wv4, kb.inp["wvT"], D, g * 512, (g + 1) * 512, vdeps)
        bi, bk, bd = kb.bank([wv_tok, kb.gT_tok])
        tv = kb.mm16(bk[0:16, :], lambda f: kb.gTb[:, f * 16:(f + 1) * 16],
                     lambda f: wv4[:, f * 512:(f + 1) * 512], bd)
        te = kb.act(lambda E, bk=bk: E.copy(Vg4[0:16, :], bk[0:16, :]), [tv])
        kb.bank_last[bi] = te
        vg_tok = te
        vt_tok = []
        for kt in range(NT):
            bi, bk, bd = kb.bank([wv_tok])
            tv = kb.mm16(bk, lambda f, kt=kt: xniT[:, f * SEGT + kt * 128:
                                                   f * SEGT + (kt + 1) * 128],
                         lambda f: wv4[:, f * 512:(f + 1) * 512], bd)
            te = kb.act(lambda E, bk=bk, kt=kt: E.copy(
                V4b[:, kt * 512:(kt + 1) * 512], bk), [tv])
            kb.bank_last[bi] = te
            vt_tok.append(te)
        for hh in range(4):
            h = g * 4 + hh
            if getattr(kb, "gamma", None):
                kb.gamma.step(6)
            wb = h % 2
            wdeps = [wq_tok[wb]] if wq_tok[wb] is not None else []
            if h >= 2:
                wdeps += [head_done[h - 2]]
            wq_tok[wb] = kb.stream_cast(wqh[wb], kb.inp["wqT"], D, h * 128,
                                        (h + 1) * 128, wdeps)
            wk_tok[wb] = kb.stream_cast(wkh[wb], kb.inp["wkT"], D, h * 128,
                                        (h + 1) * 128, wdeps)
            prev_h = [head_done[h - 1]] if h >= 1 else []
            qt_toks = []
            for c in range(2):
                bi, bk, bd = kb.bank([wq_tok[wb]] + prev_h)
                tm = kb.mm16(bk, lambda f, wb=wb: wqh[wb][:, f * 128:(f + 1) * 128],
                             lambda f, c=c: xniT[:, f * SEGT + HALO + c * 512:
                                                 f * SEGT + HALO + (c + 1) * 512], bd)
                te = kb.act(lambda E, bk=bk, c=c: E.activation(
                    out=QTb[:, c * 512:(c + 1) * 512], in_=bk, func=AF.Copy,
                    scale=INVSQ), [tm])
                kb.bank_last[bi] = te
                qt_toks.append(te)
            bi, bk, bd = kb.bank([wk_tok[wb], kb.gT_tok] + prev_h)
            tm = kb.mm16(bk[:, 0:16], lambda f, wb=wb: wkh[wb][:, f * 128:(f + 1) * 128],
                         lambda f: kb.gTb[:, f * 16:(f + 1) * 16], bd)
            te = kb.act(lambda E, bk=bk: E.copy(KTb[:, 0:16], bk[:, 0:16]), [tm])
            kb.bank_last[bi] = te
            kt_toks = [te]
            for c, cw in ((0, 512), (1, 512), (2, 256)):
                bi, bk, bd = kb.bank([wk_tok[wb]] + prev_h)
                tm = kb.mm16(bk[:, 0:cw], lambda f, wb=wb: wkh[wb][:, f * 128:(f + 1) * 128],
                             lambda f, c=c, cw=cw: xniT[:, f * SEGT + c * 512:
                                                        f * SEGT + c * 512 + cw], bd)
                te = kb.act(lambda E, bk=bk, c=c, cw=cw: E.copy(
                    KTb[:, G + c * 512:G + c * 512 + cw], bk[:, 0:cw]), [tm])
                kb.bank_last[bi] = te
                kt_toks.append(te)
            last_ot = None
            par_ot = [None, None]
            smm = {}

            def emit_score_mm(qt):
                bi, bk, bd = kb.bank(qt_toks + kt_toks + [tmask])
                t1 = kb.pe(lambda E, bk=bk, qt=qt: E.matmul(
                    bk[:, 0:384], QTb[:, qt * 128:(qt + 1) * 128],
                    KTb[:, G + qt * 128:G + qt * 128 + 384],
                    start=True, stop=True), bd)
                t2 = kb.pe(lambda E, bk=bk, qt=qt: E.matmul(
                    bk[:, 384:400], QTb[:, qt * 128:(qt + 1) * 128],
                    KTb[:, 0:16], start=True, stop=True), ())
                smm[qt] = (bi, bk, t1, t2)

            emit_score_mm(0)
            for qt in range(NTO):
                if qt + 1 < NTO:
                    emit_score_mm(qt + 1)
                pb = qt % 2
                smscr, pscr, mxs, dens = (smscr_[pb], pscr_[pb], mxs_[pb],
                                          dens_[pb])
                Ab, ATb = Ab_[pb], ATb_[pb]
                bi, bk, t1, t2 = smm.pop(qt)
                mi = min(qt, 2)
                t3 = kb.dve(lambda E, bk=bk, mi=mi, sm=smscr: E.tensor_add(
                    sm[:, 0:400], bk[:, 0:400],
                    maskf[:, mi * 400:(mi + 1) * 400]), [t1, t2])
                kb.bank_last[bi] = t3
                t5 = kb.dve(lambda E, sm=smscr, mx=mxs: E.reduce_max(
                    mx, sm[:, 0:400], axis=AX.X, negate=True), [t3])
                t7 = kb.act(lambda E, sm=smscr, mx=mxs, p=pscr, dn=dens:
                            E.activation(out=p[:, 0:400], in_=sm[:, 0:400],
                                         func=AF.Exp, bias=mx, accum_out=dn),
                            [t5, par_ot[pb]])
                t8 = kb.dve(lambda E, dn=dens: E.reciprocal(dn, dn), [t7])
                t9 = kb.dve(lambda E, A=Ab, p=pscr, dn=dens:
                            E.tensor_scalar_mul(A[:, 0:400], p[:, 0:400], dn),
                            [t8])
                soft = {qt: t9}
                treg = [kb.tgrp(
                    [Ab[:, j * 128:(j + 1) * 128] for j in range(3)]
                    + [Ab[:, 384:400]],
                    [("act", ATb[:, 0:384], 0, 3, 128),
                     ("act", ATb[0:16, 384:512], 3, 4, 16)], [soft[qt]],
                    site="atb")]
                bi2, bk2, bd2 = kb.bank(treg + vt_tok + [vg_tok])
                tm = None
                for j in range(3):
                    tm = kb.pe(lambda E, bk2=bk2,
                               lt=V4b[:, (qt + j) * 512 + hh * 128:
                                      (qt + j) * 512 + (hh + 1) * 128],
                               rh=ATb[:, j * 128:(j + 1) * 128],
                               s=(j == 0): E.matmul(bk2[:, 0:128], lt, rh,
                                                    start=s, stop=False),
                               bd2 if j == 0 else ())
                tm = kb.pe(lambda E, bk2=bk2,
                           lt=Vg4[0:16, hh * 128:(hh + 1) * 128],
                           rh=ATb[0:16, 384:512]: E.matmul(
                               bk2[:, 0:128], lt, rh, start=False, stop=True),
                           ())
                te = kb.act(lambda E, bk2=bk2, h=h, qt=qt: E.copy(
                    OTb[:, h * TOWN + qt * 128:h * TOWN + (qt + 1) * 128],
                    bk2[:, 0:128]), [tm])
                kb.bank_last[bi2] = te
                last_ot = te
                par_ot[pb] = te
            head_done[h] = last_ot
    ar.free("QTb", "KTb", "V4b", "Vg4", "wv4", "wqh0", "wqh1", "wkh0", "wkh1",
            "smscr0", "smscr1", "pscr0", "pscr1", "Ab0", "Ab1", "ATb0", "ATb1",
            "mxs0", "mxs1", "dens0", "dens1", "maskf", "xniT")
    prog.barrier(exclude=QACT_DOMS)

    woc = [ar.alloc(f"woc{i}", 16 * 512, BF16) for i in range(2)]
    wo_tok = [None, None]
    wo_mmd = {}

    def wo_stream(c2):
        if c2 < 4:
            b2 = c2 % 2
            dd = [wo_tok[b2], wo_mmd.get(c2 - 2)]
            wo_tok[b2] = kb.stream_cast(woc[b2], kb.inp["woT"], D, c2 * 512,
                                        (c2 + 1) * 512, dd)

    wo_stream(0)
    wo_stream(1)
    for ch in range(4):
        b = ch % 2
        if getattr(kb, "gamma", None):
            kb.gamma.step(4)
        for qt in range(NTO):
            bi, bk, bd = kb.bank([wo_tok[b]])
            tm = kb.mm16(bk, lambda f, qt=qt: OTb[:, f * TOWN + qt * 128:
                                                  f * TOWN + (qt + 1) * 128],
                         lambda f, b=b: woc[b][:, f * 512:(f + 1) * 512], bd)
            if qt == NTO - 1:
                wo_mmd[ch] = tm
                wo_stream(ch + 2)
            ta = kb.dve(lambda E, bk=bk, qt=qt, ch=ch: E.tensor_add(
                kb.acc[:, qt * D + ch * 512:qt * D + (ch + 1) * 512],
                kb.acc[:, qt * D + ch * 512:qt * D + (ch + 1) * 512], bk), [tm])
            kb.bank_last[bi] = ta
    ar.free("OTb", "woc0", "woc1")
    prog.barrier(exclude=QACT_DOMS)

    hTb = ar.alloc("hTb", 64 * 512, BF16)
    xnT = ar.alloc("xnT", 16 * TOWN, BF16)
    kb.dma(xnT, kb.xnT_scr, [], q="qsp")
    prog.barrier(exclude=QACT_DOMS)
    gel = ar.alloc("gel", 512, BF16)
    gmb = ar.alloc("gmb", 512, BF16)
    yts_ = [ar.alloc(f"ytsE{i}", 512, BF16) for i in range(2)]
    prev_tmk = prev_tlast = None
    for half in range(2):
        w1c = [ar.alloc(f"w1c{i}", 16 * 512, BF16) for i in range(2)]
        w1_tok = [None, None]
        w1_mmd = {}

        def w1_stream(i2):
            if i2 < 16:
                b2 = i2 % 2
                dd = [w1_tok[b2], w1_mmd.get(i2 - 2)]
                w1_tok[b2] = kb.stream_cast(w1c[b2], kb.inp["w1T0"], D,
                                            i2 * 512, (i2 + 1) * 512, dd)

        w1_stream(0)
        w1_stream(1)
        seq = [(ibc, t4) for ibc in range(16) for t4 in range(4)]
        mmq = {}

        def w1_mm(k):
            ibc, t4 = seq[k]
            b = ibc % 2
            ti = half * 4 + t4
            bi, bk, bd = kb.bank([w1_tok[b]])
            tm = kb.mm16(bk, lambda f, ti=ti: xnT[:, f * TOWN + ti * 128:
                                                  f * TOWN + (ti + 1) * 128],
                         lambda f, b=b: w1c[b][:, f * 512:(f + 1) * 512], bd)
            if t4 == 3:
                w1_mmd[ibc] = tm
                w1_stream(ibc + 2)
            mmq[k] = (bi, bk, tm)

        w1_mm(0)
        for k, (ibc, t4) in enumerate(seq):
            if k + 1 < len(seq):
                w1_mm(k + 1)
            ti = half * 4 + t4
            bi, bk, tm = mmq.pop(k)
            if t4 == 0 and getattr(kb, "gamma", None):
                kb.gamma.step(3)
            tg = kb.act(lambda E, bk=bk: E.activation(out=gel, in_=bk,
                                                      func=AF.Gelu),
                        [tm, prev_tmk])
            kb.bank_last[bi] = tg
            tmk = kb.dve(lambda E, ti=ti, ibc=ibc: E.tensor_scalar_mul(
                gmb, gel, mask_sb[:, ti * 16 + ibc:ti * 16 + ibc + 1]),
                [tg, prev_tlast])
            _, tlast = transpose16(
                kb, gmb,
                hTb.rearrange("p (f s) -> p f s", s=512)
                [:, ibc * 4:ibc * 4 + 4, t4 * 128:(t4 + 1) * 128],
                [tmk], nf=4)
            prev_tmk, prev_tlast = tmk, tlast
        prog.barrier(exclude=QACT_DOMS)
        ar.free("w1c0", "w1c1")
        w2c = [ar.alloc(f"w2c{i}", 64 * 128, BF16) for i in range(2)]
        w2_tok = [None, None]
        w2_mmd = {}

        def w2_stream(f2):
            if f2 < 16:
                b2 = f2 % 2
                dd = [w2_tok[b2], w2_mmd.get(f2 - 2)]
                w2_tok[b2] = kb.dma(w2c[b2], kb.inp["w2T0"][f2], dd, q="qpool")

        w2_stream(0)
        w2_stream(1)
        w2q_pend = {}

        def w2_mm(f):
            b = f % 2
            bi, bk, bd = kb.bank([w2_tok[b]])
            tm = None
            for it in range(64):
                tm = kb.pe(lambda E, bk=bk, lt=w2c[b][:, it * 128:(it + 1) * 128],
                           rh=hTb[:, it * 512:(it + 1) * 512],
                           s=(it == 0), p=(it == 63):
                           E.matmul(bk, lt, rh, start=s, stop=p),
                           bd if it == 0 else ())
            w2_mmd[f] = tm
            w2_stream(f + 2)
            w2q_pend[f] = (bi, bk, tm)

        w2_mm(0)
        for f in range(16):
            if f + 1 < 16:
                w2_mm(f + 1)
            if getattr(kb, "gamma", None):
                kb.gamma.step(3)
            bi, bk, tm = w2q_pend.pop(f)
            yts = yts_[f % 2]
            tev = kb.act(lambda E, bk=bk, o=yts: E.copy(o, bk), [tm])
            kb.bank_last[bi] = tev
            kb.tgrp_add(
                [yts[:, j * 128:(j + 1) * 128] for j in range(4)],
                fview(kb.acc, 8, D, f * 128, (f + 1) * 128)
                [:, half * 4:half * 4 + 4, :], [tev], fp32=False)
        prog.barrier(exclude=QACT_DOMS)
        ar.free("w2c0", "w2c1")
    ar.free("hTb", "gel", "gmb", "ytsE0", "ytsE1", "xnT", "mask_sb")
    prog.barrier(exclude=QACT_DOMS)

    dump_tok = None
    if x1_dst is not None:
        for ti in range(NTO):
            dump_tok = kb.dma(x1_dst[ti * 128:(ti + 1) * 128, :],
                              kb.acc[:, ti * D:(ti + 1) * D], [], q="qsp")
        prog.barrier(exclude=QACT_DOMS)
    return dump_tok


BIGC = 1.0e9


def declare_l0(kb):
    for name, shape in (("wqT", (D, D)), ("wkT", (D, D)), ("wvT", (D, D)),
                        ("woT", (D, D)), ("wrT", (D, G)), ("w1T0", (D, FF0)),
                        ("w2T0", (D // 128, 128, FF0)), ("gT", (D, G)),
                        ("ln0s", (128, D)), ("ln0b", (128, D)),
                        ("brb", (128, G))):
        kb.din(name, shape, F32)


def load_l0_consts(kb):
    ar = kb.ar
    kb.brb = ar.alloc("brb", G, F32)
    kb.gTb = ar.alloc("gTb", 16 * G, BF16)
    kb.wrTb = ar.alloc("wrTb", 16 * G, BF16)
    kb.dma(kb.brb, kb.inp["brb"][:, :], [], q="qsp")
    kb.gT_tok = kb.stream_cast(kb.gTb, kb.inp["gT"], D, 0, G, [])
    kb.wrT_tok = kb.stream_cast(kb.wrTb, kb.inp["wrT"], D, 0, G, [])
    kb.prog.barrier(exclude=QACT_DOMS)


def build_l0():
    kb = KB()
    setup_min(kb)
    declare_l0(kb)
    load_l0_consts(kb)
    xO = kb.din("xO", (SEGT, D))
    maskO = kb.din("maskO", (3, 128, 400))
    x1 = kb.dout("x1", (TOWN, D))
    emit_l0_segment(kb, xO, maskO, x1)
    finish(kb)
    return kb


def finish(kb):
    nc = kb.nc
    from contextlib import ExitStack
    es = ExitStack()
    sems = {d: es.enter_context(nc.semaphore(f"sem_{d}")) for d in DOMS}
    bodies = kb.prog.emitters(sems)
    final_waits = {d: kb.prog.cnt[d] for d in DOMS if kb.prog.cnt[d] > 0}

    def sp_body(e):
        bodies["sp"](e)
        for dname in sorted(final_waits):
            e.wait_ge(sems[dname], final_waits[dname])

    with es, nc.Block() as block:
        block.sync(sp_body)
        block.tensor(bodies["pe"])
        block.scalar(bodies["act"])
        block.vector(bodies["dve"])
        block.gpsimd(bodies["pool"])
    return kb




def declare_l1(kb):
    for name, shape in (("wqkv", (D, 3 * D)), ("wgate", (D, D)), ("wout", (D, D)),
                        ("rmsb", (128, D)), ("slopeb", (128, G)),
                        ("fw1T", (D, FF1)), ("fw2T", (D, FF1)), ("fw3T", (FF1, D)),
                        ("iotaQ", (128, 2)), ("iotaK", (128, 2)),
                        ("diffT", (128, 512)), ("onesf", (1, 128)),
                        ("selc", (128, 1))):
        kb.din(name, shape, F32)


def load_l1_consts(kb):
    ar = kb.ar
    kb.rmsb = ar.alloc("rmsb", D, F32)
    kb.slopeb = ar.alloc("slopeb", G, F32)
    kb.iotaQ = ar.alloc("iotaQ", 2, F32)
    kb.iotaK = ar.alloc("iotaK", 2, F32)
    kb.diffT = ar.alloc("diffT", 512, F32)
    kb.onesf = ar.alloc("onesf", 128, F32)
    kb.selc = ar.alloc("selc", 1, F32)
    for n in ("rmsb", "slopeb", "iotaQ", "iotaK", "diffT", "selc"):
        kb.dma(getattr(kb, n), kb.inp[n][:, :], [], q="qsp")
    kb.dma(kb.onesf[0:1, :], kb.inp["onesf"][:, :], [], q="qsp")
    kb.prog.barrier(exclude=QACT_DOMS)


class GammaStream:

    def __init__(self, kb):
        self.kb = kb
        ar = kb.ar
        kb.gacc = ar.alloc("gacc", 4, F32)
        self.pcolg = ar.alloc("pcolg", 1, F32)
        self.wst = [ar.alloc(f"wst{i}", 704, F32) for i in range(2)]
        self.prev_add = kb.dve(lambda E: E.memset(kb.gacc, 0.0), [])
        self.wtok = [None] * 2
        self.pieces = []
        for t, nm in (((0, "fw1T")), ((1, "fw2T")), ((2, "fw3T"))):
            rows, cols = ((D, FF1), (D, FF1), (FF1, D))[t]
            piece = 704 if cols % 704 == 0 else 512
            for rt in range(rows // 128):
                for c0 in range(0, cols, piece):
                    self.pieces.append((t, nm, rt, c0, piece))
        self.i = 0
        kb.gamma_bufs = ("wst0", "wst1", "pcolg")

    def step(self, k):
        kb = self.kb
        for _ in range(k):
            if self.i >= len(self.pieces):
                return
            t, nm, rt, c0, piece = self.pieces[self.i]
            b = self.i % 2
            src = kb.inp[nm].rearrange("(t p) n -> p t n", p=128)
            tl = kb.dma(self.wst[b][:, 0:piece], src[:, rt, c0:c0 + piece],
                        [self.wtok[b], self.prev_add], q="qact")
            self.wtok[b] = tl
            tr = kb.dve(lambda E, a=self.wst[b][:, 0:piece]: E.reduce_sum(
                self.pcolg, a, axis=AX.X, apply_absolute_value=True),
                [tl, self.prev_add])
            self.prev_add = kb.dve(
                lambda E, o=kb.gacc[:, t:t + 1]: E.tensor_add(o, o, self.pcolg),
                [tr])
            self.i += 1

    def finishup(self):
        self.step(len(self.pieces))
        self.kb.gamma_tok = self.prev_add


def emit_l1_consts(kb):
    ar = kb.ar
    kb.qdc = ar.alloc("qdc", 32, F32)
    kb.kdc = ar.alloc("kdc", 32, F32)
    kb.bdc = ar.alloc("bdc", G, F32)
    kb.Sst = ar.alloc("Sst", 16 * 128, F32)
    scr = ar.alloc("cscr", 2, F32)
    tz = kb.dve(lambda E: E.memset(kb.Sst, 0.0), [])
    prev = None
    for h in range(16):
        sc = kb.slopeb[:, h:h + 1]
        t1 = kb.dve(lambda E, sc=sc: E.tensor_scalar_mul(scr, kb.iotaQ, sc),
                    [prev] if prev else [])
        t2 = kb.act(lambda E, o=kb.qdc[:, h * 2:(h + 1) * 2]: E.activation(
            out=o, in_=scr, func=AF.Exp, scale=-1.0), [t1])
        t3 = kb.dve(lambda E, sc=sc: E.tensor_scalar_mul(scr, kb.iotaK, sc), [t2])
        t4 = kb.act(lambda E, o=kb.kdc[:, h * 2:(h + 1) * 2]: E.activation(
            out=o, in_=scr, func=AF.Exp, scale=-1.0), [t3])
        t5 = kb.act(lambda E, o=kb.bdc[:, h:h + 1], sc=sc: E.activation(
            out=o, in_=sc, func=AF.Exp, scale=-256.0), [t4])
        prev = t5
    ar.free("cscr")
    kb.prog.barrier(exclude=QACT_DOMS)
    return tz


def _rms_norm(kb, o_tm, dst_bf, sq, vv, deps):
    t1 = kb.act(lambda E: E.activation(out=sq, in_=o_tm, func=AF.Square,
                                       accum_out=vv), deps)
    t2 = kb.act(lambda E: E.activation(out=vv, in_=vv, func=AF.Sqrt,
                                       bias=kb.eps6, scale=1.0 / D), [t1])
    t3 = kb.dve(lambda E: E.reciprocal(vv, vv), [t2])
    t5 = kb.dve(lambda E: E.scalar_tensor_tensor(dst_bf, o_tm, vv, kb.rmsb,
                                                 ALU.mult, ALU.mult), [t3])
    return t5


def emit_prefix_chain(kb, x1p_ap):
    ar, prog = kb.ar, kb.prog
    xnPT = ar.alloc("xnPT", 16 * TOWN, BF16)
    xt = [ar.alloc(f"xtP{i}", D, F32) for i in range(2)]
    kb.sqdump = ar.alloc("sqsP", D, F32)
    lnscr = [tuple(ar.alloc(f"lnsP{i}_{j}", 1, F32) for j in range(4))
             for i in range(2)]
    xnv = ar.alloc("xnP", D, F32)
    xnb = [ar.alloc(f"xnbP{i}", D, BF16) for i in range(2)]
    tile_done = {}
    for i in range(NTO):
        b = i % 2
        dd = [tile_done.get(i - 2)]
        tl = kb.dma(xt[b], x1p_ap[i * 128:(i + 1) * 128, :], dd, q="qsp")
        t7 = tile_ln(kb, xt[b], xnv, lnscr[b], kb.eps5, [tl])
        t8 = kb.act(lambda E, o=xnb[b]: E.activation(out=o, in_=xnv,
                                                     func=AF.Copy), [t7])
        _, tlast = transpose16(
            kb, xnb[b], fview(xnPT, 16, TOWN, i * 128, (i + 1) * 128), [t8])
        tile_done[i] = tlast
    for n in ("xtP0", "xtP1", "sqsP", "xnP", "xnbP0", "xnbP1",
              "lnsP0_0", "lnsP0_1", "lnsP0_2", "lnsP0_3",
              "lnsP1_0", "lnsP1_1", "lnsP1_2", "lnsP1_3"):
        ar.free(n)
    prog.barrier(exclude=QACT_DOMS)

    wbuf = ar.alloc("wbufP", 16 * 512, BF16)
    k4 = ar.alloc("k4P", 8 * 512, BF16)
    v4 = ar.alloc("v4P", 8 * 512, BF16)
    kscr = ar.alloc("kscrP", 256, BF16)
    wtok = None
    grp_done = None
    for g in range(4):
        wtok = kb.stream_cast(wbuf, kb.inp["wqkv"], D, D + g * 512,
                              D + (g + 1) * 512, [wtok, grp_done])
        ktoks = []
        for ti in range(NTO):
            bi, bk, bd = kb.bank([wtok])
            tm = kb.mm16(bk, lambda f, ti=ti: xnPT[:, f * TOWN + ti * 128:
                                                   f * TOWN + (ti + 1) * 128],
                         lambda f: wbuf[:, f * 512:(f + 1) * 512], bd)
            te = kb.act(lambda E, bk=bk, o=k4[:, ti * 512:(ti + 1) * 512]:
                        E.activation(out=o, in_=bk, func=AF.Silu), [tm])
            kb.bank_last[bi] = te
            ktoks.append(te)
        wtok = kb.stream_cast(wbuf, kb.inp["wqkv"], D, 2 * D + g * 512,
                              2 * D + (g + 1) * 512, [wtok, join(ktoks)])
        vtoks = []
        for ti in range(NTO):
            bi, bk, bd = kb.bank([wtok])
            tm = kb.mm16(bk, lambda f, ti=ti: xnPT[:, f * TOWN + ti * 128:
                                                   f * TOWN + (ti + 1) * 128],
                         lambda f: wbuf[:, f * 512:(f + 1) * 512], bd)
            te = kb.act(lambda E, bk=bk, o=v4[:, ti * 512:(ti + 1) * 512]:
                        E.activation(out=o, in_=bk, func=AF.Silu), [tm])
            kb.bank_last[bi] = te
            vtoks.append(te)
        gtok = join(ktoks + vtoks)
        hlast = None
        for hh in range(4):
            h = g * 4 + hh
            for b4 in range(4):
                kst = []
                for t in range(2):
                    kst.append(kb.dve(
                        lambda E, o=kscr[:, t * 128:(t + 1) * 128],
                        a=k4[:, (b4 * 2 + t) * 512 + hh * 128:
                             (b4 * 2 + t) * 512 + (hh + 1) * 128],
                        s=kb.kdc[:, h * 2 + t:h * 2 + t + 1]:
                        E.tensor_scalar_mul(o, a, s), [gtok, hlast]))
                bi, bk, bd = kb.bank(kst)
                tm = None
                for t in range(2):
                    tm = kb.pe(lambda E, bk=bk,
                               lt=kscr[:, t * 128:(t + 1) * 128],
                               rh=v4[:, (b4 * 2 + t) * 512 + hh * 128:
                                     (b4 * 2 + t) * 512 + (hh + 1) * 128],
                               s=(t == 0), p=(t == 1):
                               E.matmul(bk[:, 0:128], lt, rh, start=s, stop=p),
                               bd if t == 0 else ())
                ts1 = kb.dve(lambda E, o=kb.Sst[:, h * 128:(h + 1) * 128],
                             s=kb.bdc[:, h:h + 1]:
                             E.tensor_scalar_mul(o, o, s), [tm, hlast])
                ts2 = kb.dve(lambda E, o=kb.Sst[:, h * 128:(h + 1) * 128], bk=bk:
                             E.tensor_add(o, o, bk[:, 0:128]), [ts1])
                kb.bank_last[bi] = ts2
                hlast = ts2
        grp_done = hlast
    tsel = kb.dve(lambda E: E.tensor_scalar_mul(kb.Sst, kb.Sst, kb.selc),
                  [grp_done])
    ar.free("xnPT", "wbufP", "k4P", "v4P", "kscrP")
    prog.barrier(exclude=QACT_DOMS)
    return tsel


def emit_l1_own(kb, dbg=None):
    ar, prog = kb.ar, kb.prog
    xn1T = ar.alloc("xn1T", 16 * TOWN, BF16)
    kb.sqdump = ar.alloc("sqsG", D, F32)
    lnscr = [tuple(ar.alloc(f"lnsG{i}_{j}", 1, F32) for j in range(4))
             for i in range(2)]
    xnv = ar.alloc("xnG", D, F32)
    xnb = [ar.alloc(f"xnbG{i}", D, BF16) for i in range(2)]
    tile_done = {}
    for i in range(NTO):
        b = i % 2
        dd = [tile_done.get(i - 2)]
        t7 = tile_ln(kb, kb.acc[:, i * D:(i + 1) * D], xnv, lnscr[b], kb.eps5, dd)
        t8 = kb.act(lambda E, o=xnb[b]: E.activation(out=o, in_=xnv,
                                                     func=AF.Copy), [t7])
        _, tlast = transpose16(
            kb, xnb[b], fview(xn1T, 16, TOWN, i * 128, (i + 1) * 128), [t8])
        tile_done[i] = tlast
    for n in ("sqsG", "xnG", "xnbG0", "xnbG1",
              "lnsG0_0", "lnsG0_1", "lnsG0_2", "lnsG0_3",
              "lnsG1_0", "lnsG1_1", "lnsG1_2", "lnsG1_3"):
        ar.free(n)
    prog.barrier(exclude=QACT_DOMS)
    kb.xn1T = xn1T

    OT = ar.alloc("OTl1", 16 * TOWN, BF16)
    wbuf = ar.alloc("wbufG", 16 * 512, BF16)
    q4 = ar.alloc("q4G", 8 * 512, BF16)
    k4 = ar.alloc("k4G", 8 * 512, BF16)
    v4 = ar.alloc("v4G", 8 * 512, BF16)
    kscr_ = [ar.alloc(f"kscrG{i}", 256, BF16) for i in range(2)]
    qscr_ = [ar.alloc(f"qscrG{i}", 256, BF16) for i in range(2)]
    qTb_ = [ar.alloc(f"qTbG{i}", 256, BF16) for i in range(2)]
    qsTb_ = [ar.alloc(f"qsTbG{i}", 256, BF16) for i in range(2)]
    kTb_ = [ar.alloc(f"kTbG{i}", 256, BF16) for i in range(2)]
    ATs_ = [ar.alloc(f"ATsG{i}", 512, BF16) for i in range(2)]
    Sb_ = [ar.alloc(f"SbG{i}", 128, BF16) for i in range(2)]
    dT4 = ar.alloc("dT4G", 512, F32)
    dscr = ar.alloc("dscrG", 512, F32)
    wtok = None
    grp_done = None
    trsq, stm, omm, tcm = {}, {}, {}, {}
    td2_prev = [None]
    at_prev = [None]
    for g in range(4):
        projs = []
        for which, dst, cbase in (("q", q4, g * 512), ("k", k4, D + g * 512),
                                  ("v", v4, 2 * D + g * 512)):
            wtok = kb.stream_cast(wbuf, kb.inp["wqkv"], D, cbase, cbase + 512,
                                  [wtok, grp_done] + projs[-1:])
            toks = []
            for ti in range(NTO):
                bi, bk, bd = kb.bank([wtok])
                tm = kb.mm16(bk, lambda f, ti=ti: xn1T[:, f * TOWN + ti * 128:
                                                       f * TOWN + (ti + 1) * 128],
                             lambda f: wbuf[:, f * 512:(f + 1) * 512], bd)
                te = kb.act(lambda E, bk=bk, o=dst[:, ti * 512:(ti + 1) * 512]:
                            E.activation(out=o, in_=bk, func=AF.Silu), [tm])
                kb.bank_last[bi] = te
                toks.append(te)
            projs.append(join(toks))
        gtok = join(projs)
        hlast = None
        for hh in range(4):
            h = g * 4 + hh
            s0 = (g * 4 + hh) * 4
            td1 = kb.dve(lambda E, s=kb.slopeb[:, h:h + 1]:
                         E.tensor_scalar_mul(dscr, kb.diffT, s),
                         [gtok, td2_prev[0]])
            td2 = kb.act(lambda E: E.activation(out=dT4, in_=dscr, func=AF.Exp,
                                                scale=-1.0),
                         [td1, at_prev[0]])
            td2_prev[0] = td2
            state_prev = None
            for b4 in range(4):
                s = s0 + b4
                pb = b4 % 2
                kscr, qscr, qTb, qsTb, kTb, ATs, Sb = (
                    kscr_[pb], qscr_[pb], qTb_[pb], qsTb_[pb], kTb_[pb],
                    ATs_[pb], Sb_[pb])
                base = lambda t, col: (b4 * 2 + t) * 512 + col * 128
                qst = []
                for t in range(2):
                    qst.append(kb.dve(
                        lambda E, o=qscr[:, t * 128:(t + 1) * 128],
                        a=q4[:, base(t, hh):base(t, hh) + 128],
                        s=kb.qdc[:, h * 2 + t:h * 2 + t + 1]:
                        E.tensor_scalar_mul(o, a, s),
                        [gtok, trsq.get(s - 2)]))
                trs = [kb.tgrp(
                    [q4[:, base(t, hh):base(t, hh) + 128] for t in range(2)]
                    + [qscr[:, t * 128:(t + 1) * 128] for t in range(2)],
                    [("act", qTb[:, 0:256], 0, 2, 128),
                     ("act", qsTb[:, 0:256], 2, 4, 128)],
                    [gtok, qst[0], qst[1], stm.get(s - 2), omm.get(s - 2)],
                    site="g1"),
                    kb.tgrp(
                    [k4[:, base(t, hh):base(t, hh) + 128] for t in range(2)],
                    [("dve", kTb[:, 0:256], 0, 2, 128)],
                    [gtok, stm.get(s - 2)], site="g1")]
                trsq[s] = trs[0]
                tq = join(trs)
                bi, bk, bd = kb.bank([tq])
                st_toks = []
                for t in range(2):
                    st_toks.append(kb.pe(
                        lambda E, bk=bk, lt=kTb[:, t * 128:(t + 1) * 128],
                        rh=qTb, o=(0 if t == 0 else 256):
                        E.matmul(bk[:, o:o + 256], lt, rh, start=True,
                                 stop=True), bd if t == 0 else ()))
                stm[s] = st_toks[1]
                at = []
                for t in range(2):
                    at.append(kb.dve(
                        lambda E, o=ATs[:, t * 256:(t + 1) * 256],
                        a=bk[:, t * 256:(t + 1) * 256],
                        dd=dT4[:, t * 256:(t + 1) * 256]:
                        E.tensor_mul(o, a, dd),
                        [st_toks[1], td2, omm.get(s - 2)]))
                kb.bank_last[bi] = join(at)
                at_prev[0] = at[1]
                tSb = kb.act(lambda E, a=kb.Sst[:, h * 128:(h + 1) * 128],
                             o=Sb: E.copy(o, a),
                             [state_prev, omm.get(s - 2)])
                bi2, bk2, bd2 = kb.bank(at + [tSb])
                tm = None
                for t in range(2):
                    tm = kb.pe(lambda E, bk2=bk2,
                               lt=v4[:, base(t, hh):base(t, hh) + 128],
                               rh=ATs[:, t * 256:(t + 1) * 256],
                               s=(t == 0):
                               E.matmul(bk2[:, 0:256], lt, rh, start=s,
                                        stop=False), bd2 if t == 0 else ())
                tm = kb.pe(lambda E, bk2=bk2, lt=Sb, rh=qsTb: E.matmul(
                    bk2[:, 0:256], lt, rh, start=False, stop=True), ())
                omm[s] = tm
                tev = kb.act(lambda E, bk2=bk2,
                             o=OT[:, h * TOWN + b4 * 256:h * TOWN + (b4 + 1) * 256]:
                             E.copy(o, bk2[:, 0:256]), [tm])
                kb.bank_last[bi2] = tev
                kst = []
                for t in range(2):
                    kst.append(kb.dve(
                        lambda E, o=kscr[:, t * 128:(t + 1) * 128],
                        a=k4[:, base(t, hh):base(t, hh) + 128],
                        s=kb.kdc[:, h * 2 + t:h * 2 + t + 1]:
                        E.tensor_scalar_mul(o, a, s),
                        [gtok, tcm.get(s - 2)]))
                bi3, bk3, bd3 = kb.bank(kst + [tSb])
                tc = None
                for t in range(2):
                    tc = kb.pe(lambda E, bk3=bk3,
                               lt=kscr[:, t * 128:(t + 1) * 128],
                               rh=v4[:, base(t, hh):base(t, hh) + 128],
                               s=(t == 0), p=(t == 1):
                               E.matmul(bk3[:, 0:128], lt, rh, start=s, stop=p),
                               bd3 if t == 0 else ())
                tcm[s] = tc
                ts1 = kb.dve(lambda E, o=kb.Sst[:, h * 128:(h + 1) * 128],
                             s=kb.bdc[:, h:h + 1]:
                             E.tensor_scalar_mul(o, o, s), [tc, tSb])
                ts2 = kb.dve(lambda E, o=kb.Sst[:, h * 128:(h + 1) * 128], bk3=bk3:
                             E.tensor_add(o, o, bk3[:, 0:128]), [ts1])
                kb.bank_last[bi3] = ts2
                state_prev = ts2
                hlast = join([ts2, tev])
        grp_done = hlast
    ar.free("wbufG", "q4G", "k4G", "v4G", "dT4G", "dscrG",
            *[f"{n}{i}" for n in ("kscrG", "qscrG", "qTbG", "qsTbG", "kTbG",
                                  "ATsG", "SbG") for i in range(2)])
    prog.barrier(exclude=QACT_DOMS)
    if dbg is not None:
        kb.dma(dbg[0], OT, [], q="qsp")
        prog.barrier(exclude=QACT_DOMS)

    o_nb = ar.alloc("o_nb", 8 * D, BF16)
    o_tm = ar.alloc("o_tm", D, F32)
    sq = ar.alloc("sqG2", D, F32)
    vv = ar.alloc("vvG2", 1, F32)
    prev = None
    for ti in range(NTO):
        toks = []
        for g in range(4):
            srcs = [OT[:, (g * 4 + j) * TOWN + ti * 128:
                       (g * 4 + j) * TOWN + (ti + 1) * 128] for j in range(4)]
            toks.append(kb.tgrp(
                srcs, [("dve" if g % 2 else "act",
                        o_tm[:, g * 512:(g + 1) * 512], 0, 4, 128)],
                [prev] if prev else [], site="g2a"))
        t5 = _rms_norm(kb, o_tm, o_nb[:, ti * D:(ti + 1) * D], sq, vv, [join(toks)])
        prev = t5
    if dbg is not None:
        prog.barrier(exclude=QACT_DOMS)
        kb.dma(dbg[1], o_nb, [], q="qsp")
    ar.free("OTl1", "o_tm", "sqG2", "vvG2")
    prog.barrier(exclude=QACT_DOMS)

    ogT = ar.alloc("ogT", 16 * TOWN, BF16)
    wgc = ar.alloc("wgcG", 16 * 512, BF16)
    sgel = ar.alloc("sgelG", 512, F32)
    ogc = ar.alloc("ogcG", 512, BF16)
    wtok = None
    prev_mul = None
    prev_tr = None
    for ch in range(4):
        wtok = kb.stream_cast(wgc, kb.inp["wgate"], D, ch * 512, (ch + 1) * 512,
                              [wtok, prev_tr])
        for ti in range(NTO):
            bi, bk, bd = kb.bank([wtok])
            tm = kb.mm16(bk, lambda f, ti=ti: xn1T[:, f * TOWN + ti * 128:
                                                   f * TOWN + (ti + 1) * 128],
                         lambda f: wgc[:, f * 512:(f + 1) * 512], bd)
            tsg = kb.act(lambda E, bk=bk: E.activation(out=sgel, in_=bk,
                                                       func=AF.Sigmoid),
                         [tm, prev_mul])
            kb.bank_last[bi] = tsg
            tmul = kb.dve(lambda E, ti=ti, ch=ch: E.tensor_mul(
                ogc, sgel, o_nb[:, ti * D + ch * 512:ti * D + (ch + 1) * 512]),
                [tsg, prev_tr])
            prev_tr = kb.tgrp(
                [ogc[:, j * 128:(j + 1) * 128] for j in range(4)],
                [("act" if ti % 2 else "dve",
                  fview(ogT, 16, TOWN, ti * 128, (ti + 1) * 128)
                  [:, ch * 4:ch * 4 + 4, :], 0, 4, 128)], [tmul],
                site="g2b")
            prev_mul = tmul
    ar.free("o_nb", "wgcG", "sgelG", "ogcG")
    prog.barrier(exclude=QACT_DOMS)

    woc = ar.alloc("wocG", 16 * 512, BF16)
    wtok = None
    chunk_done = {}
    for ch in range(4):
        wtok = kb.stream_cast(woc, kb.inp["wout"], D, ch * 512, (ch + 1) * 512,
                              [wtok, chunk_done.get(ch - 1)])
        last = None
        for ti in range(NTO):
            bi, bk, bd = kb.bank([wtok])
            tm = kb.mm16(bk, lambda f, ti=ti: ogT[:, f * TOWN + ti * 128:
                                                  f * TOWN + (ti + 1) * 128],
                         lambda f: woc[:, f * 512:(f + 1) * 512], bd)
            ta = kb.dve(lambda E, bk=bk, o=kb.acc[:, ti * D + ch * 512:
                                                  ti * D + (ch + 1) * 512]:
                        E.tensor_add(o, o, bk), [tm])
            kb.bank_last[bi] = ta
            last = ta
        chunk_done[ch] = last
    ar.free("ogT", "wocG")
    prog.barrier(exclude=QACT_DOMS)
    if dbg is not None:
        kb.dma(dbg[2], kb.acc, [], q="qsp")
        prog.barrier(exclude=QACT_DOMS)


def emit_l1_ffn(kb, dbg=None):
    ar, prog = kb.ar, kb.prog
    xn1T = kb.xn1T
    ar.free("Sst", "qdc", "kdc", "bdc", "diffT", "iotaQ", "iotaK", "rmsb",
            "slopeb", "selc")
    NW = FF1 * D
    gacc = kb.gacc
    pcol = ar.alloc("pcol", 1, F32)
    invg = ar.alloc("invg", 4, F32)
    prev_add = kb.gamma_tok
    thr3 = invg
    nthr3 = ar.alloc("nthr3", 4, F32)
    prevb = prev_add
    grow = ar.alloc("grow", 128, F32)
    for t in range(3):
        tt = kb.ttile(gacc[:, t:t + 1], grow[0:1, 0:128], [prevb], fp32=True,
                      evac="dve")
        t2 = kb.dve(lambda E: E.reduce_sum(pcol[0:1, :], grow[0:1, 0:128],
                                           axis=AX.X), [tt])
        t3 = kb.dve(lambda E: E.tensor_scalar(pcol[0:1, :], pcol[0:1, :],
                                              0.5 / NW, 0.5e-8, ALU.mult,
                                              ALU.add), [t2])
        bi, bk, bd = kb.bank([t3])
        t5 = kb.pe(lambda E, bk=bk: E.matmul(bk[:, 0:1], kb.onesf[0:1, :],
                                             pcol[0:1, 0:1], start=True,
                                             stop=True), bd)
        t6 = kb.dve(lambda E, o=thr3[:, t:t + 1], bk=bk: E.tensor_copy(
            o, bk[:, 0:1]), [t5])
        t7 = kb.dve(lambda E, o=nthr3[:, t:t + 1], a=thr3[:, t:t + 1]:
                    E.tensor_scalar_mul(o, a, -1.0), [t6])
        kb.bank_last[bi] = t7
        prevb = t7
    if dbg is not None:
        kb.dma(dbg, thr3, [prevb], q="qsp")
    ar.free(*kb.gamma_bufs)
    ar.free("pcol", "gacc", "grow")
    prog.barrier(exclude=QACT_DOMS)

    NWFC = 4
    wfc_ = [ar.alloc(f"wfcH{i}", 1024, F32) for i in range(NWFC)]
    cbuf_ = [ar.alloc(f"cbufH{i}", 1024, BF16) for i in range(2)]
    w1q_ = [ar.alloc(f"w1qH{i}", 16 * 512, BF16) for i in range(2)]
    w2q_ = [ar.alloc(f"w2qH{i}", 16 * 512, BF16) for i in range(2)]
    w3q = ar.alloc("w3qH", 4 * D, BF16)
    sgel_ = [ar.alloc(f"sgelH{i}", 512, F32) for i in range(2)]
    hidb_ = [ar.alloc(f"hidbH{i}", 512, BF16) for i in range(2)]
    hTt_ = [ar.alloc(f"hTtH{i}", 4 * 128, BF16) for i in range(2)]
    NCH = FF1 // 512
    srcs = {nm: kb.inp[nm].rearrange("(t p) n -> p t n", p=128)
            for nm in ("fw1T", "fw2T", "fw3T")}

    def pieces(hc):
        pl = []
        for tnum, nm in ((0, "fw1T"), (1, "fw2T")):
            for j in range(8):
                pl.append((tnum, 2 * j, 2, hc * 512, 512, 2 * j * 512))
        for j in range(8):
            pl.append((2, hc * 4 + j // 2, 1, (j % 2) * 1024, 1024,
                       (j // 2) * D + (j % 2) * 1024))
        return pl

    all_pieces = [p for hc in range(NCH) for p in pieces(hc)]
    wfc_readers = [None] * NWFC
    cbuf_reader = [None, None]
    users12 = {}
    user3 = [None]
    dma_toks = {}
    st = {"di": 0}

    def pump_dma(upto):
        while st["di"] < min(upto, len(all_pieces)):
            i = st["di"]
            st["di"] += 1
            tnum, rbase, ntl, c0, ncols, dbase = all_pieces[i]
            buf = wfc_[i % NWFC]
            nm = ("fw1T", "fw2T", "fw3T")[tnum]
            dma_toks[i] = kb.dma(
                buf[:, 0:ntl * ncols].rearrange("p (t n) -> p t n", t=ntl),
                srcs[nm][:, rbase:rbase + ntl, c0:c0 + ncols],
                [wfc_readers[i % NWFC]], q="qsp")

    def emit_cmp(i):
        pump_dma(i + NWFC)
        hc_i, tnum = i // 24, all_pieces[i][0]
        _, rbase, ntl, c0, ncols, dbase = all_pieces[i]
        buf, cb = wfc_[i % NWFC], cbuf_[i % 2]
        dstq = (w1q_[hc_i % 2], w2q_[hc_i % 2], w3q)[tnum]
        user = user3[0] if tnum == 2 else users12.get((tnum, hc_i % 2))
        wtok = dma_toks.pop(i)
        tq1 = kb.dve(lambda E, a=buf[:, 0:ntl * ncols],
                     o=dstq[:, dbase:dbase + ntl * ncols],
                     s=thr3[:, tnum:tnum + 1]:
                     E.tensor_single_scalar(o, a, s, ALU.is_gt),
                     [wtok, user])
        tq1b = kb.dve(lambda E, a=buf[:, 0:ntl * ncols], o=cb[:, 0:ntl * ncols],
                      s=nthr3[:, tnum:tnum + 1]:
                      E.tensor_single_scalar(o, a, s, ALU.is_lt),
                      [wtok, cbuf_reader[i % 2]])
        tq2 = kb.pool_op(lambda E, o=dstq[:, dbase:dbase + ntl * ncols],
                         a=cb[:, 0:ntl * ncols]: E.tensor_sub(o, o, a),
                         [tq1, tq1b, user])
        wfc_readers[i % NWFC] = join([tq1, tq1b])
        cbuf_reader[i % 2] = tq2
        return tq2

    q12done = {}
    q3done = {}
    for i in range(16):
        q12done[0] = emit_cmp(i)
    for i in range(16, 24):
        q3done[0] = emit_cmp(i)
    hc_done = None
    prev_hid = None
    prev_htr = None
    if dbg is not None:
        td1 = kb.dma(kb.dbgQ1, w1q_[0], [q12done[0]], q="qsp")
        td3 = kb.dma(kb.dbgQ3, w3q, [q3done[0]], q="qsp")
        q12done[0] = join([q12done[0], td1])
        q3done[0] = join([q3done[0], td3])
        hc_done = join([td1, td3])
    for hc in range(NCH):
        w1q, w2q = w1q_[hc % 2], w2q_[hc % 2]
        last = None
        mmq = {}

        def h1_mm(ti, hc=hc, w1q=w1q, w2q=w2q):
            if hc + 1 < NCH:
                for i in range(24 * (hc + 1) + 2 * ti,
                               24 * (hc + 1) + 2 * ti + 2):
                    q12done[hc + 1] = emit_cmp(i)
            bi1, bk1, bd1 = kb.bank([q12done[hc]])
            tm1 = kb.mm16(bk1, lambda f, ti=ti: xn1T[:, f * TOWN + ti * 128:
                                                     f * TOWN + (ti + 1) * 128],
                          lambda f, w1q=w1q: w1q[:, f * 512:(f + 1) * 512], bd1)
            bi2, bk2, bd2 = kb.bank([q12done[hc]])
            tm2 = kb.mm16(bk2, lambda f, ti=ti: xn1T[:, f * TOWN + ti * 128:
                                                     f * TOWN + (ti + 1) * 128],
                          lambda f, w2q=w2q: w2q[:, f * 512:(f + 1) * 512], bd2)
            mmq[ti] = (bi1, bk1, tm1, bi2, bk2, tm2)

        h1_mm(0)
        for ti in range(NTO):
            if ti + 1 < NTO:
                h1_mm(ti + 1)
            sgel, hidb, hTt = sgel_[ti % 2], hidb_[ti % 2], hTt_[ti % 2]
            bi1, bk1, tm1, bi2, bk2, tm2 = mmq.pop(ti)
            tsg = kb.act(lambda E, bk1=bk1, o=sgel: E.activation(
                out=o, in_=bk1, func=AF.Sigmoid), [tm1, prev_hid])
            kb.bank_last[bi1] = tsg
            thid = kb.dve(lambda E, bk2=bk2, o=hidb, sg=sgel:
                          E.tensor_mul(o, sg, bk2), [tsg, tm2, prev_htr])
            kb.bank_last[bi2] = thid
            last_tm = {"fw1T": tm1, "fw2T": tm2}
            if dbg is not None and hc == 0 and ti == 0:
                thid = join([thid, kb.dma(kb.dbgHid, hidb, [thid], q="qsp")])
            htr = kb.tgrp([hidb[:, j * 128:(j + 1) * 128] for j in range(4)],
                          [("act" if ti % 2 else "dve", hTt[:, 0:512],
                            0, 4, 128)], [thid], site="h1")
            for ch in range(4):
                bi3, bk3, bd3 = kb.bankT([htr, q3done[hc]])
                tm3 = None
                for jt in range(4):
                    tm3 = kb.pe(lambda E, bk3=bk3,
                                lt=hTt[:, jt * 128:(jt + 1) * 128],
                                rh=w3q[:, jt * D + ch * 512:jt * D + (ch + 1) * 512],
                                s=(jt == 0), p=(jt == 3):
                                E.matmul(bk3, lt, rh, start=s, stop=p),
                                bd3 if jt == 0 else ())
                ta = kb.dve(lambda E, bk3=bk3,
                            o=kb.acc[:, ti * D + ch * 512:ti * D + (ch + 1) * 512]:
                            E.tensor_add(o, o, bk3), [tm3])
                kb.psT_last[bi3] = ta
                last = ta
            last_tm["fw3T"] = tm3
            prev_hid = tsg
            prev_htr = htr
        users12[(0, hc % 2)] = last_tm["fw1T"]
        users12[(1, hc % 2)] = last_tm["fw2T"]
        user3[0] = last_tm["fw3T"]
        if hc + 1 < NCH:
            for i in range(24 * (hc + 1) + 16, 24 * (hc + 1) + 24):
                q3done[hc + 1] = emit_cmp(i)
        hc_done = last
    ar.free("w3qH", "invg", "nthr3",
            *[f"wfcH{i}" for i in range(NWFC)],
            *[f"{n}{i}" for n in ("cbufH", "w1qH", "w2qH",
                                  "sgelH", "hidbH", "hTtH") for i in range(2)])
    prog.barrier(exclude=QACT_DOMS)


def emit_out(kb, out_ap):
    tok = None
    for ti in range(NTO):
        tok = kb.dma(out_ap[ti * 128:(ti + 1) * 128, :],
                     kb.acc[:, ti * D:(ti + 1) * D], [], q="qsp")
    kb.prog.barrier(exclude=QACT_DOMS)
    return tok


def setup_min(kb):
    for name, shape in (("identb", (128, 128)), ("identf", (128, 128)),
                        ("eps5", (128, 1)), ("eps6", (128, 1))):
        dt = BF16 if name == "identb" else F32
        kb.din(name, shape, dt)
    ar = kb.ar
    kb.identb = ar.alloc("identb", 128, BF16)
    kb.identf = ar.alloc("identf", 128, F32)
    kb.eps5 = ar.alloc("eps5", 1, F32)
    kb.eps6 = ar.alloc("eps6", 1, F32)
    for n in ("identb", "identf", "eps5", "eps6"):
        kb.dma(getattr(kb, n), kb.inp[n][:, :], [], q="qsp")


DEBUG_L1 = False


def build_l1():
    kb = KB()
    setup_min(kb)
    declare_l1(kb)
    load_l1_consts(kb)
    kb.gamma = GammaStream(kb)
    x1own = kb.din("x1own", (TOWN, D))
    x1pre = kb.din("x1pre", (TOWN, D))
    out = kb.dout("out", (TOWN, D))
    if DEBUG_L1:
        kb.dbgQ1 = kb.dout("dbgQ1", (128, 16 * 512), BF16)
        kb.dbgQ3 = kb.dout("dbgQ3", (128, 4 * D), BF16)
        kb.dbgHid = kb.dout("dbgHid", (128, 512), BF16)
        dbgS = kb.dout("dbgS", (128, 16 * 128))
        dbgOT = kb.dout("dbgOT", (128, 16 * TOWN), BF16)
        dbgON = kb.dout("dbgON", (128, 8 * D), BF16)
        dbgA = kb.dout("dbgA", (128, 8 * D))
        dbgI = kb.dout("dbgI", (128, 4))
    for ti in range(NTO):
        kb.dma(kb.acc[:, ti * D:(ti + 1) * D], x1own[ti * 128:(ti + 1) * 128, :],
               [], q="qsp")
    kb.prog.barrier(exclude=QACT_DOMS)
    kb.gamma.finishup()
    emit_l1_consts(kb)
    emit_prefix_chain(kb, x1pre)
    if DEBUG_L1:
        kb.prog.barrier(exclude=QACT_DOMS)
        kb.dma(dbgS, kb.Sst, [], q="qsp")
        kb.prog.barrier(exclude=QACT_DOMS)
    emit_l1_own(kb, dbg=(dbgOT, dbgON, dbgA) if DEBUG_L1 else None)
    emit_l1_ffn(kb, dbg=dbgI if DEBUG_L1 else None)
    emit_out(kb, out)
    finish(kb)
    return kb



def _seg_x(x, b, start):
    out = np.zeros((SEGT, D), np.float32)
    lo = start - HALO
    src_lo = max(lo, 0)
    out[src_lo - lo:HALO] = x[b, src_lo:start]
    out[HALO:] = x[b, start:start + TOWN]
    return out


def _seg_mask(start):
    kmin = HALO - start if start < HALO else 0
    i = np.arange(128)[:, None]
    c = np.arange(384)[None, :]
    band = (c >= i + 1) & (c <= i + 256)
    out = np.zeros((3, 128, 400), np.float32)
    for q in range(3):
        out[q, :, 0:384] = np.where(band & ((c + q * 128) >= kmin), 0.0, NEG)
    return np.ascontiguousarray(out)


def _blkw2(a):
    return np.ascontiguousarray(
        a.reshape(FF0 // 128, 128, D // 128, 128).transpose(2, 1, 0, 3)
        .reshape(D // 128, 128, FF0))


def _bcast(v):
    v = np.asarray(v, np.float32).reshape(1, -1)
    return np.ascontiguousarray(np.broadcast_to(v, (128, v.shape[1])))


def stage_shared(ins):
    t = lambda a: np.ascontiguousarray(np.asarray(a, np.float32).T)
    sh = {
        "wqT": t(ins["L0_attn_Wq"]), "wkT": t(ins["L0_attn_Wk"]),
        "wvT": t(ins["L0_attn_Wv"]), "woT": t(ins["L0_attn_Wo"]),
        "wrT": t(ins["L0_ffn_Wr"]),
        "w1T0": t(ins["L0_ffn_W1"]), "w2T0": _blkw2(t(ins["L0_ffn_W2"])),
        "gT": t(np.asarray(ins["L0_attn_global_tokens"], np.float32)[0]),
        "ln0s": _bcast(ins["L0_attn_ln_scale"]),
        "ln0b": _bcast(ins["L0_attn_ln_bias"]),
        "brb": _bcast(ins["L0_ffn_br"]),
        "identb": np.eye(128, dtype=ml_dtypes.bfloat16),
        "identf": np.eye(128, dtype=np.float32),
        "eps5": np.full((128, 1), 1e-5, np.float32),
        "eps6": np.full((128, 1), 1e-6, np.float32),
    }
    return sh


def stage_core_l0(x, core):
    b, h = core // 2, core % 2
    return {"xO": _seg_x(x, b, h * TOWN), "maskO": _seg_mask(h * TOWN)}


def stage_shared_l1(ins):
    t = lambda a: np.ascontiguousarray(np.asarray(a, np.float32).T)
    c = lambda a: np.ascontiguousarray(np.asarray(a, np.float32))
    m = np.arange(256)[:, None]
    l = np.arange(256)[None, :]
    dd = np.where(l >= m, (l - m).astype(np.float32), 1.0e5)
    diffT = np.ascontiguousarray(
        dd.reshape(2, 128, 256).transpose(1, 0, 2).reshape(128, 512))
    return {
        "wqkv": c(ins["L1_attn_w_qkv"]), "wgate": c(ins["L1_attn_w_gate"]),
        "wout": c(ins["L1_attn_w_out"]),
        "rmsb": _bcast(ins["L1_attn_rms_weight"]),
        "slopeb": _bcast(np.asarray(ins["L1_attn_slope_rate"], np.float32).reshape(-1)),
        "fw1T": t(ins["L1_ffn_w1"]), "fw2T": t(ins["L1_ffn_w2"]),
        "fw3T": t(ins["L1_ffn_w3"]),
        "iotaQ": np.ascontiguousarray(
            (np.arange(256).reshape(2, 128).T + 1).astype(np.float32)),
        "iotaK": np.ascontiguousarray(
            (255 - np.arange(256).reshape(2, 128).T).astype(np.float32)),
        "diffT": diffT,
        "onesf": np.ones((1, 128), np.float32),
        "identb": np.eye(128, dtype=ml_dtypes.bfloat16),
        "identf": np.eye(128, dtype=np.float32),
        "eps5": np.full((128, 1), 1e-5, np.float32),
        "eps6": np.full((128, 1), 1e-6, np.float32),
    }


def build_fused():
    kb = KB()
    setup_min(kb)
    declare_l0(kb)
    declare_l1(kb)
    xO = kb.din("xO", (SEGT, D))
    xP = kb.din("xP", (SEGT, D))
    maskO = kb.din("maskO", (3, 128, 400))
    maskP = kb.din("maskP", (3, 128, 400))
    out = kb.dout("out", (TOWN, D))
    x1P = kb.nc.dram_tensor("x1Pscr", [TOWN, D], F32).ap()
    load_l0_consts(kb)
    kb.gamma = GammaStream(kb)
    emit_l0_segment(kb, xP, maskP, x1P)
    emit_l0_segment(kb, xO, maskO, None)
    kb.ar.free("brb", "gTb", "wrTb")
    load_l1_consts(kb)
    kb.gamma.finishup()
    emit_l1_consts(kb)
    emit_prefix_chain(kb, x1P)
    emit_l1_own(kb)
    emit_l1_ffn(kb)
    emit_out(kb, out)
    finish(kb)
    return kb


FUSED = True


def kernel(**ins):
    if FUSED:
        return kernel_fused(ins)
    return kernel_2launch(ins)


def kernel_fused(ins):
    x = np.asarray(ins["L0_attn_x"], np.float32)
    sh = stage_shared(ins)
    sh.update(stage_shared_l1(ins))
    kb = build_fused()
    maps = []
    for c in range(8):
        b, h = c // 2, c % 2
        m = dict(sh)
        m["xO"] = _seg_x(x, b, h * TOWN)
        m["maskO"] = _seg_mask(h * TOWN)
        m["xP"] = _seg_x(x, b, (1 - h) * TOWN)
        m["maskP"] = _seg_mask((1 - h) * TOWN)
        m["selc"] = np.full((128, 1), float(h), np.float32)
        maps.append({k: m[k] for k in kb.inp})
    r = run_bass_kernel_spmd(kb.nc, maps, core_ids=list(range(8)))
    out = np.empty((4, 2048, D), np.float32)
    for c in range(8):
        b, h = c // 2, c % 2
        out[b, h * TOWN:(h + 1) * TOWN] = r.results[c]["out"]
    return out


def kernel_2launch(ins):
    x = np.asarray(ins["L0_attn_x"], np.float32)
    sh = stage_shared(ins)
    kb1 = build_l0()
    maps1 = []
    for c in range(8):
        m = dict(sh)
        m.update(stage_core_l0(x, c))
        maps1.append({k: m[k] for k in kb1.inp})
    r1 = run_bass_kernel_spmd(kb1.nc, maps1, core_ids=list(range(8)))
    sh1 = stage_shared_l1(ins)
    kb2 = build_l1()
    maps2 = []
    for c in range(8):
        h = c % 2
        m = dict(sh1)
        m["x1own"] = np.ascontiguousarray(r1.results[c]["x1"])
        m["x1pre"] = np.ascontiguousarray(r1.results[c ^ 1]["x1"])
        m["selc"] = np.full((128, 1), float(h), np.float32)
        maps2.append({k: m[k] for k in kb2.inp})
    r2 = run_bass_kernel_spmd(kb2.nc, maps2, core_ids=list(range(8)))
    out = np.empty((4, 2048, D), np.float32)
    for c in range(8):
        b, h = c // 2, c % 2
        out[b, h * TOWN:(h + 1) * TOWN] = r2.results[c]["out"]
    return out
